# version 19
# baseline (speedup 1.0000x reference)
import numpy as np
import concourse.bass as bass
import concourse.mybir as mybir
from concourse.bass_utils import run_bass_kernel_spmd
from concourse import library_config

F32 = mybir.dt.float32
BF16 = mybir.dt.bfloat16
AF = mybir.ActivationFunctionType
ALU = mybir.AluOpType
AXX = mybir.AxisListType.X

MAGIC = 12582912.0

NT = 2048
D = 1024
DIN = 2048
NH = 8
HD = 256
NS = 64
DIP = 4232
NE = 8
HE = 256
RA = 64
HB = 4096

_TAP = None
_TAP_RESULTS = None


def _make_rint_op():
    import concourse.dve_ops as dve_ops
    from concourse.dve_op import DveOp
    from concourse.dve_spec import Spec, Src0, C0, C1, lower
    from concourse.dve_table_gen import dve_ver_for

    name = "RINT_SCALE_ANT"
    for op in dve_ops.OPS:
        if op.name == name:
            return op
    spec = Spec(body=(Src0 * C1 + C0) - C0)
    ver = dve_ver_for("TRN2")
    sha = lower(spec, ver=ver, perf_en=False, name=name).sha(ver)
    op = DveOp(
        name, spec, subdim=False, uops_sha={ver: sha},
        reference=lambda in0, in1, c0, c1, c2: np.rint(in0 * c1),
    )
    dve_ops.OPS.append(op)
    dve_ops.CUSTOM_DVE_SPECS[name] = spec
    dve_ops.CUSTOM_DVE_REFS[name] = op.ref
    dve_ops._SUB_OPCODE_FOR_NAME[name] = (
        dve_ops._CUSTOM_DVE_ROW_BASE + len(dve_ops.OPS) - 1)
    return op


RINT = _make_rint_op()


class _CallRec:
    __slots__ = ("name", "args", "kw", "incs")

    def __init__(self, name, args, kw):
        self.name = name
        self.args = args
        self.kw = kw
        self.incs = []

    def then_inc(self, sem, k):
        self.incs.append((sem, k))
        return self

    def replay(self, engine):
        inst = getattr(engine, self.name)(*self.args, **self.kw)
        for s, k in self.incs:
            inst = inst.then_inc(s, k)


class _EngProxy:
    def __init__(self, prog, eng):
        self._prog = prog
        self._eng = eng

    def __getattr__(self, name):
        def call(*args, **kw):
            rec = _CallRec(name, args, kw)
            self._prog.ops[self._eng].append(rec)
            return rec
        return call


class Sem:
    __slots__ = ("h", "n")

    def __init__(self, handle):
        self.h = handle
        self.n = 0

    def step(self, tok, k=1):
        self.n += k
        tok.then_inc(self.h, k)
        return self.n


class Prog:
    ENGINES = ("sync", "tensor", "vector", "scalar", "gpsimd")

    def __init__(self, nc, sems):
        self.nc = nc
        self.ops = {e: [] for e in self.ENGINES}
        self.sems = sems
        self.sy = _EngProxy(self, "sync")
        self.t = _EngProxy(self, "tensor")
        self.v = _EngProxy(self, "vector")
        self.s = _EngProxy(self, "scalar")
        self.g = _EngProxy(self, "gpsimd")

    def vd(self):
        self.ops["vector"].append(_CallRec("drain", (), {}))

    def sd(self):
        self.ops["scalar"].append(_CallRec("drain", (), {}))

    def wait(self, eng, sem, val):
        if val > 0:
            getattr(self, {"sync": "sy", "tensor": "t", "vector": "v",
                           "scalar": "s", "gpsimd": "g"}[eng]).wait_ge(sem.h, val)

    def barrier(self, engines=None):
        bar = self.sems["bar"]
        engines = engines or self.ENGINES
        bar.n += len(engines)
        tgt = bar.n
        for e in engines:
            if e in ("vector", "scalar", "tensor", "gpsimd"):
                self.ops[e].append(_CallRec("drain", (), {}))
            self.ops[e].append(_CallRec("sem_inc", (bar.h, 1), {}))
            self.ops[e].append(_CallRec("wait_ge", (bar.h, tgt), {}))


def _host_prep(inputs):
    f32 = np.float32
    ac = np.ascontiguousarray
    w = {}
    w["wip_t"] = ac(inputs["L0_attn_in_proj_w"].T)
    w["conv_w"] = ac(inputs["L0_attn_conv_w"])
    w["conv_b"] = ac(np.asarray(inputs["L0_attn_conv_b"]).reshape(DIN, 1))
    w["a_log"] = ac(np.asarray(inputs["L0_attn_A_log"]).reshape(NH, 1))
    w["d_par"] = ac(np.asarray(inputs["L0_attn_D_param"]).reshape(1, NH))
    w["dt_bias"] = ac(np.asarray(inputs["L0_attn_dt_bias"]).reshape(NH, 1))
    w["mnorm_w"] = ac(np.asarray(inputs["L0_attn_norm_w"]).reshape(1, DIN))
    w["wop_t"] = ac(inputs["L0_attn_out_proj_w"].T)
    w["w_base"] = ac(inputs["L0_ffn_W_base"])
    w["b_base"] = ac(np.asarray(inputs["L0_ffn_b_base"]).reshape(1, D))
    w["wd1"] = ac(np.asarray(inputs["L0_ffn_Wd1"]).reshape(NE * D, HE))
    w["bd1"] = ac(np.asarray(inputs["L0_ffn_bd1"]).reshape(NE * HE, 1))
    w["wd2"] = ac(np.asarray(inputs["L0_ffn_Wd2"]).reshape(NE * HE, D))
    w["bd2"] = ac(np.asarray(inputs["L0_ffn_bd2"]).reshape(NE * D, 1))
    w["wa_down"] = ac(np.asarray(inputs["L0_ffn_Wa_down"]).reshape(NE * D, RA))
    w["ba_down"] = ac(inputs["L0_ffn_ba_down"])
    w["wa_up"] = ac(np.asarray(inputs["L0_ffn_Wa_up"]).reshape(NE * RA, D))
    w["ba_up"] = ac(inputs["L0_ffn_ba_up"])
    w["wq_t"] = ac(inputs["L1_attn_Wq"].T)
    w["wk_t"] = ac(inputs["L1_attn_Wk"].T)
    w["wv_t"] = ac(inputs["L1_attn_Wv"].T)
    w["wo_t"] = ac(inputs["L1_attn_Wo"].T)
    w["decay"] = ac(np.asarray(inputs["L1_attn_decay_logit"], dtype=f32).reshape(1, 1))
    w["oscale"] = ac(np.asarray(inputs["L1_attn_out_scale"], dtype=f32).reshape(1, 1))
    w["norm1_w"] = ac(np.asarray(inputs["L1_ffn_norm1_w"]).reshape(1, D))
    w["w1_t"] = ac(inputs["L1_ffn_w1"].T)
    w["act_s"] = ac(np.asarray(inputs["L1_ffn_act_scale"], dtype=f32).reshape(1, 1))
    w["act_b"] = ac(np.asarray(inputs["L1_ffn_act_bias"], dtype=f32).reshape(1, 1))
    w["norm2_w"] = ac(np.asarray(inputs["L1_ffn_norm2_w"]).reshape(1, HB))
    w["w2_t"] = ac(inputs["L1_ffn_w2"].T)

    c = {}
    c["ident"] = np.eye(128, dtype=f32)
    r = np.arange(128)
    c["tri_incl"] = (r[:, None] <= r[None, :]).astype(f32)
    c["tri_strict"] = (r[:, None] > r[None, :]).astype(f32)
    c["ones_row"] = np.ones((1, 128), dtype=f32)
    c["iota"] = np.arange(128, dtype=f32).reshape(128, 1)
    c["ones_col"] = np.ones((128, 1), dtype=f32)

    x_full = np.asarray(inputs["L0_attn_x"])
    per_core = []
    for core in range(8):
        b, h = core // 2, core % 2
        m = dict(w)
        m.update(c)
        m["x0"] = ac(x_full[b, h * NT:(h + 1) * NT])
        m["x_prev"] = (np.concatenate([np.zeros((128, D), dtype=f32),
                                       x_full[b, 0:NT // 2]]) if h == 0
                       else ac(x_full[b, NT // 2 - 128:NT]))
        m["pgate"] = np.full((1, 1), 0.0 if h == 0 else 1.0, dtype=f32)
        m["gate_attn"] = np.full((1, 1), 1.0 if h == 0 else 0.0, dtype=f32)
        per_core.append(m)
    return per_core


class Ctx:
    pass


def build_program(tap=None):
    nc = bass.Bass("TRN2", target_bir_lowering=False, debug=False)
    cx = Ctx()
    cx.nc = nc
    cx.tap = tap
    cx.extra_outs = []

    dr = {}

    def din(name, shape, dtype=F32):
        dr[name] = nc.dram_tensor(name, list(shape), dtype, kind="ExternalInput").ap()

    din("x0", (NT, D)); din("x_prev", (NT // 2 + 128, D)); din("pgate", (1, 1))
    din("wip_t", (D, DIP)); din("conv_w", (DIN, 4)); din("conv_b", (DIN, 1))
    din("a_log", (NH, 1)); din("d_par", (1, NH)); din("dt_bias", (NH, 1))
    din("mnorm_w", (1, DIN)); din("wop_t", (DIN, D))
    din("w_base", (D, D)); din("b_base", (1, D))
    din("wd1", (NE * D, HE)); din("bd1", (NE * HE, 1))
    din("wd2", (NE * HE, D)); din("bd2", (NE * D, 1))
    din("wa_down", (NE * D, RA)); din("ba_down", (NE, RA))
    din("wa_up", (NE * RA, D)); din("ba_up", (NE, D))
    din("wq_t", (D, D)); din("wk_t", (D, D)); din("wv_t", (D, D)); din("wo_t", (D, D))
    din("decay", (1, 1)); din("oscale", (1, 1))
    din("norm1_w", (1, D)); din("w1_t", (D, HB))
    din("act_s", (1, 1)); din("act_b", (1, 1))
    din("norm2_w", (1, HB)); din("w2_t", (HB, D))
    din("ident", (128, 128)); din("tri_incl", (128, 128)); din("tri_strict", (128, 128))
    din("ones_row", (1, 128)); din("iota", (128, 1)); din("ones_col", (128, 1))
    din("gate_attn", (1, 1))
    dr["out"] = nc.dram_tensor("out", [NT, D], F32, kind="ExternalOutput").ap()
    dr["w1i_hbm"] = nc.dram_tensor("w1i_hbm", [D, HB], BF16, kind="Internal").ap()
    dr["s1_defer"] = nc.dram_tensor("s1_defer", [512, 1024], F32, kind="Internal").ap()
    dr["q_defer"] = nc.dram_tensor("q_defer", [512, 1024], F32, kind="Internal").ap()
    dr["w2i_hbm"] = nc.dram_tensor("w2i_hbm", [HB, D], BF16, kind="Internal").ap()
    cx.dr = dr

    sb = {}
    sb_handles = {}

    def sbuf(name, shape, dtype=F32):
        t = nc.alloc_sbuf_tensor(name, list(shape), dtype)
        sb_handles[name] = t
        sb[name] = t.ap()

    sbuf("BX", (128, 16384))
    sbuf("BA", (128, 16384))
    sbuf("BB", (128, 16384))
    sbuf("consts", (128, 480))
    sbuf("stats", (128, 128))
    sbuf("hstate", (64, 2048))
    sbuf("X1TB", (128, 1024), BF16)
    sbuf("X2TB", (128, 512), BF16)
    sbuf("keep", (128, 64))
    wb4 = nc.alloc_sbuf_tensor_at(
        "WB4", [128, 4096], BF16,
        offset=nc.lookup_mloc(sb_handles["hstate"]).addr)
    sb["WB4"] = wb4.ap()
    cx.sb = sb

    cx.ps = nc.alloc_psum_tensor("PS", [128, 4096], F32).ap()

    import contextlib
    stack = contextlib.ExitStack()
    sems = {}
    for name in ("ld", "wl", "wl2", "mm", "ev", "act", "gp", "store", "bar",
                 "prep", "lsem", "rsem", "rsem2", "lsem2", "rsem3",
                 "gdma", "gdb", "gdc", "gdd",
                 "qa", "qb", "qc", "oa", "ob", "sta", "stb"):
        sems[name] = Sem(stack.enter_context(nc.semaphore(name)))
    cx.sems = sems

    p = Prog(nc, sems)
    cx.p = p

    _build_all(cx)

    with nc.Block() as block:
        @block.sync
        def _(eng):
            for f in p.ops["sync"]:
                f.replay(eng)

        @block.tensor
        def _(eng):
            for f in p.ops["tensor"]:
                f.replay(eng)

        @block.vector
        def _(eng):
            for f in p.ops["vector"]:
                f.replay(eng)

        @block.scalar
        def _(eng):
            for f in p.ops["scalar"]:
                f.replay(eng)

        @block.gpsimd
        def _(eng):
            for f in p.ops["gpsimd"]:
                f.replay(eng)

    stack.close()
    mybir.codegen_inst_isa_subclasses(nc)
    return nc


def _build_all(cx):
    p = cx.p
    p.g.load_library(library_config.remote_dma)
    _load_consts(cx)
    p.v.memset(cx.sb["stats"][:, :], 0.0)
    p.v.memset(cx.sb["keep"][:, :], 0.0)
    p.v.drain()
    p.barrier()
    _stage_ln0(cx)
    p.barrier()
    if cx.tap == "xn0":
        _tap(cx, "t_xn0", cx.sb["BA"][:, 0:16384])
        return
    if cx.tap == "stats":
        _tap(cx, "t_stats", cx.sb["stats"][:, 0:96])
        return
    _stage_prev_state(cx)
    p.barrier()
    if cx.tap == "hstate":
        _tap(cx, "t_hstate", cx.sb["hstate"][:, 0:2048])
        return
    _stage_experts(cx)
    p.barrier()
    if cx.tap == "losses":
        _tap(cx, "t_losses", cx.sb["stats"][:, 64:96])
        return
    _stage_gate_ffn(cx)
    p.barrier()
    if cx.tap == "g0":
        _tap(cx, "t_g0", cx.sb["BA"][:, 0:16384])
        return
    _stage_mamba(cx)
    p.barrier()
    if cx.tap == "x1":
        _tap(cx, "t_x1", cx.sb["BX"][:, 0:16384])
        return
    _stage_l1_ffn(cx)
    p.barrier()
    if cx.tap == "g1":
        _tap(cx, "t_g1", cx.sb["BA"][:, 0:16384])
        return
    _stage_l1_attn(cx)
    p.barrier()
    _store_out(cx)


def _tap(cx, name, src_ap):
    nc, p, sems = cx.nc, cx.p, cx.sems
    rows, cols = src_ap.shape
    dt = nc.dram_tensor(name, [rows, cols], F32, kind="ExternalOutput").ap()
    st = sems["store"]
    v = st.step(p.g.dma_start(out=dt[:, :], in_=src_ap), 16)
    _store_out(cx)


def _store_out(cx):
    p, sems, sb, dr = cx.p, cx.sems, cx.sb, cx.dr
    st = sems["store"]
    bx = sb["BX"]
    for c in range(16):
        st.step(p.g.dma_start(out=dr["out"][c * 128:(c + 1) * 128, :],
                              in_=bx[:, c * 1024:(c + 1) * 1024]), 16)
    p.wait("gpsimd", st, st.n)


CON = {}


def _load_consts(cx):
    p, sems, sb, dr = cx.p, cx.sems, cx.sb, cx.dr
    ld = sems["ld"]
    cons = sb["consts"]
    off = 0

    def load(name, src, rows, cols):
        nonlocal off
        CON[name] = (off, cols)
        ld.step(p.sy.dma_start(out=cons[0:rows, off:off + cols], in_=src), 16)
        off += cols

    load("ident", dr["ident"][:, :], 128, 128)
    load("tri_incl", dr["tri_incl"][:, :], 128, 128)
    load("tri_strict", dr["tri_strict"][:, :], 128, 128)
    load("iota", dr["iota"][:, :], 128, 1)
    CON["ones_row"] = (CON["tri_incl"][0], 128)
    CON["ones_col"] = (CON["tri_incl"][0] + 127, 1)
    load("a_log", dr["a_log"][:, :], NH, 1)
    load("dt_bias", dr["dt_bias"][:, :], NH, 1)
    load("decay", dr["decay"][:, :], 1, 1)
    load("oscale", dr["oscale"][:, :], 1, 1)
    load("act_s", dr["act_s"][:, :], 1, 1)
    load("act_b", dr["act_b"][:, :], 1, 1)
    load("gate_attn", dr["gate_attn"][:, :], 1, 1)
    load("d_par", dr["d_par"][:, :], 1, NH)
    CON["conv_b"] = (off, 16)
    for c in range(16):
        ld.step(p.sy.dma_start(out=cons[0:128, off + c:off + c + 1],
                               in_=dr["conv_b"][c * 128:(c + 1) * 128, :]), 16)
    off += 16
    CON["conv_w"] = (off, 64)
    for c in range(16):
        ld.step(p.sy.dma_start(out=cons[0:128, off + c * 4:off + c * 4 + 4],
                               in_=dr["conv_w"][c * 128:(c + 1) * 128, :]), 16)
    off += 64
    assert off <= 480


def C(cx, name, rows=128, cols=None):
    off, w = CON[name]
    return cx.sb["consts"][0:rows, off:off + (cols or w)]


def _stage_ln0(cx):
    p, sems, sb, dr = cx.p, cx.sems, cx.sb, cx.dr
    ld = sems["ld"]
    bx = sb["BX"]
    for c in range(16):
        ld.step(p.sy.dma_start(out=bx[:, c * 1024:(c + 1) * 1024],
                               in_=dr["x0"][c * 128:(c + 1) * 128, :]), 16)
    src = [bx[:, c * 1024:(c + 1) * 1024] for c in range(16)]
    _layernorm_16(cx, src, stat_base=0, wait_sem=(ld, ld.n))
    stats = sb["stats"]
    ev, act, mm = sems["ev"], sems["act"], sems["mm"]
    tmp = sb["BB"]
    guard = [0, 0]
    for c in range(16):
        mu = stats[:, c:c + 1]
        rs = stats[:, 16 + c:16 + c + 1]
        t = tmp[:, 0:1024] if c % 2 == 0 else tmp[:, 1024:2048]
        p.wait("vector", mm, guard[c % 2])
        ve = ev.step(p.v.tensor_scalar(t, src[c], mu, rs, ALU.subtract, ALU.mult))
        dst = [sb["BA"][:, (c // 4) * 4096 + k * 512 + (c % 4) * 128:
                        (c // 4) * 4096 + k * 512 + (c % 4) * 128 + 128]
               for k in range(8)]
        guard[c % 2] = _transpose128(cx, t, dst, wait=(ev, ve))


def _layernorm_16(cx, chunks, stat_base, wait_sem, dvar=D):
    p, sems, sb = cx.p, cx.sems, cx.sb
    act, ev = sems["act"], sems["ev"]
    stats = sb["stats"]
    junk = sb["BB"][:, 8192:8192 + dvar]
    n = len(chunks)
    p.wait("scalar", *wait_sem)
    p.wait("vector", *wait_sem)
    for c, ch in enumerate(chunks):
        p.v.reduce_sum(stats[:, 64 + c:64 + c + 1], ch, axis=AXX)
        if c:
            p.s.drain()
        a = act.step(p.s.activation(junk[:, 0:dvar], ch, AF.Square,
                                    accum_out=stats[:, 80 + c:80 + c + 1]))
    p.v.drain()
    p.wait("vector", act, act.n)
    inv = 1.0 / dvar
    mu = stats[:, stat_base:stat_base + n]
    rs = stats[:, stat_base + 16:stat_base + 16 + n]
    sm = stats[:, 64:64 + n]
    sq = stats[:, 80:80 + n]
    p.v.tensor_scalar_mul(mu, sm, inv)
    p.v.tensor_scalar_mul(sq, sq, inv)
    p.v.drain()
    p.v.tensor_tensor(sm, mu, mu, ALU.mult)
    p.v.drain()
    p.v.tensor_sub(sq, sq, sm)
    p.v.drain()
    ve = ev.step(p.v.tensor_scalar_add(sq, sq, 1e-5))
    p.v.drain()
    _rsqrt(cx, rs, sq, stats[:, 64:64 + n], (ev, ve))


def _recip(cx, dst, src):
    p = cx.p
    p.v.reciprocal(dst, src)
    p.v.drain()


def _rsqrt(cx, dst, src, scr, wait):
    p, sems = cx.p, cx.sems
    act, ev = sems["act"], sems["ev"]
    p.wait("scalar", *wait)
    a = act.step(p.s.activation(scr, src, AF.Sqrt))
    p.wait("vector", act, a)
    _recip(cx, dst, scr)
    p.v.tensor_tensor(scr, dst, dst, ALU.mult)
    p.v.drain()
    p.v.tensor_tensor(scr, scr, src, ALU.mult)
    p.v.drain()
    p.v.tensor_scalar(scr, scr, -0.5, 1.5, ALU.mult, ALU.add)
    p.v.drain()
    p.v.tensor_tensor(dst, dst, scr, ALU.mult)
    p.v.drain()
    ev.step(p.v.engine_nop())


def _transpose128(cx, tm_tile, dsts, wait, pbase=2048, out_dtype=None, n_rot=4):
    p, sems, ps = cx.p, cx.sems, cx.ps
    mm, ev = sems["mm"], sems["ev"]
    ident = C(cx, "ident")
    if not hasattr(cx, "_tr_ev"):
        cx._tr_ev = {}
    p.wait("tensor", *wait)
    for k, dst in enumerate(dsts):
        bank = k % n_rot
        key = (pbase, bank)
        g = cx._tr_ev.get(key, 0)
        if g:
            p.wait("tensor", ev, g)
        pcol = pbase + bank * 512
        pdst = ps[:, pcol:pcol + 128]
        if out_dtype is not None:
            pdst = pdst.bitcast(out_dtype)[:, 0:128]
        me = mm.step(p.t.transpose(pdst, tm_tile[:, k * 128:(k + 1) * 128], ident))
        p.wait("vector", mm, me)
        ve = ev.step(p.v.tensor_copy(dst, pdst))
        cx._tr_ev[key] = ve
    return mm.n


XCE, RING, XNF, XST, XH, SZ = 0, 2304, 4352, 5376, 4352, 6400
BCD_B, BCD_C, BCD_D = 8448, 8576, 8704
BTM, XTM, NORMB, OPR, GMS, GT, PBC = 8832, 8896, 9216, 11264, 13312, 13440, 13568
S_DT, S_LA, S_LATM, S_STM, S_ETM, S_E2 = 13696, 13824, 13952, 13960, 13968, 13976
S_SBC, S_SCOL, S_SROW, S_DBC, S_BE = 13984, 13992, 13994, 14010, 14018
S_RSTD, S_RSCR, S_MNW = 14082, 14083, 14200


def _mamba_pass(cx, full):
    p, sems, sb, dr, ps = cx.p, cx.sems, cx.sb, cx.dr, cx.ps
    ld, wl2, mm, ev, act = (sems["ld"], sems["wl2"], sems["mm"],
                            sems["ev"], sems["act"])
    bb, bx, stats, keep, hst = (sb["BB"], sb["BX"], sb["stats"],
                                sb["keep"], sb["hstate"])

    if not full:
        p.v.memset(hst[:, :], 0.0)
        p.vd()
        p.v.memset(keep[:, 0:48], 0.0)
        p.vd()
        p.v.memset(stats[0:64, 112:120], 1.0)
        p.vd()
        ev.step(p.v.engine_nop())

    if full:
        lv = ld.step(p.sy.dma_start(out=bb[0:1, S_MNW:S_MNW + DIN],
                                    in_=dr["mnorm_w"][:, :]), 16)
        p.wait("tensor", ld, lv)
        for q in range(4):
            mv = mm.step(p.t.matmul(ps[:, 0:512], ONES1(cx, 128),
                                    bb[0:1, S_MNW + q * 512:S_MNW + (q + 1) * 512],
                                    start=True, stop=True))
            p.wait("vector", mm, mv)
            vv = ev.step(p.v.tensor_copy(
                bb[:, NORMB + q * 512:NORMB + (q + 1) * 512], ps[:, 0:512]))
            p.vd()
            p.wait("tensor", ev, vv)
        mv = mm.step(p.t.matmul(ps[:, 1024:1032], ONES1(cx, 128),
                                C(cx, "d_par", 1), start=True, stop=True))
        p.wait("vector", mm, mv)
        vv = ev.step(p.v.tensor_copy(keep[:, 48:56], ps[:, 1024:1032]))
        p.vd()
        p.wait("tensor", ev, vv)

    a1 = act.step(p.s.activation(stats[0:8, 93:94], C(cx, "a_log", 8), AF.Exp))
    p.sd()
    p.wait("vector", act, a1)
    p.v.tensor_scalar_mul(stats[0:8, 93:94], stats[0:8, 93:94], -1.0)
    p.vd()
    ev.step(p.v.engine_nop())

    p.barrier()

    for sup in range(16 if full else 9):
        _mamba_super(cx, sup, full)
        if not full and sup == 0:
            p.vd()
            p.v.memset(hst[:, :], 0.0)
            p.vd()
            p.v.memset(stats[0:64, 112:120], 1.0)
            p.vd()
            ev.step(p.v.engine_nop())


def _mamba_super(cx, sup, full):
    p, sems, sb, dr, ps = cx.p, cx.sems, cx.sb, cx.dr, cx.ps
    ld, wl2, mm, ev, act = (sems["ld"], sems["wl2"], sems["mm"],
                            sems["ev"], sems["act"])
    bb, bx, stats, keep, hst = (sb["BB"], sb["BX"], sb["stats"],
                                sb["keep"], sb["hstate"])

    if full and getattr(cx, "_xn_pre", -1) == sup:
        xn_ready = cx._xn_ready
        sv = None
    elif full:
        p.vd()
        p.wait("vector", mm, mm.n)
        sv = ev.step(p.v.tensor_scalar(bb[:, XST:XST + 1024],
                                       bx[:, sup * 1024:(sup + 1) * 1024],
                                       stats[:, sup:sup + 1],
                                       stats[:, 16 + sup:16 + sup + 1],
                                       ALU.subtract, ALU.mult))
        p.vd()
    else:
        p.wait("sync", mm, sems["mm"].n)
        p.wait("sync", ev, ev.n)
        lv = ld.step(p.sy.dma_start(out=bb[:, XST:XST + 1024],
                                    in_=dr["x_prev"][sup * 128:(sup + 1) * 128, :]),
                     16)
        p.wait("vector", ld, lv)
        p.wait("scalar", ld, lv)
        p.vd()
        p.v.reduce_sum(stats[:, 94:95], bb[:, XST:XST + 1024], axis=AXX)
        p.vd()
        p.sd()
        aa = act.step(p.s.activation(bb[:, SZ:SZ + 1024], bb[:, XST:XST + 1024],
                                     AF.Square, accum_out=stats[:, 95:96]))
        p.wait("vector", act, aa)
        inv = 1.0 / D
        p.v.tensor_scalar_mul(stats[:, 94:95], stats[:, 94:95], inv)
        p.v.tensor_scalar_mul(stats[:, 95:96], stats[:, 95:96], inv)
        p.vd()
        p.v.tensor_tensor(stats[:, 92:93], stats[:, 94:95], stats[:, 94:95], ALU.mult)
        p.vd()
        p.v.tensor_sub(stats[:, 95:96], stats[:, 95:96], stats[:, 92:93])
        p.vd()
        vv = ev.step(p.v.tensor_scalar_add(stats[:, 95:96], stats[:, 95:96], 1e-5))
        p.vd()
        _rsqrt(cx, stats[:, 92:93], stats[:, 95:96], stats[:, 91:92], (ev, vv))
        sv = ev.step(p.v.tensor_scalar(bb[:, XST:XST + 1024], bb[:, XST:XST + 1024],
                                       stats[:, 94:95], stats[:, 92:93],
                                       ALU.subtract, ALU.mult))
        p.vd()

    if sv is not None:
        dsts = [bb[:, XNF + k * 128:XNF + (k + 1) * 128] for k in range(8)]
        _transpose128(cx, bb[:, XST:XST + 1024], dsts, (ev, sv),
                      pbase=0, n_rot=4)
        xn_ready = ev.n

    p.wait("tensor", ev, xn_ready)
    p.wait("tensor", ev, ev.n)
    ring_guard = getattr(cx, "_mring", [0, 0, 0, 0])
    cx._mring = ring_guard
    ring_i = getattr(cx, "_mring_i", 0)

    ssems = (wl2, sems["qa"], sems["qb"], sems["qc"])

    def stream_tile(col0, ncols, kchunk):
        nonlocal ring_i
        slot = ring_i % 4
        ring_i += 1
        sem = ssems[slot]
        p.wait("sync", mm, ring_guard[slot])
        wv = sem.step(p.sy.dma_start(
            out=bb[:, RING + slot * 512:RING + slot * 512 + ncols],
            in_=dr["wip_t"][kchunk * 128:(kchunk + 1) * 128, col0:col0 + ncols]), 16)
        return slot, wv, sem

    bank_act = [0, 0]
    if full:
        for g in range(4):
            pb = (g % 2) * 512
            if bank_act[g % 2]:
                p.wait("tensor", act, bank_act[g % 2])
            for k in range(8):
                slot, wv, sem = stream_tile(g * 512, 512, k)
                p.wait("tensor", sem, wv)
                tok = p.t.matmul(ps[:, pb:pb + 512],
                                 bb[:, XNF + k * 128:XNF + (k + 1) * 128],
                                 bb[:, RING + slot * 512:RING + slot * 512 + 512],
                                 start=(k == 0), stop=(k == 7))
                ring_guard[slot] = mm.step(tok)
            mv = mm.n
            p.wait("scalar", mm, mv)
            p.sd()
            bank_act[g % 2] = act.step(p.s.activation(
                bb[:, SZ + g * 512:SZ + (g + 1) * 512],
                ps[:, pb:pb + 512], AF.Silu))
        p.wait("tensor", act, act.n)
    for g in range(4):
        for k in range(8):
            slot, wv, sem = stream_tile(2048 + g * 512, 512, k)
            p.wait("tensor", sem, wv)
            for fs in range(4):
                tok = p.t.matmul(
                    ps[:, fs * 512:fs * 512 + 128],
                    bb[:, RING + slot * 512 + fs * 128:
                        RING + slot * 512 + fs * 128 + 128],
                    bb[:, XNF + k * 128:XNF + (k + 1) * 128],
                    start=(k == 0), stop=(k == 7))
            ring_guard[slot] = mm.step(tok)
        mv = mm.n
        p.wait("vector", mm, mv)
        for fs in range(4):
            f = g * 4 + fs
            ev.step(p.v.tensor_copy(
                bb[:, XCE + f * 144 + 3:XCE + f * 144 + 3 + 128],
                ps[:, fs * 512:fs * 512 + 128]))
        p.vd()
        p.wait("tensor", ev, ev.n)
    for k in range(8):
        slot, wv, sem = stream_tile(4096, 136, k)
        p.wait("tensor", sem, wv)
        st0 = RING + slot * 512
        p.t.matmul(ps[0:8, 1024:1152], bb[:, st0:st0 + 8],
                   bb[:, XNF + k * 128:XNF + (k + 1) * 128],
                   start=(k == 0), stop=(k == 7))
        tok = p.t.matmul(ps[0:64, 1536:1664], bb[:, st0 + 8:st0 + 72],
                         bb[:, XNF + k * 128:XNF + (k + 1) * 128],
                         start=(k == 0), stop=(k == 7))
        if full:
            tok = p.t.matmul(ps[0:64, 2048:2176], bb[:, st0 + 72:st0 + 136],
                             bb[:, XNF + k * 128:XNF + (k + 1) * 128],
                             start=(k == 0), stop=(k == 7))
        ring_guard[slot] = mm.step(tok)
    mv = mm.n
    bcdt_mv = mv
    p.wait("vector", mm, mv)
    ev.step(p.v.tensor_copy(bb[0:8, BCD_B:BCD_B + 128], ps[0:8, 1024:1152]))
    ev.step(p.v.tensor_copy(bb[0:64, BCD_C:BCD_C + 128], ps[0:64, 1536:1664]))
    if full:
        ev.step(p.v.tensor_copy(bb[0:64, BCD_D:BCD_D + 128], ps[0:64, 2048:2176]))
    bcd_ev = ev.n
    p.vd()
    cx._mring_i = ring_i

    p.wait("vector", mm, mv)
    p.v.tensor_copy(
        bb[:, XCE:XCE + 2304].rearrange("p (f c) -> p f c", c=144)[:, :, 0:3],
        keep[:, 0:48].rearrange("p (f c) -> p f c", c=3))
    p.vd()
    p.v.tensor_copy(
        keep[:, 0:48].rearrange("p (f c) -> p f c", c=3),
        bb[:, XCE:XCE + 2304].rearrange("p (f c) -> p f c", c=144)[:, :, 128:131])
    p.vd()
    dst3 = bb[:, XH:XH + 2048].rearrange("p (f j) -> p f j", j=128)
    tmp3 = bb[:, RING:RING + 2048].rearrange("p (f j) -> p f j", j=128)
    xce3 = bb[:, XCE:XCE + 2304].rearrange("p (f c) -> p f c", c=144)
    for tap in range(4):
        src3 = xce3[:, :, tap:tap + 128]
        w3 = (C(cx, "conv_w")[:, tap:tap + 61:4]
              .unsqueeze(2).broadcast_to([128, 16, 128]))
        if tap == 0:
            p.v.tensor_tensor(dst3, src3, w3, ALU.mult)
        else:
            p.v.tensor_tensor(tmp3, src3, w3, ALU.mult)
            p.vd()
            p.v.tensor_tensor(dst3, dst3, tmp3, ALU.add)
        p.vd()
    cv = ev.step(p.v.engine_nop())
    p.wait("scalar", ev, cv)
    p.sd()
    for f in range(16):
        aa = act.step(p.s.activation(bb[:, XH + f * 128:XH + (f + 1) * 128],
                                     bb[:, XH + f * 128:XH + (f + 1) * 128],
                                     AF.Silu, bias=C(cx, "conv_b")[:, f:f + 1]))
    p.sd()
    xh_ready = act.n

    p.wait("scalar", ev, bcd_ev)
    p.sd()
    aa = act.step(p.s.activation(bb[0:8, S_DT:S_DT + 128],
                                 bb[0:8, BCD_B:BCD_B + 128], AF.Exp,
                                 bias=C(cx, "dt_bias", 8)))
    p.wait("vector", act, aa)
    vv_sp = ev.step(p.v.tensor_scalar_add(bb[0:8, S_DT:S_DT + 128],
                                          bb[0:8, S_DT:S_DT + 128], 1.0))
    p.vd()
    p.wait("scalar", ev, vv_sp)
    p.sd()
    aa = act.step(p.s.activation(bb[0:8, S_DT:S_DT + 128],
                                 bb[0:8, S_DT:S_DT + 128], AF.Ln))
    p.wait("vector", act, aa)
    p.v.tensor_scalar(bb[0:8, S_LA:S_LA + 128], bb[0:8, S_DT:S_DT + 128],
                      stats[0:8, 93:94], None, ALU.mult)
    p.vd()
    p.v.reduce_sum(bb[0:8, S_SCOL:S_SCOL + 1], bb[0:8, S_LA:S_LA + 128], axis=AXX)
    p.vd()
    lv = ev.step(p.v.engine_nop())
    p.wait("tensor", ev, lv)
    m1 = mm.step(p.t.transpose(ps[:, 1536:1544], bb[0:8, S_LA:S_LA + 128],
                               C(cx, "ident", 8, 8)))
    m2 = mm.step(p.t.transpose(ps[0:1, 1544:1552], bb[0:8, S_SCOL:S_SCOL + 1],
                               C(cx, "ident", 8, 8)))
    p.wait("vector", mm, m2)
    ev.step(p.v.tensor_copy(bb[:, S_LATM:S_LATM + 8], ps[:, 1536:1544]))
    ev.step(p.v.tensor_copy(bb[0:1, S_SROW:S_SROW + 8], ps[0:1, 1544:1552]))
    lv2 = ev.n
    p.vd()
    p.wait("tensor", ev, lv2)
    m3 = mm.step(p.t.matmul(ps[:, 1792:1800], C(cx, "tri_incl"),
                            bb[:, S_LATM:S_LATM + 8], start=True, stop=True))
    m4 = mm.step(p.t.matmul(ps[:, 3584:3592], ONES1(cx, 128),
                            bb[0:1, S_SROW:S_SROW + 8], start=True, stop=True))
    m5 = mm.step(p.t.matmul(ps[0:64, 1024:1032], ONES1(cx, 64),
                            bb[0:1, S_SROW:S_SROW + 8], start=True, stop=True))
    p.wait("vector", mm, m5)
    ev.step(p.v.tensor_copy(bb[:, S_STM:S_STM + 8], ps[:, 1792:1800]))
    p.vd()
    p.v.tensor_sub(bb[:, S_SBC:S_SBC + 8], ps[:, 3584:3592], bb[:, S_STM:S_STM + 8])
    p.vd()
    bc_ev = ev.step(p.v.tensor_copy(bb[0:64, S_DBC:S_DBC + 8], ps[0:64, 1024:1032]))
    p.vd()
    p.wait("scalar", ev, bc_ev)
    p.sd()
    act.step(p.s.activation(bb[:, S_ETM:S_ETM + 8], bb[:, S_SBC:S_SBC + 8], AF.Exp))
    p.sd()
    act.step(p.s.activation(bb[:, S_E2:S_E2 + 8], bb[:, S_STM:S_STM + 8], AF.Exp))
    p.sd()
    prep_a = act.step(p.s.activation(bb[0:64, S_DBC:S_DBC + 8],
                                     bb[0:64, S_DBC:S_DBC + 8], AF.Exp))
    p.sd()
    p.wait("vector", act, prep_a)
    if not full:
        p.v.tensor_tensor(stats[0:64, 112:120], stats[0:64, 112:120],
                          bb[0:64, S_DBC:S_DBC + 8], ALU.mult)
        p.vd()
    ev.step(p.v.engine_nop())

    p.wait("tensor", ev, ev.n)
    m6 = mm.step(p.t.transpose(ps[:, 1920:1984], bb[0:64, BCD_C:BCD_C + 128],
                               C(cx, "ident", 64, 64)))
    p.wait("vector", mm, m6)
    btm_ev = ev.step(p.v.tensor_copy(bb[:, BTM:BTM + 64], ps[:, 1920:1984]))
    p.vd()

    if full:
        p.wait("tensor", ev, bc_ev)
        m7 = mm.step(p.t.matmul(ps[:, 1024:1152], bb[0:64, BCD_C:BCD_C + 128],
                                bb[0:64, BCD_D:BCD_D + 128], start=True, stop=True))
        p.wait("vector", mm, m7)
        gms_ev = ev.step(p.v.tensor_tensor(bb[:, GMS:GMS + 128], ps[:, 1024:1152],
                                           C(cx, "tri_incl"), ALU.mult))
        p.vd()
    else:
        gms_ev = 0

    PBCa, GTa, BEa, XTM2 = RING, RING + 1024, RING, GMS
    ident = C(cx, "ident")
    p.wait("tensor", act, xh_ready)
    if full:
        p.wait("vector", mm, bcdt_mv)
        for h in range(NH):
            p.v.tensor_scalar(bb[:, PBCa + h * 128:PBCa + (h + 1) * 128],
                              ident, 0.0, bb[:, S_STM + h:S_STM + h + 1],
                              ALU.mult, ALU.add)
        pav = ev.step(p.v.engine_nop())
        p.vd()
        p.wait("tensor", ev, pav)
        bank_ev = [0, 0, 0, 0]
        lastme = 0
        for h in range(NH):
            b = h % 4
            if bank_ev[b]:
                p.wait("tensor", ev, bank_ev[b])
            pc = 1024 + b * 512
            lastme = mm.step(p.t.transpose(
                ps[:, pc:pc + 128], bb[:, PBCa + h * 128:PBCa + (h + 1) * 128],
                ident))
            p.wait("vector", mm, lastme)
            bank_ev[b] = ev.step(p.v.tensor_scalar(
                bb[:, GTa + h * 128:GTa + (h + 1) * 128], ps[:, pc:pc + 128],
                bb[:, S_STM + h:S_STM + h + 1], 0.0, ALU.subtract, ALU.min))
        p.vd()
        dv = ev.n
        p.wait("scalar", ev, dv)
        p.sd()
        for h in range(NH):
            aa = act.step(p.s.activation(bb[:, GTa + h * 128:GTa + (h + 1) * 128],
                                         bb[:, GTa + h * 128:GTa + (h + 1) * 128],
                                         AF.Exp))
        p.wait("vector", act, aa)
        for h in range(NH):
            p.v.tensor_tensor(bb[:, GTa + h * 128:GTa + (h + 1) * 128],
                              bb[:, GTa + h * 128:GTa + (h + 1) * 128],
                              bb[:, GMS:GMS + 128], ALU.mult)
        gt_all = ev.step(p.v.engine_nop())
        p.vd()
    else:
        gt_all = 0
    p.wait("vector", mm, lastme if full else bcdt_mv)
    p.wait("vector", ev, btm_ev)
    for h in range(NH):
        p.v.tensor_scalar(bb[:, BEa + h * 64:BEa + (h + 1) * 64],
                          bb[:, BTM:BTM + 64],
                          bb[:, S_ETM + h:S_ETM + h + 1], None, ALU.mult)
    be_all = ev.step(p.v.engine_nop())
    p.vd()

    xtm_guard = [0, 0]
    xtm_ev = [0, 0]
    xtm_dve = [0, 0]
    yi_guard = 0
    y2_guard = 0
    u_guard = 0
    p.wait("tensor", ev, be_all)
    if full:
        p.wait("tensor", ev, gt_all)
    for h in range(NH):
        sl = h % 2
        xb = XTM if sl == 0 else XTM2
        pb4 = 2048 if sl == 0 else 1536
        p.wait("tensor", act, xtm_ev[sl])
        for half in range(2):
            m8 = mm.step(p.t.transpose(
                ps[:, pb4 + half * 128:pb4 + (half + 1) * 128],
                bb[:, XH + h * 256 + half * 128:XH + h * 256 + (half + 1) * 128],
                ident))
        p.wait("scalar", mm, m8)
        p.wait("scalar", mm, xtm_guard[sl])
        p.wait("scalar", ev, xtm_dve[sl])
        p.sd()
        xtm_ev[sl] = act.step(p.s.activation(bb[:, xb:xb + 256],
                                             ps[:, pb4:pb4 + 256], AF.Copy))
        p.wait("tensor", act, xtm_ev[sl])
        if full:
            p.wait("tensor", ev, yi_guard)
            myi = mm.step(p.t.matmul(ps[:, 2560:2816],
                                     bb[:, GTa + h * 128:GTa + (h + 1) * 128],
                                     bb[:, xb:xb + 256], start=True, stop=True))
            p.wait("tensor", act, y2_guard)
            my2 = mm.step(p.t.matmul(ps[:, 3072:3328], bb[0:64, BCD_D:BCD_D + 128],
                                     hst[:, h * 256:(h + 1) * 256],
                                     start=True, stop=True))
        p.wait("tensor", ev, u_guard)
        mu_ = mm.step(p.t.matmul(ps[0:64, 3584:3840], bb[:, BEa + h * 64:BEa + (h + 1) * 64],
                                 bb[:, xb:xb + 256], start=True, stop=True))
        xtm_guard[sl] = mu_
        p.wait("vector", mm, mu_)
        u_guard = ev.step(p.v.scalar_tensor_tensor(
            hst[:, h * 256:(h + 1) * 256], hst[:, h * 256:(h + 1) * 256],
            bb[0:64, S_DBC + h:S_DBC + h + 1], ps[0:64, 3584:3840],
            ALU.mult, ALU.add))
        if full:
            yb = XCE + h * 256
            p.wait("scalar", mm, my2)
            p.sd()
            y2_guard = act.step(p.s.activation(
                bb[:, yb:yb + 256], ps[:, 3072:3328], AF.Copy,
                scale=bb[:, S_E2 + h:S_E2 + h + 1]))
            p.wait("vector", mm, myi)
            p.wait("vector", act, y2_guard)
            yi_guard = ev.step(p.v.tensor_tensor(
                bb[:, yb:yb + 256], bb[:, yb:yb + 256],
                ps[:, 2560:2816], ALU.add))
            p.vd()
            xtm_dve[sl] = ev.step(p.v.scalar_tensor_tensor(
                bb[:, yb:yb + 256], bb[:, xb:xb + 256],
                keep[:, 48 + h:48 + h + 1],
                bb[:, yb:yb + 256], ALU.mult, ALU.add))
            p.vd()
    mend = mm.n
    for i in range(4):
        ring_guard[i] = max(ring_guard[i], mend)
    cx._last_m8 = m8

    if full:
        _mamba_tail(cx, sup)


def _mamba_tail(cx, sup):
    p, sems, sb, dr, ps = cx.p, cx.sems, cx.sb, cx.dr, cx.ps
    mm, ev, act = sems["mm"], sems["ev"], sems["act"]
    bb, bx, stats, keep = sb["BB"], sb["BX"], sb["stats"], sb["keep"]
    wl2 = sems["wl2"]

    if sup + 1 < 16:
        p.wait("vector", mm, cx._last_m8)
        sv2 = ev.step(p.v.tensor_scalar(bb[:, XST:XST + 1024],
                                        bx[:, (sup + 1) * 1024:(sup + 2) * 1024],
                                        stats[:, sup + 1:sup + 2],
                                        stats[:, 17 + sup:18 + sup],
                                        ALU.subtract, ALU.mult))
        p.vd()
        p.wait("tensor", act, act.n)
        dsts = [bb[:, XNF + k * 128:XNF + (k + 1) * 128] for k in range(8)]
        _transpose128(cx, bb[:, XST:XST + 1024], dsts, (ev, sv2),
                      pbase=2048, n_rot=4)
        cx._xn_pre = sup + 1
        cx._xn_ready = ev.n

    p.v.tensor_tensor(bb[:, XCE:XCE + 2048], bb[:, XCE:XCE + 2048],
                      bb[:, SZ:SZ + 2048], ALU.mult)
    yv = ev.step(p.v.engine_nop())
    p.vd()
    p.wait("scalar", ev, yv)
    p.sd()
    aa = act.step(p.s.activation(bb[:, SZ:SZ + 2048], bb[:, XCE:XCE + 2048],
                                 AF.Square, accum_out=stats[:, 92:93]))
    p.wait("vector", act, aa)
    vv = ev.step(p.v.tensor_scalar(stats[:, 92:93], stats[:, 92:93],
                                   1.0 / DIN, 1e-6, ALU.mult, ALU.add))
    p.vd()
    _rsqrt(cx, bb[:, S_RSTD:S_RSTD + 1], stats[:, 92:93],
           bb[:, S_RSCR:S_RSCR + 1], (ev, vv))
    p.v.scalar_tensor_tensor(bb[:, SZ:SZ + 2048], bb[:, XCE:XCE + 2048],
                             bb[:, S_RSTD:S_RSTD + 1], bb[:, NORMB:NORMB + 2048],
                             ALU.mult, ALU.mult)
    rv = ev.step(p.v.engine_nop())
    p.vd()
    dsts = [bb[:, XCE + k * 128:XCE + (k + 1) * 128] for k in range(16)]
    _transpose128(cx, bb[:, SZ:SZ + 2048], dsts, (ev, rv), pbase=0, n_rot=4)
    yfm_ready = ev.n

    opr_guard = getattr(cx, "_opr_guard", [0, 0])
    osems = (sems["oa"], sems["ob"])
    p.wait("tensor", ev, yfm_ready)
    for k in range(16):
        slot = k % 2
        osem = osems[slot]
        p.wait("sync", mm, opr_guard[slot])
        wv = osem.step(p.sy.dma_start(
            out=bb[:, OPR + slot * 1024:OPR + (slot + 1) * 1024],
            in_=dr["wop_t"][k * 128:(k + 1) * 128, :]), 16)
        p.wait("tensor", osem, wv)
        for half in range(2):
            tok = p.t.matmul(ps[:, half * 512:(half + 1) * 512],
                             bb[:, XCE + k * 128:XCE + (k + 1) * 128],
                             bb[:, OPR + slot * 1024 + half * 512:
                                 OPR + slot * 1024 + (half + 1) * 512],
                             start=(k == 0), stop=(k == 15))
        opr_guard[slot] = mm.step(tok)
    cx._opr_guard = opr_guard
    mv = mm.n
    g0 = _g0_ap(cx, sup)
    p.wait("vector", mm, mv)
    p.v.tensor_tensor(bb[:, SZ:SZ + 1024].rearrange("p (k t) -> p k t", t=128),
                      ps[:, 0:1024].rearrange("p (k t) -> p k t", t=128),
                      g0, ALU.mult)
    fv = ev.step(p.v.engine_nop())
    p.vd()
    p.v.tensor_tensor(bx[:, sup * 1024:(sup + 1) * 1024],
                      bx[:, sup * 1024:(sup + 1) * 1024],
                      bb[:, SZ:SZ + 1024], ALU.add)
    fz = ev.step(p.v.engine_nop())
    p.vd()
    p.wait("tensor", ev, fz)


def _stage_prev_state(cx):
    _mamba_pass(cx, full=False)
    p, sems, sb, dr, ps = cx.p, cx.sems, cx.sb, cx.dr, cx.ps
    mm, ev, act, ld = sems["mm"], sems["ev"], sems["act"], sems["ld"]
    bb, stats, keep, hst = sb["BB"], sb["stats"], sb["keep"], sb["hstate"]
    RCV, SND = OPR, OPR + 1024

    ld.step(p.sy.dma_start(out=stats[0:1, 120:121], in_=dr["pgate"][:, :]), 16)
    p.wait("sync", ev, ev.n)
    ld.step(p.sy.dma_start(out=bb[0:64, SND:SND + 1024],
                           in_=hst[0:64, 0:1024]), 16)
    ld.step(p.sy.dma_start(out=bb[64:128, SND:SND + 1024],
                           in_=hst[0:64, 1024:2048]), 16)

    prep, l2, r3 = sems["prep"], sems["lsem2"], sems["rsem3"]
    p.wait("gpsimd", ld, ld.n)
    rdests = [None] * 8
    rdests[1] = (0, 1)
    pv = prep.step(p.g.remote_dma_broadcast(
        bb[:, RCV:RCV + 1024], bb[:, SND:SND + 1024], r3.h, l2.h, rdests=rdests))
    p.wait("gpsimd", prep, pv)
    p.g.trigger_dma()
    l2.n += 16
    r3.n += 2
    p.wait("gpsimd", l2, l2.n)
    p.wait("gpsimd", r3, r3.n)
    gv = sems["gp"].step(p.g.engine_nop())

    p.wait("vector", sems["gp"], gv)
    p.v.tensor_copy(bb[:, RCV:RCV + 1024].bitcast(mybir.dt.int32),
                    bb[:, RCV:RCV + 1024].bitcast(mybir.dt.int32))
    pgv = ev.step(p.v.engine_nop())
    p.vd()
    p.wait("sync", ev, pgv)
    uf = ld.step(p.sy.dma_start(out=bb[0:64, SND:SND + 1024],
                                in_=bb[64:128, RCV:RCV + 1024]), 16)
    p.wait("vector", ld, uf)
    p.wait("tensor", ev, pgv)
    mv = mm.step(p.t.matmul(ps[:, 1024:1025], ONES1(cx, 128),
                            stats[0:1, 120:121], start=True, stop=True))
    p.wait("vector", mm, mv)
    p.v.tensor_copy(stats[:, 121:122], ps[:, 1024:1025])
    p.vd()
    for h in range(NH):
        rh = (bb[0:64, RCV + h * 256:RCV + (h + 1) * 256] if h < 4
              else bb[0:64, SND + (h - 4) * 256:SND + (h - 3) * 256])
        p.v.scalar_tensor_tensor(hst[:, h * 256:(h + 1) * 256], rh,
                                 stats[0:64, 112 + h:113 + h],
                                 hst[:, h * 256:(h + 1) * 256],
                                 ALU.mult, ALU.add)
    p.vd()
    p.v.tensor_scalar(hst[:, 0:2048], hst[:, 0:2048],
                      stats[0:64, 121:122], None, ALU.mult)
    p.vd()
    p.v.tensor_scalar(keep[:, 0:48], keep[:, 0:48],
                      stats[:, 121:122], None, ALU.mult)
    ev.step(p.v.engine_nop())
    p.vd()
    p.wait("sync", ev, ev.n)


def ONES1(cx, m):
    return C(cx, "ones_row", 1)[0:1, 0:m]


def _xn0g(cx, k, g):
    return cx.sb["BA"][:, g * 4096 + k * 512: g * 4096 + k * 512 + 512]


def _stage_experts(cx):
    p, sems, sb, dr, ps = cx.p, cx.sems, cx.sb, cx.dr, cx.ps
    ld, wl, mm, ev, act = sems["ld"], sems["wl"], sems["mm"], sems["ev"], sems["act"]
    bb, stats = sb["BB"], sb["stats"]
    wguard = [0, 0]
    wguard_ev = [0, 0]
    wguard_act = [0, 0]
    e_losscol = 13500

    p.v.memset(bb[:, e_losscol:e_losscol + 1], 0.0)
    p.vd()

    wv_prev = 0
    for e in range(NE):
        buf = e % 2
        w0 = buf * 4096
        p.wait("sync", mm, wguard[buf])
        p.wait("sync", ev, wguard_ev[buf])
        p.wait("sync", act, wguard_act[buf])
        p.wait("sync", wl, wv_prev)
        for k in range(8):
            wl.step(p.sy.dma_start(
                out=bb[:, w0 + k * 256: w0 + k * 256 + 256],
                in_=dr["wd1"][e * D + k * 128: e * D + (k + 1) * 128, :]), 16)
        for fc in range(2):
            wl.step(p.sy.dma_start(
                out=bb[:, w0 + 2048 + fc * 1024: w0 + 2048 + (fc + 1) * 1024],
                in_=dr["wd2"][e * HE + fc * 128: e * HE + (fc + 1) * 128, :]), 16)
        for fc in range(2):
            wl.step(p.sy.dma_start(
                out=bb[:, 12864 + buf * 2 + fc: 12864 + buf * 2 + fc + 1],
                in_=dr["bd1"][e * HE + fc * 128: e * HE + (fc + 1) * 128, :]), 16)
        for k in range(8):
            wl.step(p.sy.dma_start(
                out=bb[:, 12880 + buf * 8 + k: 12880 + buf * 8 + k + 1],
                in_=dr["bd2"][e * D + k * 128: e * D + (k + 1) * 128, :]), 16)
        wv = wl.n
        wv_prev = wv

        p.wait("tensor", wl, wv)
        hev = [0, 0]
        for g in range(4):
            for fc in range(2):
                hb = 2048 + fc * 512
                p.wait("tensor", act, hev[fc])
                for k in range(8):
                    tok = p.t.matmul(ps[:, hb:hb + 512],
                                     bb[:, w0 + k * 256 + fc * 128: w0 + k * 256 + fc * 128 + 128],
                                     _xn0g(cx, k, g),
                                     start=(k == 0), stop=(k == 7))
                mv = mm.step(tok)
                p.wait("scalar", mm, mv)
                p.sd()
                a = act.step(p.s.activation(
                    bb[:, 8192 + fc * 2048 + g * 512: 8192 + fc * 2048 + g * 512 + 512],
                    ps[:, hb:hb + 512], AF.Relu,
                    bias=bb[:, 12864 + buf * 2 + fc: 12864 + buf * 2 + fc + 1]))
                hev[fc] = a
        hready = act.n

        p.wait("tensor", act, hready)
        rguard = [0, 0, 0, 0]
        accv = 0
        for dchunk in range(8):
            for g in range(4):
                bank = (dchunk * 4 + g) % 4
                rb = bank * 512
                p.wait("tensor", ev, rguard[bank])
                for fc in range(2):
                    tok = p.t.matmul(ps[:, rb:rb + 512],
                                     bb[:, w0 + 2048 + fc * 1024 + dchunk * 128:
                                         w0 + 2048 + fc * 1024 + dchunk * 128 + 128],
                                     bb[:, 8192 + fc * 2048 + g * 512:
                                         8192 + fc * 2048 + g * 512 + 512],
                                     start=(fc == 0), stop=(fc == 1))
                mv = mm.step(tok)
                p.wait("vector", mm, mv)
                ve = ev.step(p.v.scalar_tensor_tensor(
                    bb[:, 12288:12800], ps[:, rb:rb + 512],
                    bb[:, 12880 + buf * 8 + dchunk: 12880 + buf * 8 + dchunk + 1],
                    _xn0g(cx, dchunk, g), ALU.add, ALU.subtract))
                rguard[bank] = ve
                p.wait("scalar", ev, ve)
                p.wait("scalar", ev, accv)
                p.sd()
                a = act.step(p.s.activation(
                    bb[:, 12900:13412], bb[:, 12288:12800], AF.Square,
                    accum_out=stats[:, 92:93]))
                p.wait("vector", act, a)
                p.vd()
                accv = ev.step(p.v.tensor_add(bb[:, e_losscol:e_losscol + 1],
                                              bb[:, e_losscol:e_losscol + 1],
                                              stats[:, 92:93]))
                p.vd()
        wguard[buf] = mm.n
        wguard_ev[buf] = ev.n
        wguard_act[buf] = act.n

        p.wait("tensor", ev, accv)
        mv = mm.step(p.t.matmul(ps[0:1, 3072:3073],
                                bb[:, e_losscol:e_losscol + 1],
                                C(cx, "ones_col"), start=True, stop=True))
        p.wait("vector", mm, mv)
        ev.step(p.v.tensor_copy(stats[0:1, 80 + e:80 + e + 1], ps[0:1, 3072:3073]))
        p.vd()
        p.v.memset(bb[:, e_losscol:e_losscol + 1], 0.0)
        p.vd()

    gp, prep, lsem, rsem = sems["gp"], sems["prep"], sems["lsem"], sems["rsem"]
    vready = ev.n
    p.wait("gpsimd", ev, vready)
    rdests = [None] * 8
    rdests[1] = (0, 1)
    pv = prep.step(p.g.remote_dma_broadcast(
        stats[:, 72:80], stats[:, 80:88], rsem.h, lsem.h, rdests=rdests))
    p.wait("gpsimd", prep, pv)
    p.g.trigger_dma()
    lsem.n += 16
    rsem.n += 2
    dr = cx.dr
    _quant_weights(cx, dr["w1_t"], D, HB, dr["w1i_hbm"],
                   qw_col=cx.sb["stats"][:, 96:97])
    _quant_weights(cx, dr["w2_t"], HB, D, dr["w2i_hbm"],
                   qw_col=cx.sb["stats"][:, 97:98])
    p.wait("gpsimd", lsem, lsem.n)
    p.wait("gpsimd", rsem, rsem.n)
    gv = sems["gp"].step(p.g.engine_nop())
    p.wait("vector", sems["gp"], gv)
    p.v.tensor_copy(stats[:, 72:80].bitcast(mybir.dt.int32),
                    stats[:, 72:80].bitcast(mybir.dt.int32))
    p.vd()
    p.v.tensor_add(stats[0:1, 88:96], stats[0:1, 80:88], stats[0:1, 72:80])
    p.vd()
    p.v.tensor_reduce(stats[0:1, 71:72], stats[0:1, 88:96], AXX, ALU.min)
    p.vd()
    p.v.tensor_single_scalar(stats[0:1, 64:72], stats[0:1, 88:96],
                             stats[0:1, 71:72], ALU.is_le)
    p.vd()
    ev.step(p.v.engine_nop())


def _g0_ap(cx, c):
    g, cc = c // 4, c % 4
    return (cx.sb["BA"][:, g * 4096:(g + 1) * 4096]
            .rearrange("p (k t) -> p k t", t=512)[:, :, cc * 128:(cc + 1) * 128])


def _stage_gate_ffn(cx):
    p, sems, sb, dr, ps = cx.p, cx.sems, cx.sb, cx.dr, cx.ps
    ld, wl, mm, ev, act = sems["ld"], sems["wl"], sems["mm"], sems["ev"], sems["act"]
    bb, ba, stats = sb["BB"], sb["BA"], sb["stats"]

    for k in range(8):
        wl.step(p.sy.dma_start(out=bb[:, k * 1024:(k + 1) * 1024],
                               in_=dr["w_base"][k * 128:(k + 1) * 128, :]), 16)
    wl.step(p.sy.dma_start(out=bb[0:8, 13072:13136], in_=dr["ba_down"][:, :]), 16)
    wl.step(p.sy.dma_start(out=bb[0:8, 13136:14160], in_=dr["ba_up"][:, :]), 16)
    wl.step(p.sy.dma_start(out=bb[0:1, 14160:15184], in_=dr["b_base"][:, :]), 16)
    wv0 = wl.n

    ohrow = stats[0:1, 64:72]
    m1 = mm.step(p.t.matmul(ps[0:8, 3072:3080], ONES1(cx, 8),
                            ohrow, start=True, stop=True))
    p.wait("vector", mm, m1)
    ev.step(p.v.tensor_copy(bb[0:8, 15184:15192], ps[0:8, 3072:3080]))
    p.vd()
    p.v.tensor_tensor(bb[0:8, 15184:15192], bb[0:8, 15184:15192],
                      C(cx, "ident", 8, 8), ALU.mult)
    p.vd()
    e1 = ev.step(p.v.reduce_sum(bb[0:8, 15376:15377], bb[0:8, 15184:15192], axis=AXX))
    p.vd()
    m2 = mm.step(p.t.matmul(ps[:, 3584:3592], ONES1(cx, 128), ohrow,
                            start=True, stop=True))
    p.wait("vector", mm, m2)
    ev.step(p.v.tensor_copy(bb[:, 10752:10760], ps[:, 3584:3592]))
    p.vd()

    p.wait("tensor", ev, e1)
    p.wait("tensor", wl, wv0)
    m3 = mm.step(p.t.matmul(ps[0:1, 3072:3136], bb[0:8, 15376:15377],
                            bb[0:8, 13072:13136], start=True, stop=True))
    p.wait("vector", mm, m3)
    e2 = ev.step(p.v.tensor_copy(bb[0:1, 15400:15464], ps[0:1, 3072:3136]))
    p.vd()
    m4 = mm.step(p.t.matmul(ps[0:1, 2048:2560], bb[0:8, 15376:15377],
                            bb[0:8, 13136:13136 + 512], start=True, stop=True))
    m4 = mm.step(p.t.matmul(ps[0:1, 2560:3072], bb[0:8, 15376:15377],
                            bb[0:8, 13136 + 512:13136 + 1024], start=True, stop=True))
    p.wait("vector", mm, m4)
    e3 = ev.step(p.v.tensor_copy(bb[0:1, 11024:12048],
                                 ps[0:1, 2048:3072]))
    p.vd()
    baup_row = bb[0:1, 11024:12048]

    p.wait("tensor", ev, e2)
    m5 = mm.step(p.t.matmul(ps[0:64, 3072:3136], ONES1(cx, 64),
                            bb[0:1, 15400:15464], start=True, stop=True))
    p.wait("vector", mm, m5)
    ev.step(p.v.tensor_copy(bb[0:64, 15312:15376], ps[0:64, 3072:3136]))
    p.vd()
    p.v.tensor_tensor(bb[0:64, 15312:15376], bb[0:64, 15312:15376],
                      C(cx, "ident", 64, 64), ALU.mult)
    p.vd()
    e4 = ev.step(p.v.reduce_sum(bb[0:64, 10764:10765], bb[0:64, 15312:15376], axis=AXX))
    p.vd()

    p.wait("tensor", ev, e3)
    for half in range(2):
        p.t.matmul(ps[:, half * 512:(half + 1) * 512], ONES1(cx, 128),
                   bb[0:1, 14160 + half * 512:14160 + (half + 1) * 512],
                   start=True, stop=False)
        tok = p.t.matmul(ps[:, half * 512:(half + 1) * 512], ONES1(cx, 128),
                         baup_row[0:1, half * 512:(half + 1) * 512],
                         start=False, stop=True)
    m6 = mm.step(tok)
    p.wait("vector", mm, m6)
    e5 = ev.step(p.v.tensor_copy(bb[:, 9728:10752], ps[:, 0:1024]))
    p.vd()

    osems = (sems["oa"], sems["ob"])
    ring = [10896, 10960]
    rl = [0, 0]
    for e in range(NE):
        for k in range(8):
            slot = k % 2
            p.wait("sync", ev, rl[slot])
            wv = osems[slot].step(p.sy.dma_start(
                out=bb[:, ring[slot]:ring[slot] + 64],
                in_=dr["wa_down"][e * D + k * 128:e * D + (k + 1) * 128, :]), 16)
            p.wait("vector", osems[slot], wv)
            if e == 0:
                tok = p.v.tensor_scalar(bb[:, 8192 + k * 64:8192 + (k + 1) * 64],
                                        bb[:, ring[slot]:ring[slot] + 64],
                                        bb[:, 10752 + e:10752 + e + 1], None,
                                        ALU.mult)
            else:
                tok = p.v.scalar_tensor_tensor(
                    bb[:, 8192 + k * 64:8192 + (k + 1) * 64],
                    bb[:, ring[slot]:ring[slot] + 64],
                    bb[:, 10752 + e:10752 + e + 1],
                    bb[:, 8192 + k * 64:8192 + (k + 1) * 64],
                    ALU.mult, ALU.add)
            rl[slot] = ev.step(tok)
            p.vd()
    rl2 = list(rl)
    for e in range(NE):
        slot = e % 2
        base = 12048 + slot * 1024
        p.wait("sync", ev, rl2[slot])
        wv = osems[slot].step(p.sy.dma_start(out=bb[0:64, base:base + 1024],
                                             in_=dr["wa_up"][e * RA:(e + 1) * RA, :]), 16)
        p.wait("vector", osems[slot], wv)
        if e == 0:
            p.v.tensor_scalar(bb[0:64, 8704:9728], bb[0:64, base:base + 1024],
                              bb[0:64, 10752 + e:10752 + e + 1], None, ALU.mult)
        else:
            p.v.scalar_tensor_tensor(bb[0:64, 8704:9728], bb[0:64, base:base + 1024],
                                     bb[0:64, 10752 + e:10752 + e + 1],
                                     bb[0:64, 8704:9728], ALU.mult, ALU.add)
        p.vd()
        rl2[slot] = ev.step(p.v.engine_nop())

    p.wait("tensor", ev, ev.n)
    p.wait("tensor", act, act.n)
    bank_guard = [0, 0]
    last_aa = 0
    for c in range(16):
        g, cc = c // 4, c % 4
        p.wait("tensor", act, last_aa)
        for k in range(8):
            tok = p.t.matmul(ps[0:64, 3072:3200],
                             bb[:, 8192 + k * 64:8192 + (k + 1) * 64],
                             _xn0c(cx, k, c), start=(k == 0), stop=(k == 7))
        ma = mm.step(tok)
        p.wait("scalar", mm, ma)
        p.sd()
        aa = act.step(p.s.activation(bb[0:64, 10768:10896], ps[0:64, 3072:3200],
                                     AF.Relu, bias=bb[0:64, 10764:10765]))
        last_aa = aa
        pb = (c % 2) * 1024
        p.wait("tensor", ev, bank_guard[c % 2])
        for k in range(8):
            for half in range(2):
                p.t.matmul(ps[:, pb + half * 512:pb + (half + 1) * 512],
                           _xn0c(cx, k, c),
                           bb[:, k * 1024 + half * 512:k * 1024 + (half + 1) * 512],
                           start=(k == 0), stop=False)
        p.wait("tensor", act, aa)
        for half in range(2):
            tok = p.t.matmul(ps[:, pb + half * 512:pb + (half + 1) * 512],
                             bb[0:64, 10768:10896],
                             bb[0:64, 8704 + half * 512:8704 + (half + 1) * 512],
                             start=False, stop=True)
        mv = mm.step(tok)
        g0 = _g0_ap(cx, c)
        p.wait("vector", mm, mv)
        vv = ev.step(p.v.tensor_tensor(
            g0, ps[:, pb:pb + 1024].rearrange("p (k t) -> p k t", t=128),
            bb[:, 9728:10752].rearrange("p (k t) -> p k t", t=128), ALU.add))
        bank_guard[c % 2] = vv
        p.wait("scalar", ev, vv)
        p.sd()
        act.step(p.s.activation(g0, g0, AF.Silu))
    p.wait("vector", act, act.n)
    ev.step(p.v.engine_nop())


def _xn0c(cx, k, c):
    g, cc = c // 4, c % 4
    base = g * 4096 + k * 512 + cc * 128
    return cx.sb["BA"][:, base:base + 128]


def _stage_mamba(cx):
    _mamba_pass(cx, full=True)


def _stage_l1_ffn(cx):
    p, sems, sb, dr, ps = cx.p, cx.sems, cx.sb, cx.dr, cx.ps
    ld, wl, wl2, mm, ev, act = (sems["ld"], sems["wl"], sems["wl2"], sems["mm"],
                                sems["ev"], sems["act"])
    bb, bx, ba, stats = sb["BB"], sb["BX"], sb["BA"], sb["stats"]
    H, X1I, X2I, N1BC, N2BC = 0, 4096, 5120, 11264, 12288

    chunks = [bx[:, c * 1024:(c + 1) * 1024] for c in range(16)]
    _layernorm_16(cx, chunks, stat_base=32, wait_sem=(sems["bar"], 0))
    p.barrier()

    lv = ld.step(p.sy.dma_start(out=bb[0:1, N1BC:N1BC + 1024],
                                in_=dr["norm1_w"][:, :]), 16)
    lv = ld.step(p.sy.dma_start(out=bb[0:1, N2BC:N2BC + 4096],
                                in_=dr["norm2_w"][:, :]), 16)
    p.wait("tensor", ld, lv)
    p.wait("tensor", ev, ev.n)
    for base, n in ((N1BC, 1024), (N2BC, 4096)):
        for q in range(n // 512):
            mv = mm.step(p.t.matmul(ps[:, 1024:1536], ONES1(cx, 128),
                                    bb[0:1, base + q * 512:base + (q + 1) * 512],
                                    start=True, stop=True))
            p.wait("vector", mm, mv)
            vv = ev.step(p.v.tensor_copy(bb[:, base + q * 512:base + (q + 1) * 512],
                                         ps[:, 1024:1536]))
            p.vd()
            p.wait("tensor", ev, vv)
    mv = mm.step(p.t.matmul(ps[:, 1024:1025], ONES1(cx, 128), C(cx, "act_s", 1),
                            start=True, stop=True))
    mv = mm.step(p.t.matmul(ps[:, 1025:1026], ONES1(cx, 128), C(cx, "act_b", 1),
                            start=True, stop=True))
    p.wait("vector", mm, mv)
    ev.step(p.v.tensor_copy(stats[:, 89:90], ps[:, 1024:1025]))
    ev.step(p.v.tensor_copy(stats[:, 90:91], ps[:, 1025:1026]))
    p.vd()
    p.barrier()

    wring_guard = [0, 0, 0, 0]
    h1_aa = [0, 0]
    x1tb, x2tb = sb["X1TB"], sb["X2TB"]
    wbv = sb["WB4"]
    WSL = [s * 1024 for s in range(4)]
    gds = (sems["gdma"], sems["gdb"], sems["gdc"], sems["gdd"])
    ident = C(cx, "ident")
    if not hasattr(cx, "_tb_ev"):
        cx._tb_ev = [0, 0, 0, 0]

    def t_one(src, dst):
        b = cx._tb_i = (getattr(cx, "_tb_i", -1) + 1) % 4
        if cx._tb_ev[b]:
            p.wait("tensor", ev, cx._tb_ev[b])
        pc = 1024 + b * 512
        me = mm.step(p.t.transpose(ps[:, pc:pc + 128], src, ident))
        p.wait("vector", mm, me)
        cx._tb_ev[b] = ev.step(p.v.tensor_copy(dst, ps[:, pc:pc + 128]))
        return cx._tb_ev[b]

    pend = None

    def flush_g1():
        nonlocal pend
        if pend is None:
            return
        pc_, pmv, psc = pend
        p.wait("vector", mm, pmv)
        p.v.tensor_scalar(ba[:, pc_ * 1024:(pc_ + 1) * 1024], ps[:, 0:1024],
                          stats[:, psc:psc + 1], None, ALU.mult)
        gv = ev.step(p.v.engine_nop())
        p.vd()
        p.wait("scalar", ev, gv)
        p.sd()
        act.step(p.s.activation(ba[:, pc_ * 1024:(pc_ + 1) * 1024],
                                ba[:, pc_ * 1024:(pc_ + 1) * 1024], AF.Silu))
        pend = None

    for c in range(16):
        SC, R1, R2, R3 = (84, 91, 92, 85) if c % 2 == 0 else (101, 102, 103, 104)
        p.v.tensor_scalar(bb[:, H:H + 1024], bx[:, c * 1024:(c + 1) * 1024],
                          stats[:, 32 + c:32 + c + 1], stats[:, 48 + c:48 + c + 1],
                          ALU.subtract, ALU.mult)
        p.vd()
        xv = ev.step(p.v.engine_nop())
        p.wait("scalar", ev, xv)
        p.sd()
        aa = act.step(p.s.activation(bb[:, H + 1024:H + 2048], bb[:, H:H + 1024],
                                     AF.Square, accum_out=stats[:, R2:R2 + 1]))
        p.wait("vector", act, aa)
        vv = ev.step(p.v.tensor_scalar(stats[:, R2:R2 + 1], stats[:, R2:R2 + 1],
                                       1.0 / D, 1e-8, ALU.mult, ALU.add))
        p.vd()
        _rsqrt(cx, stats[:, R1:R1 + 1], stats[:, R2:R2 + 1],
               stats[:, R3:R3 + 1], (ev, vv))
        p.v.scalar_tensor_tensor(bb[:, H:H + 1024], bb[:, H:H + 1024],
                                 stats[:, R1:R1 + 1], bb[:, N1BC:N1BC + 1024],
                                 ALU.mult, ALU.mult)
        p.vd()
        p.v.reduce_max(stats[:, R2:R2 + 1], bb[:, H:H + 1024], axis=AXX,
                       apply_absolute_value=True)
        p.vd()
        p.v.tensor_scalar_max(stats[:, R2:R2 + 1], stats[:, R2:R2 + 1], 1e-5)
        p.vd()
        _recip(cx, stats[:, R1:R1 + 1], stats[:, R2:R2 + 1])
        p.v.tensor_scalar_mul(stats[:, R1:R1 + 1], stats[:, R1:R1 + 1], 127.0)
        p.vd()
        p.v.tensor_scalar(stats[:, SC:SC + 1], stats[:, R2:R2 + 1], 1.0 / 127.0,
                          None, ALU.mult)
        p.vd()
        p.v.tensor_tensor(stats[:, SC:SC + 1], stats[:, SC:SC + 1],
                          stats[:, 96:97], ALU.mult)
        p.vd()
        p.v._custom_dve(RINT, out=bb[:, X1I:X1I + 1024], in0=bb[:, H:H + 1024],
                        s0=MAGIC, s1=stats[:, R1:R1 + 1])
        xv = ev.step(p.v.engine_nop())
        p.vd()
        p.wait("tensor", ev, xv)
        for k in range(8):
            t_one(bb[:, X1I + k * 128:X1I + (k + 1) * 128],
                  x1tb[:, k * 128:(k + 1) * 128])

        flush_g1()

        p.wait("tensor", ev, ev.n)
        for grp in range(4):
            pb = 0 if grp % 2 == 0 else 3072
            if h1_aa[grp % 2]:
                p.wait("tensor", act, h1_aa[grp % 2])
            for k in range(8):
                s = cx._wli = (getattr(cx, "_wli", -1) + 1) % 4
                gd = gds[s]
                p.wait("gpsimd", mm, wring_guard[s])
                wv = gd.step(p.g.dma_start(
                    out=wbv[:, WSL[s]:WSL[s] + 1024],
                    in_=dr["w1i_hbm"][k * 128:(k + 1) * 128,
                                      grp * 1024:(grp + 1) * 1024]), 16)
                p.wait("tensor", gd, wv)
                for half in range(2):
                    tok = p.t.matmul(
                        ps[:, pb + half * 512:pb + (half + 1) * 512],
                        x1tb[:, k * 128:(k + 1) * 128],
                        wbv[:, WSL[s] + half * 512:WSL[s] + (half + 1) * 512],
                        start=(k == 0), stop=(k == 7))
                wring_guard[s] = mm.step(tok)
            mv = mm.n
            p.wait("scalar", mm, mv)
            p.sd()
            hg = bb[:, H + grp * 1024:H + (grp + 1) * 1024]
            h1_aa[grp % 2] = act.step(p.s.activation(
                hg, ps[:, pb:pb + 1024], AF.Relu, scale=stats[:, SC:SC + 1]))
            p.sd()
            sqa = act.step(p.s.activation(hg, hg, AF.Square))
            p.wait("vector", act, sqa)
            afv = ev.step(p.v.tensor_scalar(hg, hg, stats[:, 89:90],
                                            stats[:, 90:91], ALU.mult, ALU.add))
            p.vd()
            p.wait("scalar", ev, afv)
            p.sd()
            act.step(p.s.activation(bb[:, X2I + grp * 1024:X2I + (grp + 1) * 1024],
                                    hg, AF.Square,
                                    accum_out=stats[:, 105 + grp:106 + grp]))
        p.wait("vector", act, act.n)
        p.v.tensor_reduce(stats[:, R2:R2 + 1], stats[:, 105:109], AXX, ALU.add)
        p.vd()
        vv = ev.step(p.v.tensor_scalar(stats[:, R2:R2 + 1], stats[:, R2:R2 + 1],
                                       1.0 / HB, 1e-8, ALU.mult, ALU.add))
        p.vd()
        _rsqrt(cx, stats[:, R1:R1 + 1], stats[:, R2:R2 + 1],
               stats[:, R3:R3 + 1], (ev, vv))
        p.v.scalar_tensor_tensor(bb[:, H:H + 4096], bb[:, H:H + 4096],
                                 stats[:, R1:R1 + 1], bb[:, N2BC:N2BC + 4096],
                                 ALU.mult, ALU.mult)
        p.vd()
        p.v.reduce_max(stats[:, R2:R2 + 1], bb[:, H:H + 4096], axis=AXX,
                       apply_absolute_value=True)
        p.vd()
        p.v.tensor_scalar_max(stats[:, R2:R2 + 1], stats[:, R2:R2 + 1], 1e-5)
        p.vd()
        _recip(cx, stats[:, R1:R1 + 1], stats[:, R2:R2 + 1])
        p.v.tensor_scalar_mul(stats[:, R1:R1 + 1], stats[:, R1:R1 + 1], 127.0)
        p.vd()
        p.v.tensor_scalar(stats[:, SC:SC + 1], stats[:, R2:R2 + 1], 1.0 / 127.0,
                          None, ALU.mult)
        p.vd()
        p.v.tensor_tensor(stats[:, SC:SC + 1], stats[:, SC:SC + 1],
                          stats[:, 97:98], ALU.mult)
        p.vd()
        p.v._custom_dve(RINT, out=bb[:, X2I:X2I + 4096], in0=bb[:, H:H + 4096],
                        s0=MAGIC, s1=stats[:, R1:R1 + 1])
        xv = ev.step(p.v.engine_nop())
        p.vd()
        p.wait("tensor", ev, xv)
        evk = [0] * 32
        for k in range(4):
            evk[k] = t_one(bb[:, X2I + k * 128:X2I + (k + 1) * 128],
                           x2tb[:, (k % 4) * 128:(k % 4) * 128 + 128])

        if h1_aa[0]:
            p.wait("tensor", act, h1_aa[0])
        for k in range(32):
            s = cx._wli = (cx._wli + 1) % 4
            gd = gds[s]
            p.wait("gpsimd", mm, wring_guard[s])
            wv = gd.step(p.g.dma_start(
                out=wbv[:, WSL[s]:WSL[s] + 1024],
                in_=dr["w2i_hbm"][k * 128:(k + 1) * 128, :]), 16)
            p.wait("tensor", gd, wv)
            p.wait("tensor", ev, evk[k])
            for half in range(2):
                tok = p.t.matmul(
                    ps[:, half * 512:(half + 1) * 512],
                    x2tb[:, (k % 4) * 128:(k % 4) * 128 + 128],
                    wbv[:, WSL[s] + half * 512:WSL[s] + (half + 1) * 512],
                    start=(k == 0), stop=(k == 31))
            wring_guard[s] = mm.step(tok)
            if k + 4 < 32:
                evk[k + 4] = t_one(
                    bb[:, X2I + (k + 4) * 128:X2I + (k + 5) * 128],
                    x2tb[:, ((k + 4) % 4) * 128:((k + 4) % 4) * 128 + 128])
        mv = mm.n
        pend = (c, mv, SC)
    flush_g1()
    p.wait("vector", act, act.n)
    ev.step(p.v.engine_nop())


def _quant_weights(cx, w_dram, rows, cols, wi_hbm, qw_col):
    p, sems, sb, dr, ps = cx.p, cx.sems, cx.sb, cx.dr, cx.ps
    mm, ev, act = sems["mm"], sems["ev"], sems["act"]
    qs = (sems["qa"], sems["qb"])
    sts = (sems["sta"], sems["stb"])
    bb, stats = sb["BB"], sb["stats"]
    nk = rows // 128
    p.wait("sync", mm, mm.n)
    p.wait("sync", ev, ev.n)
    p.wait("sync", act, act.n)
    p.v.memset(stats[:, 98:99], 0.0)
    p.vd()
    lv = [0, 0]
    red_ev = [0, 0]
    for k in range(nk + 1):
        s = k % 2
        if k < nk:
            if red_ev[s]:
                p.wait("sync", ev, red_ev[s])
            lv[s] = qs[s].step(p.sy.dma_start(
                out=bb[:, s * cols:(s + 1) * cols],
                in_=w_dram[k * 128:(k + 1) * 128, :]), 16)
        if k > 0:
            sp = (k - 1) % 2
            p.wait("vector", qs[sp], lv[sp])
            p.v.tensor_reduce(stats[:, 100:101], bb[:, sp * cols:(sp + 1) * cols],
                              AXX, ALU.add, apply_absolute_value=True)
            p.vd()
            p.v.tensor_add(stats[:, 98:99], stats[:, 98:99], stats[:, 100:101])
            red_ev[sp] = ev.step(p.v.engine_nop())
            p.vd()
    p.wait("tensor", ev, ev.n)
    mv = mm.step(p.t.matmul(ps[0:1, 1024:1025], stats[:, 98:99],
                            C(cx, "ones_col"), start=True, stop=True))
    p.wait("vector", mm, mv)
    p.v.tensor_scalar(stats[0:1, 100:101], ps[0:1, 1024:1025],
                      1.0 / (rows * cols), 1e-5, ALU.mult, ALU.max)
    p.vd()
    sv = ev.step(p.v.engine_nop())
    p.wait("tensor", ev, sv)
    mv = mm.step(p.t.matmul(ps[:, 1024:1025], ONES1(cx, 128), stats[0:1, 100:101],
                            start=True, stop=True))
    p.wait("vector", mm, mv)
    p.v.tensor_copy(qw_col, ps[:, 1024:1025])
    p.vd()
    _recip(cx, stats[:, 99:100], qw_col)
    ev.step(p.v.engine_nop())
    p.vd()
    lv = [0, 0]
    used = list(red_ev)
    stv = [0, 0]
    for k in range(nk + 1):
        s = k % 2
        if k < nk:
            if used[s]:
                p.wait("sync", ev, used[s])
            lv[s] = qs[s].step(p.sy.dma_start(
                out=bb[:, s * cols:(s + 1) * cols],
                in_=w_dram[k * 128:(k + 1) * 128, :]), 16)
        if k > 0:
            kk = k - 1
            sp = kk % 2
            p.wait("vector", qs[sp], lv[sp])
            if stv[sp]:
                p.wait("vector", sts[sp], stv[sp])
            p.v._custom_dve(RINT, out=bb[:, (2 + sp) * cols:(3 + sp) * cols],
                            in0=bb[:, sp * cols:(sp + 1) * cols],
                            s0=MAGIC, s1=stats[:, 99:100])
            used[sp] = ev.step(p.v.engine_nop())
            p.vd()
            p.v.tensor_scalar(bb[:, (2 + sp) * cols:(3 + sp) * cols],
                              bb[:, (2 + sp) * cols:(3 + sp) * cols],
                              -1.0, 1.0, ALU.max, ALU.min)
            qv = ev.step(p.v.engine_nop())
            p.vd()
            p.wait("gpsimd", ev, qv)
            stv[sp] = sts[sp].step(p.g.dma_start(
                out=wi_hbm[kk * 128:(kk + 1) * 128, :],
                in_=bb[:, (2 + sp) * cols:(3 + sp) * cols]), 16)
    for e in ("sync", "vector", "gpsimd", "tensor"):
        p.wait(e, sems["sta"], sems["sta"].n)
        p.wait(e, sems["stb"], sems["stb"].n)


AM, APR, ATMP, ARG, AQ, AV, AKT, AKF, ASB = (
    0, 8192, 8336, 9360, 11408, 12432, 13456, 14480, 15504)


def _stage_l1_attn(cx):
    p, sems, sb, dr, ps = cx.p, cx.sems, cx.sb, cx.dr, cx.ps
    ld, wl2, mm, ev, act = (sems["ld"], sems["wl2"], sems["mm"],
                            sems["ev"], sems["act"])
    bb, bx, ba, stats, keep = sb["BB"], sb["BX"], sb["BA"], sb["stats"], sb["keep"]

    P0 = APR
    WL = APR + 16
    p.v.memset(bb[:, AM:AM + 8192], 0.0)
    p.vd()
    mzv = ev.step(p.v.engine_nop())

    aa = act.step(p.s.activation(stats[0:1, 82:83], C(cx, "decay", 1), AF.Sigmoid))
    p.sd()
    p.wait("scalar", act, aa)
    aa = act.step(p.s.activation(stats[0:1, 82:83], stats[0:1, 82:83], AF.Ln))
    p.sd()
    p.wait("tensor", act, aa)
    mv = mm.step(p.t.matmul(ps[:, 1024:1025], ONES1(cx, 128), stats[0:1, 82:83],
                            start=True, stop=True))
    mv = mm.step(p.t.matmul(ps[:, 1025:1026], ONES1(cx, 128), C(cx, "oscale", 1),
                            start=True, stop=True))
    mv = mm.step(p.t.matmul(ps[:, 1026:1027], ONES1(cx, 128), C(cx, "gate_attn", 1),
                            start=True, stop=True))
    p.wait("vector", mm, mv)
    ev.step(p.v.tensor_copy(bb[:, P0 + 5:P0 + 6], ps[:, 1024:1025]))
    ev.step(p.v.tensor_copy(bb[:, P0 + 3:P0 + 4], ps[:, 1025:1026]))
    ev.step(p.v.tensor_copy(bb[:, P0 + 4:P0 + 5], ps[:, 1026:1027]))
    p.vd()
    p.v.tensor_tensor(bb[:, P0 + 0:P0 + 1], C(cx, "iota"), bb[:, P0 + 5:P0 + 6],
                      ALU.mult)
    p.vd()
    p.v.tensor_scalar(bb[:, P0 + 1:P0 + 2], C(cx, "iota"), 127.0, -1.0,
                      ALU.subtract, ALU.mult)
    p.vd()
    p.v.tensor_tensor(bb[:, P0 + 1:P0 + 2], bb[:, P0 + 1:P0 + 2],
                      bb[:, P0 + 5:P0 + 6], ALU.mult)
    p.vd()
    p.v.tensor_scalar_mul(bb[:, P0 + 2:P0 + 3], bb[:, P0 + 5:P0 + 6], 128.0)
    vv = ev.step(p.v.engine_nop())
    p.vd()
    p.wait("scalar", ev, vv)
    p.sd()
    act.step(p.s.activation(bb[:, P0 + 0:P0 + 1], bb[:, P0 + 0:P0 + 1], AF.Exp))
    p.sd()
    act.step(p.s.activation(bb[:, P0 + 1:P0 + 2], bb[:, P0 + 1:P0 + 2], AF.Exp))
    p.sd()
    aa = act.step(p.s.activation(bb[:, P0 + 2:P0 + 3], bb[:, P0 + 2:P0 + 3], AF.Exp))
    p.sd()
    p.wait("vector", act, aa)
    p.v.tensor_scalar(bb[:, WL:WL + 128], C(cx, "ident"), 0.0, C(cx, "iota"),
                      ALU.mult, ALU.add)
    tv = ev.step(p.v.engine_nop())
    p.vd()
    p.wait("tensor", ev, tv)
    mv = mm.step(p.t.transpose(ps[:, 1280:1408], bb[:, WL:WL + 128],
                               C(cx, "ident")))
    p.wait("vector", mm, mv)
    p.v.tensor_scalar(bb[:, WL:WL + 128], ps[:, 1280:1408], C(cx, "iota"), None,
                      ALU.subtract)
    p.vd()
    p.v.tensor_scalar(bb[:, WL:WL + 128], bb[:, WL:WL + 128], 1.0, None, ALU.add)
    p.vd()
    p.v.tensor_scalar(bb[:, WL:WL + 128], bb[:, WL:WL + 128], -1.0,
                      bb[:, P0 + 5:P0 + 6], ALU.mult, ALU.mult)
    p.vd()
    p.v.tensor_scalar_min(bb[:, WL:WL + 128], bb[:, WL:WL + 128], 0.0)
    wv = ev.step(p.v.engine_nop())
    p.vd()
    p.wait("scalar", ev, wv)
    p.sd()
    aa = act.step(p.s.activation(bb[:, WL:WL + 128], bb[:, WL:WL + 128], AF.Exp))
    p.wait("vector", act, aa)
    wlv = ev.step(p.v.tensor_tensor(bb[:, WL:WL + 128], bb[:, WL:WL + 128],
                                    C(cx, "tri_strict"), ALU.mult))
    p.vd()

    p.barrier()

    ring_guard = [0, 0]
    for ci in range(16):
        c = 15 - ci
        defer = (c >= 12)
        _l1_chunk(cx, c, defer, ring_guard)

    gp, prep, lsem, rsem = sems["gp"], sems["prep"], sems["lsem"], sems["rsem"]
    rsem2 = sems["rsem2"]
    keep = sb["keep"]
    p.wait("gpsimd", ev, ev.n)
    p.wait("gpsimd", mm, mm.n)
    rdests = [None] * 8
    rdests[1] = (0, 1)
    pv = prep.step(p.g.remote_dma_broadcast(
        keep[:, 60:64], keep[:, 56:60], rsem2.h, lsem.h, rdests=rdests))
    p.wait("gpsimd", prep, pv)
    p.g.trigger_dma(1)
    lsem.n += 16
    rsem2.n += 2
    p.wait("gpsimd", lsem, lsem.n)
    p.wait("gpsimd", rsem2, rsem2.n)
    for i in range(8):
        pv = prep.step(p.g.remote_dma_broadcast(
            ba[:, i * 1024:(i + 1) * 1024], bb[:, AM + i * 1024:AM + (i + 1) * 1024],
            rsem.h, lsem.h, rdests=rdests))
    p.wait("gpsimd", prep, prep.n)
    p.g.trigger_dma(8)
    lsem.n += 8 * 16
    rsem.n += 8 * 2
    p.wait("gpsimd", lsem, lsem.n)
    p.wait("gpsimd", rsem, rsem.n)
    gv = gp.step(p.g.engine_nop())
    p.wait("vector", gp, gv)
    p.wait("tensor", gp, gv)
    p.wait("sync", gp, gv)
    p.v.tensor_copy(ba[:, 0:8192].bitcast(mybir.dt.int32),
                    ba[:, 0:8192].bitcast(mybir.dt.int32))
    p.vd()
    ev.step(p.v.engine_nop())
    p.wait("tensor", sems["ev"], sems["ev"].n)

    for c in range(12, 16):
        _l1_correction(cx, c, ring_guard)


def _l1_chunk(cx, c, defer, ring_guard):
    p, sems, sb, dr, ps = cx.p, cx.sems, cx.sb, cx.dr, cx.ps
    ld, wl2, mm, ev, act = (sems["ld"], sems["wl2"], sems["mm"],
                            sems["ev"], sems["act"])
    bb, bx, ba, stats = sb["BB"], sb["BX"], sb["BA"], sb["stats"]
    P0 = APR

    p.wait("vector", mm, mm.n)
    sv = ev.step(p.v.tensor_scalar(bb[:, ATMP:ATMP + 1024],
                                   bx[:, c * 1024:(c + 1) * 1024],
                                   stats[:, 32 + c:32 + c + 1],
                                   stats[:, 48 + c:48 + c + 1],
                                   ALU.subtract, ALU.mult))
    p.vd()
    _inplace_t128(cx, ATMP, 8, (ev, sv))
    xnf_ready = ev.n

    p.wait("tensor", ev, xnf_ready)
    osems = (sems["oa"], sems["ob"])
    pbank = [0, 0]
    for which, wsrc, dstc, scale, pp in (("k", "wk_t", AKT, None, 0),
                                         ("v", "wv_t", AV, None, 1024),
                                         ("q", "wq_t", AQ, 1.0 / 32.0, 0)):
        if pbank[pp // 1024]:
            p.wait("tensor", ev, pbank[pp // 1024])
        for k in range(8):
            slot = k % 2
            p.wait("sync", mm, ring_guard[slot])
            wv = osems[slot].step(p.sy.dma_start(
                out=bb[:, ARG + slot * 1024:ARG + (slot + 1) * 1024],
                in_=dr[wsrc][k * 128:(k + 1) * 128, :]), 16)
            p.wait("tensor", osems[slot], wv)
            for half in range(2):
                tok = p.t.matmul(ps[:, pp + half * 512:pp + (half + 1) * 512],
                                 bb[:, ATMP + k * 128:ATMP + (k + 1) * 128],
                                 bb[:, ARG + slot * 1024 + half * 512:
                                     ARG + slot * 1024 + (half + 1) * 512],
                                 start=(k == 0), stop=(k == 7))
            ring_guard[slot] = mm.step(tok)
        mv = mm.n
        p.wait("vector", mm, mv)
        if scale is None:
            vv = ev.step(p.v.tensor_copy(bb[:, dstc:dstc + 1024],
                                         ps[:, pp:pp + 1024]))
        else:
            p.wait("vector", sems["store"], sems["store"].n)
            vv = ev.step(p.v.tensor_scalar_mul(bb[:, dstc:dstc + 1024],
                                               ps[:, pp:pp + 1024], scale))
        p.vd()
        pbank[pp // 1024] = vv
    _oop_t128(cx, AKT, AKF, 8)
    _inplace_t128(cx, AQ, 8, (ev, ev.n))
    kq_ready = ev.n
    if defer:
        p.wait("gpsimd", ev, kq_ready)
        sems["store"].step(p.g.dma_start(
            out=dr["q_defer"][(c - 12) * 128:(c - 11) * 128, :],
            in_=bb[:, AQ:AQ + 1024]), 16)

    p.wait("tensor", ev, kq_ready)
    for kk in range(8):
        tok = p.t.matmul(ps[:, 1024:1152],
                         bb[:, AKF + kk * 128:AKF + (kk + 1) * 128],
                         bb[:, AQ + kk * 128:AQ + (kk + 1) * 128],
                         start=(kk == 0), stop=(kk == 7))
    msv = mm.step(tok)
    p.wait("vector", mm, msv)
    ssv = ev.step(p.v.tensor_tensor(bb[:, ASB:ASB + 128], ps[:, 1024:1152],
                                    bb[:, APR + 16:APR + 16 + 128], ALU.mult))
    p.vd()

    p.wait("tensor", ev, ssv)
    for half in range(2):
        tok = p.t.matmul(ps[:, 2560 + half * 512:2560 + (half + 1) * 512],
                         bb[:, ASB:ASB + 128],
                         bb[:, AV + half * 512:AV + (half + 1) * 512],
                         start=True, stop=True)
    myi = mm.step(tok)
    for kk in range(8):
        for half in range(2):
            tok = p.t.matmul(ps[:, 1536 + half * 512:1536 + (half + 1) * 512],
                             bb[:, AQ + kk * 128:AQ + (kk + 1) * 128],
                             bb[:, AM + kk * 1024 + half * 512:
                                 AM + kk * 1024 + (half + 1) * 512],
                             start=(kk == 0), stop=(kk == 7))
    my2 = mm.step(tok)
    p.wait("vector", mm, my2)
    p.v.tensor_scalar(bb[:, ATMP:ATMP + 1024], ps[:, 1536:2560],
                      bb[:, P0 + 1:P0 + 2], None, ALU.mult)
    p.vd()
    rv = ev.step(p.v.tensor_tensor(bb[:, ATMP:ATMP + 1024], bb[:, ATMP:ATMP + 1024],
                                   ps[:, 2560:3584], ALU.add))
    p.vd()

    p.v.tensor_scalar(bb[:, AKT:AKT + 1024], bb[:, AKT:AKT + 1024],
                      bb[:, P0 + 0:P0 + 1], None, ALU.mult)
    klv = ev.step(p.v.engine_nop())
    p.vd()
    p.wait("tensor", ev, klv)
    p.wait("tensor", ev, rv)
    ug = [rv, rv]
    for kk in range(8):
        pb = 2560 if kk % 2 == 0 else 1536
        p.wait("tensor", ev, ug[kk % 2])
        for half in range(2):
            tok = p.t.matmul(ps[:, pb + half * 512:pb + (half + 1) * 512],
                             bb[:, AKT + kk * 128:AKT + (kk + 1) * 128],
                             bb[:, AV + half * 512:AV + (half + 1) * 512],
                             start=True, stop=True)
        muv = mm.step(tok)
        p.wait("vector", mm, muv)
        ug[kk % 2] = ev.step(p.v.scalar_tensor_tensor(
            bb[:, AM + kk * 1024:AM + (kk + 1) * 1024],
            bb[:, AM + kk * 1024:AM + (kk + 1) * 1024],
            bb[:, P0 + 2:P0 + 3], ps[:, pb:pb + 1024], ALU.mult, ALU.add))
    p.vd()

    _inplace_t128(cx, ATMP, 8, (ev, ev.n))
    rf_ready = ev.n
    p.wait("tensor", ev, rf_ready)
    for k in range(8):
        slot = k % 2
        p.wait("sync", mm, ring_guard[slot])
        wv = osems[slot].step(p.sy.dma_start(
            out=bb[:, ARG + slot * 1024:ARG + (slot + 1) * 1024],
            in_=dr["wo_t"][k * 128:(k + 1) * 128, :]), 16)
        p.wait("tensor", osems[slot], wv)
        for half in range(2):
            tok = p.t.matmul(ps[:, half * 512:(half + 1) * 512],
                             bb[:, ATMP + k * 128:ATMP + (k + 1) * 128],
                             bb[:, ARG + slot * 1024 + half * 512:
                                 ARG + slot * 1024 + (half + 1) * 512],
                             start=(k == 0), stop=(k == 7))
        ring_guard[slot] = mm.step(tok)
    mv = mm.n
    p.wait("vector", mm, mv)
    if defer:
        vv = ev.step(p.v.tensor_scalar(bb[:, ATMP:ATMP + 1024], ps[:, 0:1024],
                                       bb[:, P0 + 3:P0 + 4], None, ALU.mult))
        p.vd()
        p.wait("gpsimd", ev, vv)
        p.wait("gpsimd", sems["store"], sems["store"].n)
        sems["store"].step(p.g.dma_start(
            out=dr["s1_defer"][(c - 12) * 128:(c - 11) * 128, :],
            in_=bb[:, ATMP:ATMP + 1024]), 16)
        p.wait("vector", sems["store"], sems["store"].n)
    else:
        p.v.scalar_tensor_tensor(bb[:, ATMP:ATMP + 1024], ps[:, 0:1024],
                                 bb[:, P0 + 3:P0 + 4],
                                 ba[:, c * 1024:(c + 1) * 1024],
                                 ALU.mult, ALU.mult)
        p.vd()
        fv = ev.step(p.v.tensor_tensor(bx[:, c * 1024:(c + 1) * 1024],
                                       bx[:, c * 1024:(c + 1) * 1024],
                                       bb[:, ATMP:ATMP + 1024], ALU.add))
        p.vd()
        p.wait("tensor", ev, fv)


def _l1_correction(cx, c, ring_guard):
    p, sems, sb, dr, ps = cx.p, cx.sems, cx.sb, cx.dr, cx.ps
    wl2, mm, ev, act = sems["wl2"], sems["mm"], sems["ev"], sems["act"]
    bb, bx, ba, stats = sb["BB"], sb["BX"], sb["BA"], sb["stats"]
    P0 = APR

    p.v.tensor_scalar(bb[:, P0 + 6:P0 + 7], C(cx, "iota"),
                      float(2047 - c * 128), -1.0, ALU.subtract, ALU.mult)
    p.vd()
    p.v.tensor_tensor(bb[:, P0 + 6:P0 + 7], bb[:, P0 + 6:P0 + 7],
                      bb[:, P0 + 5:P0 + 6], ALU.mult)
    cv = ev.step(p.v.engine_nop())
    p.vd()
    p.wait("scalar", ev, cv)
    p.sd()
    aa = act.step(p.s.activation(bb[:, P0 + 6:P0 + 7], bb[:, P0 + 6:P0 + 7],
                                 AF.Exp))
    p.wait("vector", act, aa)
    p.v.tensor_tensor(bb[:, P0 + 6:P0 + 7], bb[:, P0 + 6:P0 + 7],
                      bb[:, P0 + 3:P0 + 4], ALU.mult)
    p.vd()
    p.v.tensor_tensor(bb[:, P0 + 6:P0 + 7], bb[:, P0 + 6:P0 + 7],
                      bb[:, P0 + 4:P0 + 5], ALU.mult)
    p.vd()
    osems = (sems["oa"], sems["ob"])
    p.wait("sync", ev, ev.n)
    lq = sems["ld"].step(p.sy.dma_start(
        out=bb[:, AQ:AQ + 1024],
        in_=dr["q_defer"][(c - 12) * 128:(c - 11) * 128, :]), 16)
    p.wait("tensor", sems["ld"], lq)
    p.wait("tensor", ev, ev.n)
    for kk in range(8):
        for half in range(2):
            tok = p.t.matmul(ps[:, 1536 + half * 512:1536 + (half + 1) * 512],
                             bb[:, AQ + kk * 128:AQ + (kk + 1) * 128],
                             ba[:, kk * 1024 + half * 512:
                                 kk * 1024 + (half + 1) * 512],
                             start=(kk == 0), stop=(kk == 7))
    my = mm.step(tok)
    p.wait("vector", mm, my)
    rv = ev.step(p.v.tensor_scalar(bb[:, ATMP:ATMP + 1024], ps[:, 1536:2560],
                                   bb[:, P0 + 6:P0 + 7], None, ALU.mult))
    p.vd()
    _inplace_t128(cx, ATMP, 8, (ev, rv))
    p.wait("tensor", ev, ev.n)
    for k in range(8):
        slot = k % 2
        p.wait("sync", mm, ring_guard[slot])
        wv = osems[slot].step(p.sy.dma_start(
            out=bb[:, ARG + slot * 1024:ARG + (slot + 1) * 1024],
            in_=dr["wo_t"][k * 128:(k + 1) * 128, :]), 16)
        p.wait("tensor", osems[slot], wv)
        for half in range(2):
            tok = p.t.matmul(ps[:, half * 512:(half + 1) * 512],
                             bb[:, ATMP + k * 128:ATMP + (k + 1) * 128],
                             bb[:, ARG + slot * 1024 + half * 512:
                                 ARG + slot * 1024 + (half + 1) * 512],
                             start=(k == 0), stop=(k == 7))
        ring_guard[slot] = mm.step(tok)
    mv = mm.n
    lv = sems["ld"].step(p.sy.dma_start(
        out=bb[:, AQ:AQ + 1024],
        in_=dr["s1_defer"][(c - 12) * 128:(c - 11) * 128, :]), 16)
    p.wait("vector", sems["ld"], lv)
    p.wait("vector", mm, mv)
    p.v.tensor_tensor(bb[:, ATMP:ATMP + 1024], ps[:, 0:1024],
                      bb[:, AQ:AQ + 1024], ALU.add)
    p.vd()
    p.v.tensor_tensor(bb[:, ATMP:ATMP + 1024], bb[:, ATMP:ATMP + 1024],
                      ba[:, c * 1024:(c + 1) * 1024], ALU.mult)
    p.vd()
    fv = ev.step(p.v.tensor_tensor(bx[:, c * 1024:(c + 1) * 1024],
                                   bx[:, c * 1024:(c + 1) * 1024],
                                   bb[:, ATMP:ATMP + 1024], ALU.add))
    p.vd()
    p.wait("tensor", ev, fv)


def _t128_rot(cx, srcf, dstf, nblk, wait):
    p, sems, ps = cx.p, cx.sems, cx.ps
    mm, ev = sems["mm"], sems["ev"]
    ident = C(cx, "ident")
    if not hasattr(cx, "_it_ev"):
        cx._it_ev = [0, 0, 0, 0]
    p.wait("tensor", *wait)
    for k in range(nblk):
        b = k % 4
        if cx._it_ev[b]:
            p.wait("tensor", ev, cx._it_ev[b])
        pc = 2048 + b * 512
        me = mm.step(p.t.transpose(ps[:, pc:pc + 128], srcf(k), ident))
        p.wait("vector", mm, me)
        cx._it_ev[b] = ev.step(p.v.tensor_copy(dstf(k), ps[:, pc:pc + 128]))
    p.vd()


def _inplace_t128(cx, base, nblk, wait):
    bb = cx.sb["BB"]
    _t128_rot(cx, lambda k: bb[:, base + k * 128:base + (k + 1) * 128],
              lambda k: bb[:, base + k * 128:base + (k + 1) * 128], nblk, wait)


def _oop_t128(cx, src, dst, nblk):
    bb = cx.sb["BB"]
    _t128_rot(cx, lambda k: bb[:, src + k * 128:src + (k + 1) * 128],
              lambda k: bb[:, dst + k * 128:dst + (k + 1) * 128], nblk,
              (cx.sems["ev"], cx.sems["ev"].n))


def kernel(**inputs):
    global _TAP_RESULTS
    per_core = _host_prep(inputs)
    nc = build_program(tap=_TAP)
    res = run_bass_kernel_spmd(nc, per_core, core_ids=list(range(8)))
    _TAP_RESULTS = res.results
    out = np.empty((4, 4096, 1024), dtype=np.float32)
    for core in range(8):
        b, h = core // 2, core % 2
        out[b, h * NT:(h + 1) * NT] = res.results[core]["out"]
    return out


# version 20
# speedup vs baseline: 1.1470x; 1.1470x over previous
import numpy as np
import concourse.bass as bass
import concourse.mybir as mybir
from concourse.bass_utils import run_bass_kernel_spmd
from concourse import library_config

F32 = mybir.dt.float32
BF16 = mybir.dt.bfloat16
AF = mybir.ActivationFunctionType
ALU = mybir.AluOpType
AXX = mybir.AxisListType.X

MAGIC = 12582912.0

NT = 2048
D = 1024
DIN = 2048
NH = 8
HD = 256
NS = 64
DIP = 4232
NE = 8
HE = 256
RA = 64
HB = 4096

_TAP = None
_TAP_RESULTS = None


def _make_rint_op():
    import concourse.dve_ops as dve_ops
    from concourse.dve_op import DveOp
    from concourse.dve_spec import Spec, Src0, C0, C1, lower
    from concourse.dve_table_gen import dve_ver_for

    name = "RINT_SCALE_ANT"
    for op in dve_ops.OPS:
        if op.name == name:
            return op
    spec = Spec(body=(Src0 * C1 + C0) - C0)
    ver = dve_ver_for("TRN2")
    sha = lower(spec, ver=ver, perf_en=False, name=name).sha(ver)
    op = DveOp(
        name, spec, subdim=False, uops_sha={ver: sha},
        reference=lambda in0, in1, c0, c1, c2: np.rint(in0 * c1),
    )
    dve_ops.OPS.append(op)
    dve_ops.CUSTOM_DVE_SPECS[name] = spec
    dve_ops.CUSTOM_DVE_REFS[name] = op.ref
    dve_ops._SUB_OPCODE_FOR_NAME[name] = (
        dve_ops._CUSTOM_DVE_ROW_BASE + len(dve_ops.OPS) - 1)
    return op


RINT = _make_rint_op()


class _CallRec:
    __slots__ = ("name", "args", "kw", "incs")

    def __init__(self, name, args, kw):
        self.name = name
        self.args = args
        self.kw = kw
        self.incs = []

    def then_inc(self, sem, k):
        self.incs.append((sem, k))
        return self

    def replay(self, engine):
        inst = getattr(engine, self.name)(*self.args, **self.kw)
        for s, k in self.incs:
            inst = inst.then_inc(s, k)


class _EngProxy:
    def __init__(self, prog, eng):
        self._prog = prog
        self._eng = eng

    def __getattr__(self, name):
        def call(*args, **kw):
            rec = _CallRec(name, args, kw)
            self._prog.ops[self._eng].append(rec)
            return rec
        return call


class Sem:
    __slots__ = ("h", "n")

    def __init__(self, handle):
        self.h = handle
        self.n = 0

    def step(self, tok, k=1):
        self.n += k
        tok.then_inc(self.h, k)
        return self.n


class Prog:
    ENGINES = ("sync", "tensor", "vector", "scalar", "gpsimd")

    def __init__(self, nc, sems):
        self.nc = nc
        self.ops = {e: [] for e in self.ENGINES}
        self.sems = sems
        self.sy = _EngProxy(self, "sync")
        self.t = _EngProxy(self, "tensor")
        self.v = _EngProxy(self, "vector")
        self.s = _EngProxy(self, "scalar")
        self.g = _EngProxy(self, "gpsimd")

    def vd(self):
        self.ops["vector"].append(_CallRec("drain", (), {}))

    def sd(self):
        self.ops["scalar"].append(_CallRec("drain", (), {}))

    def wait(self, eng, sem, val):
        if val > 0:
            getattr(self, {"sync": "sy", "tensor": "t", "vector": "v",
                           "scalar": "s", "gpsimd": "g"}[eng]).wait_ge(sem.h, val)

    def barrier(self, engines=None):
        bar = self.sems["bar"]
        engines = engines or self.ENGINES
        bar.n += len(engines)
        tgt = bar.n
        for e in engines:
            if e in ("vector", "scalar", "tensor", "gpsimd"):
                self.ops[e].append(_CallRec("drain", (), {}))
            self.ops[e].append(_CallRec("sem_inc", (bar.h, 1), {}))
            self.ops[e].append(_CallRec("wait_ge", (bar.h, tgt), {}))


def _host_prep(inputs):
    f32 = np.float32
    ac = np.ascontiguousarray
    w = {}
    w["wip_t"] = ac(inputs["L0_attn_in_proj_w"].T)
    w["conv_w"] = ac(inputs["L0_attn_conv_w"])
    w["conv_b"] = ac(np.asarray(inputs["L0_attn_conv_b"]).reshape(DIN, 1))
    w["a_log"] = ac(np.asarray(inputs["L0_attn_A_log"]).reshape(NH, 1))
    w["d_par"] = ac(np.asarray(inputs["L0_attn_D_param"]).reshape(1, NH))
    w["dt_bias"] = ac(np.asarray(inputs["L0_attn_dt_bias"]).reshape(NH, 1))
    w["mnorm_w"] = ac(np.asarray(inputs["L0_attn_norm_w"]).reshape(1, DIN))
    w["wop_t"] = ac(inputs["L0_attn_out_proj_w"].T)
    w["w_base"] = ac(inputs["L0_ffn_W_base"])
    w["b_base"] = ac(np.asarray(inputs["L0_ffn_b_base"]).reshape(1, D))
    w["wd1"] = ac(np.asarray(inputs["L0_ffn_Wd1"]).reshape(NE * D, HE))
    w["bd1"] = ac(np.asarray(inputs["L0_ffn_bd1"]).reshape(NE * HE, 1))
    w["wd2"] = ac(np.asarray(inputs["L0_ffn_Wd2"]).reshape(NE * HE, D))
    w["bd2"] = ac(np.asarray(inputs["L0_ffn_bd2"]).reshape(NE * D, 1))
    w["wa_down"] = ac(np.asarray(inputs["L0_ffn_Wa_down"]).reshape(NE * D, RA))
    w["ba_down"] = ac(inputs["L0_ffn_ba_down"])
    w["wa_up"] = ac(np.asarray(inputs["L0_ffn_Wa_up"]).reshape(NE * RA, D))
    w["ba_up"] = ac(inputs["L0_ffn_ba_up"])
    w["wq_t"] = ac(inputs["L1_attn_Wq"].T)
    w["wk_t"] = ac(inputs["L1_attn_Wk"].T)
    w["wv_t"] = ac(inputs["L1_attn_Wv"].T)
    w["wo_t"] = ac(inputs["L1_attn_Wo"].T)
    w["decay"] = ac(np.asarray(inputs["L1_attn_decay_logit"], dtype=f32).reshape(1, 1))
    w["oscale"] = ac(np.asarray(inputs["L1_attn_out_scale"], dtype=f32).reshape(1, 1))
    w["norm1_w"] = ac(np.asarray(inputs["L1_ffn_norm1_w"]).reshape(1, D))
    w["w1_t"] = ac(inputs["L1_ffn_w1"].T)
    w["act_s"] = ac(np.asarray(inputs["L1_ffn_act_scale"], dtype=f32).reshape(1, 1))
    w["act_b"] = ac(np.asarray(inputs["L1_ffn_act_bias"], dtype=f32).reshape(1, 1))
    w["norm2_w"] = ac(np.asarray(inputs["L1_ffn_norm2_w"]).reshape(1, HB))
    w["w2_t"] = ac(inputs["L1_ffn_w2"].T)

    c = {}
    c["ident"] = np.eye(128, dtype=f32)
    r = np.arange(128)
    c["tri_incl"] = (r[:, None] <= r[None, :]).astype(f32)
    c["tri_strict"] = (r[:, None] > r[None, :]).astype(f32)
    c["ones_row"] = np.ones((1, 128), dtype=f32)
    c["iota"] = np.arange(128, dtype=f32).reshape(128, 1)
    c["ones_col"] = np.ones((128, 1), dtype=f32)

    x_full = np.asarray(inputs["L0_attn_x"])
    per_core = []
    for core in range(8):
        b, h = core // 2, core % 2
        m = dict(w)
        m.update(c)
        m["x0"] = ac(x_full[b, h * NT:(h + 1) * NT])
        m["x_prev"] = (np.concatenate([np.zeros((128, D), dtype=f32),
                                       x_full[b, 0:NT // 2]]) if h == 0
                       else ac(x_full[b, NT // 2 - 128:NT]))
        m["pgate"] = np.full((1, 1), 0.0 if h == 0 else 1.0, dtype=f32)
        m["gate_attn"] = np.full((1, 1), 1.0 if h == 0 else 0.0, dtype=f32)
        per_core.append(m)
    return per_core


class Ctx:
    pass


def build_program(tap=None):
    nc = bass.Bass("TRN2", target_bir_lowering=False, debug=False)
    cx = Ctx()
    cx.nc = nc
    cx.tap = tap
    cx.extra_outs = []

    dr = {}

    def din(name, shape, dtype=F32):
        dr[name] = nc.dram_tensor(name, list(shape), dtype, kind="ExternalInput").ap()

    din("x0", (NT, D)); din("x_prev", (NT // 2 + 128, D)); din("pgate", (1, 1))
    din("wip_t", (D, DIP)); din("conv_w", (DIN, 4)); din("conv_b", (DIN, 1))
    din("a_log", (NH, 1)); din("d_par", (1, NH)); din("dt_bias", (NH, 1))
    din("mnorm_w", (1, DIN)); din("wop_t", (DIN, D))
    din("w_base", (D, D)); din("b_base", (1, D))
    din("wd1", (NE * D, HE)); din("bd1", (NE * HE, 1))
    din("wd2", (NE * HE, D)); din("bd2", (NE * D, 1))
    din("wa_down", (NE * D, RA)); din("ba_down", (NE, RA))
    din("wa_up", (NE * RA, D)); din("ba_up", (NE, D))
    din("wq_t", (D, D)); din("wk_t", (D, D)); din("wv_t", (D, D)); din("wo_t", (D, D))
    din("decay", (1, 1)); din("oscale", (1, 1))
    din("norm1_w", (1, D)); din("w1_t", (D, HB))
    din("act_s", (1, 1)); din("act_b", (1, 1))
    din("norm2_w", (1, HB)); din("w2_t", (HB, D))
    din("ident", (128, 128)); din("tri_incl", (128, 128)); din("tri_strict", (128, 128))
    din("ones_row", (1, 128)); din("iota", (128, 1)); din("ones_col", (128, 1))
    din("gate_attn", (1, 1))
    dr["out"] = nc.dram_tensor("out", [NT, D], F32, kind="ExternalOutput").ap()
    dr["w1i_hbm"] = nc.dram_tensor("w1i_hbm", [D, HB], BF16, kind="Internal").ap()
    dr["s1_defer"] = nc.dram_tensor("s1_defer", [512, 1024], F32, kind="Internal").ap()
    dr["q_defer"] = nc.dram_tensor("q_defer", [512, 1024], F32, kind="Internal").ap()
    dr["w2i_hbm"] = nc.dram_tensor("w2i_hbm", [HB, D], BF16, kind="Internal").ap()
    cx.dr = dr

    sb = {}
    sb_handles = {}

    def sbuf(name, shape, dtype=F32):
        t = nc.alloc_sbuf_tensor(name, list(shape), dtype)
        sb_handles[name] = t
        sb[name] = t.ap()

    sbuf("BX", (128, 16384))
    sbuf("BA", (128, 16384))
    sbuf("BB", (128, 16384))
    sbuf("consts", (128, 480))
    sbuf("stats", (128, 128))
    sbuf("hstate", (64, 2048))
    sbuf("X1TB", (128, 1024), BF16)
    sbuf("X2TB", (128, 512), BF16)
    sbuf("keep", (128, 64))
    wb4 = nc.alloc_sbuf_tensor_at(
        "WB4", [128, 4096], BF16,
        offset=nc.lookup_mloc(sb_handles["hstate"]).addr)
    sb["WB4"] = wb4.ap()
    cx.sb = sb

    cx.ps = nc.alloc_psum_tensor("PS", [128, 4096], F32).ap()

    import contextlib
    stack = contextlib.ExitStack()
    sems = {}
    for name in ("ld", "wl", "wl2", "mm", "ev", "act", "gp", "store", "bar",
                 "prep", "lsem", "rsem", "rsem2", "lsem2", "rsem3",
                 "gdma", "gdb", "gdc", "gdd",
                 "qa", "qb", "qc", "oa", "ob", "sta", "stb"):
        sems[name] = Sem(stack.enter_context(nc.semaphore(name)))
    cx.sems = sems

    p = Prog(nc, sems)
    cx.p = p

    _build_all(cx)

    with nc.Block() as block:
        @block.sync
        def _(eng):
            for f in p.ops["sync"]:
                f.replay(eng)

        @block.tensor
        def _(eng):
            for f in p.ops["tensor"]:
                f.replay(eng)

        @block.vector
        def _(eng):
            for f in p.ops["vector"]:
                f.replay(eng)

        @block.scalar
        def _(eng):
            for f in p.ops["scalar"]:
                f.replay(eng)

        @block.gpsimd
        def _(eng):
            for f in p.ops["gpsimd"]:
                f.replay(eng)

    stack.close()
    mybir.codegen_inst_isa_subclasses(nc)
    return nc


def _build_all(cx):
    p = cx.p
    p.g.load_library(library_config.remote_dma)
    _load_consts(cx)
    p.v.memset(cx.sb["stats"][:, :], 0.0)
    p.v.memset(cx.sb["keep"][:, :], 0.0)
    p.v.drain()
    p.barrier()
    _stage_ln0(cx)
    p.barrier()
    if cx.tap == "xn0":
        _tap(cx, "t_xn0", cx.sb["BA"][:, 0:16384])
        return
    if cx.tap == "stats":
        _tap(cx, "t_stats", cx.sb["stats"][:, 0:96])
        return
    _stage_prev_state(cx)
    p.barrier()
    if cx.tap == "hstate":
        _tap(cx, "t_hstate", cx.sb["hstate"][:, 0:2048])
        return
    _stage_experts(cx)
    p.barrier()
    if cx.tap == "losses":
        _tap(cx, "t_losses", cx.sb["stats"][:, 64:96])
        return
    _stage_gate_ffn(cx)
    p.barrier()
    if cx.tap == "g0":
        _tap(cx, "t_g0", cx.sb["BA"][:, 0:16384])
        return
    _stage_mamba(cx)
    p.barrier()
    if cx.tap == "x1":
        _tap(cx, "t_x1", cx.sb["BX"][:, 0:16384])
        return
    _stage_l1_ffn(cx)
    p.barrier()
    if cx.tap == "g1":
        _tap(cx, "t_g1", cx.sb["BA"][:, 0:16384])
        return
    _stage_l1_attn(cx)
    p.barrier()
    _store_out(cx)


def _tap(cx, name, src_ap):
    nc, p, sems = cx.nc, cx.p, cx.sems
    rows, cols = src_ap.shape
    dt = nc.dram_tensor(name, [rows, cols], F32, kind="ExternalOutput").ap()
    st = sems["store"]
    v = st.step(p.g.dma_start(out=dt[:, :], in_=src_ap), 16)
    _store_out(cx)


def _store_out(cx):
    p, sems, sb, dr = cx.p, cx.sems, cx.sb, cx.dr
    st = sems["store"]
    bx = sb["BX"]
    for c in range(16):
        st.step(p.g.dma_start(out=dr["out"][c * 128:(c + 1) * 128, :],
                              in_=bx[:, c * 1024:(c + 1) * 1024]), 16)
    p.wait("gpsimd", st, st.n)


CON = {}


def _load_consts(cx):
    p, sems, sb, dr = cx.p, cx.sems, cx.sb, cx.dr
    ld = sems["ld"]
    cons = sb["consts"]
    off = 0

    def load(name, src, rows, cols):
        nonlocal off
        CON[name] = (off, cols)
        ld.step(p.sy.dma_start(out=cons[0:rows, off:off + cols], in_=src), 16)
        off += cols

    load("ident", dr["ident"][:, :], 128, 128)
    load("tri_incl", dr["tri_incl"][:, :], 128, 128)
    load("tri_strict", dr["tri_strict"][:, :], 128, 128)
    load("iota", dr["iota"][:, :], 128, 1)
    CON["ones_row"] = (CON["tri_incl"][0], 128)
    CON["ones_col"] = (CON["tri_incl"][0] + 127, 1)
    load("a_log", dr["a_log"][:, :], NH, 1)
    load("dt_bias", dr["dt_bias"][:, :], NH, 1)
    load("decay", dr["decay"][:, :], 1, 1)
    load("oscale", dr["oscale"][:, :], 1, 1)
    load("act_s", dr["act_s"][:, :], 1, 1)
    load("act_b", dr["act_b"][:, :], 1, 1)
    load("gate_attn", dr["gate_attn"][:, :], 1, 1)
    load("d_par", dr["d_par"][:, :], 1, NH)
    CON["conv_b"] = (off, 16)
    for c in range(16):
        ld.step(p.sy.dma_start(out=cons[0:128, off + c:off + c + 1],
                               in_=dr["conv_b"][c * 128:(c + 1) * 128, :]), 16)
    off += 16
    CON["conv_w"] = (off, 64)
    for c in range(16):
        ld.step(p.sy.dma_start(out=cons[0:128, off + c * 4:off + c * 4 + 4],
                               in_=dr["conv_w"][c * 128:(c + 1) * 128, :]), 16)
    off += 64
    assert off <= 480


def C(cx, name, rows=128, cols=None):
    off, w = CON[name]
    return cx.sb["consts"][0:rows, off:off + (cols or w)]


def _stage_ln0(cx):
    p, sems, sb, dr = cx.p, cx.sems, cx.sb, cx.dr
    ld = sems["ld"]
    bx = sb["BX"]
    for c in range(16):
        ld.step(p.sy.dma_start(out=bx[:, c * 1024:(c + 1) * 1024],
                               in_=dr["x0"][c * 128:(c + 1) * 128, :]), 16)
    src = [bx[:, c * 1024:(c + 1) * 1024] for c in range(16)]
    _layernorm_16(cx, src, stat_base=0, wait_sem=(ld, ld.n))
    stats = sb["stats"]
    ev, act, mm = sems["ev"], sems["act"], sems["mm"]
    tmp = sb["BB"]
    guard = [0, 0]
    for c in range(16):
        mu = stats[:, c:c + 1]
        rs = stats[:, 16 + c:16 + c + 1]
        t = tmp[:, 0:1024] if c % 2 == 0 else tmp[:, 1024:2048]
        p.wait("vector", mm, guard[c % 2])
        ve = ev.step(p.v.tensor_scalar(t, src[c], mu, rs, ALU.subtract, ALU.mult))
        dst = [sb["BA"][:, (c // 4) * 4096 + k * 512 + (c % 4) * 128:
                        (c // 4) * 4096 + k * 512 + (c % 4) * 128 + 128]
               for k in range(8)]
        guard[c % 2] = _transpose128(cx, t, dst, wait=(ev, ve))


def _layernorm_16(cx, chunks, stat_base, wait_sem, dvar=D):
    p, sems, sb = cx.p, cx.sems, cx.sb
    act, ev = sems["act"], sems["ev"]
    stats = sb["stats"]
    junk = sb["BB"][:, 8192:8192 + dvar]
    n = len(chunks)
    p.wait("scalar", *wait_sem)
    p.wait("vector", *wait_sem)
    for c, ch in enumerate(chunks):
        p.v.reduce_sum(stats[:, 64 + c:64 + c + 1], ch, axis=AXX)
        if c:
            p.s.drain()
        a = act.step(p.s.activation(junk[:, 0:dvar], ch, AF.Square,
                                    accum_out=stats[:, 80 + c:80 + c + 1]))
    p.v.drain()
    p.wait("vector", act, act.n)
    inv = 1.0 / dvar
    mu = stats[:, stat_base:stat_base + n]
    rs = stats[:, stat_base + 16:stat_base + 16 + n]
    sm = stats[:, 64:64 + n]
    sq = stats[:, 80:80 + n]
    p.v.tensor_scalar_mul(mu, sm, inv)
    p.v.tensor_scalar_mul(sq, sq, inv)
    p.v.drain()
    p.v.tensor_tensor(sm, mu, mu, ALU.mult)
    p.v.drain()
    p.v.tensor_sub(sq, sq, sm)
    p.v.drain()
    ve = ev.step(p.v.tensor_scalar_add(sq, sq, 1e-5))
    p.v.drain()
    _rsqrt(cx, rs, sq, stats[:, 64:64 + n], (ev, ve))


def _recip(cx, dst, src):
    p = cx.p
    p.v.reciprocal(dst, src)
    p.v.drain()


def _rsqrt(cx, dst, src, scr, wait):
    p, sems = cx.p, cx.sems
    act, ev = sems["act"], sems["ev"]
    p.wait("scalar", *wait)
    a = act.step(p.s.activation(scr, src, AF.Sqrt))
    p.wait("vector", act, a)
    _recip(cx, dst, scr)
    p.v.tensor_tensor(scr, dst, dst, ALU.mult)
    p.v.drain()
    p.v.tensor_tensor(scr, scr, src, ALU.mult)
    p.v.drain()
    p.v.tensor_scalar(scr, scr, -0.5, 1.5, ALU.mult, ALU.add)
    p.v.drain()
    p.v.tensor_tensor(dst, dst, scr, ALU.mult)
    p.v.drain()
    ev.step(p.v.engine_nop())


def _transpose128(cx, tm_tile, dsts, wait, pbase=2048, out_dtype=None, n_rot=4):
    p, sems, ps = cx.p, cx.sems, cx.ps
    mm, ev = sems["mm"], sems["ev"]
    ident = C(cx, "ident")
    if not hasattr(cx, "_tr_ev"):
        cx._tr_ev = {}
    p.wait("tensor", *wait)
    for k, dst in enumerate(dsts):
        bank = k % n_rot
        key = (pbase, bank)
        g = cx._tr_ev.get(key, 0)
        if g:
            p.wait("tensor", ev, g)
        pcol = pbase + bank * 512
        pdst = ps[:, pcol:pcol + 128]
        if out_dtype is not None:
            pdst = pdst.bitcast(out_dtype)[:, 0:128]
        me = mm.step(p.t.transpose(pdst, tm_tile[:, k * 128:(k + 1) * 128], ident))
        p.wait("vector", mm, me)
        ve = ev.step(p.v.tensor_copy(dst, pdst))
        cx._tr_ev[key] = ve
    return mm.n


XCE, RING, XNF, XST, XH, SZ = 0, 2304, 4352, 5376, 4352, 6400
BCD_B, BCD_C, BCD_D = 8448, 8576, 8704
BTM, XTM, NORMB, OPR, GMS, GT, PBC = 8832, 8896, 9216, 11264, 13312, 13440, 13568
S_DT, S_LA, S_LATM, S_STM, S_ETM, S_E2 = 13696, 13824, 13952, 13960, 13968, 13976
S_SBC, S_SCOL, S_SROW, S_DBC, S_BE = 13984, 13992, 13994, 14010, 14018
S_RSTD, S_RSCR, S_MNW = 14082, 14083, 14200


def _mamba_pass(cx, full):
    p, sems, sb, dr, ps = cx.p, cx.sems, cx.sb, cx.dr, cx.ps
    ld, wl2, mm, ev, act = (sems["ld"], sems["wl2"], sems["mm"],
                            sems["ev"], sems["act"])
    bb, bx, stats, keep, hst = (sb["BB"], sb["BX"], sb["stats"],
                                sb["keep"], sb["hstate"])

    if not full:
        p.v.memset(hst[:, :], 0.0)
        p.vd()
        p.v.memset(keep[:, 0:48], 0.0)
        p.vd()
        p.v.memset(stats[0:64, 112:120], 1.0)
        p.vd()
        ev.step(p.v.engine_nop())

    if full:
        lv = ld.step(p.sy.dma_start(out=bb[0:1, S_MNW:S_MNW + DIN],
                                    in_=dr["mnorm_w"][:, :]), 16)
        p.wait("tensor", ld, lv)
        for q in range(4):
            mv = mm.step(p.t.matmul(ps[:, 0:512], ONES1(cx, 128),
                                    bb[0:1, S_MNW + q * 512:S_MNW + (q + 1) * 512],
                                    start=True, stop=True))
            p.wait("vector", mm, mv)
            vv = ev.step(p.v.tensor_copy(
                bb[:, NORMB + q * 512:NORMB + (q + 1) * 512], ps[:, 0:512]))
            p.vd()
            p.wait("tensor", ev, vv)
        mv = mm.step(p.t.matmul(ps[:, 1024:1032], ONES1(cx, 128),
                                C(cx, "d_par", 1), start=True, stop=True))
        p.wait("vector", mm, mv)
        vv = ev.step(p.v.tensor_copy(keep[:, 48:56], ps[:, 1024:1032]))
        p.vd()
        p.wait("tensor", ev, vv)

    a1 = act.step(p.s.activation(stats[0:8, 93:94], C(cx, "a_log", 8), AF.Exp))
    p.sd()
    p.wait("vector", act, a1)
    p.v.tensor_scalar_mul(stats[0:8, 93:94], stats[0:8, 93:94], -1.0)
    p.vd()
    ev.step(p.v.engine_nop())

    p.barrier()

    for sup in range(16 if full else 9):
        _mamba_super(cx, sup, full)
        if not full and sup == 0:
            p.vd()
            p.v.memset(hst[:, :], 0.0)
            p.vd()
            p.v.memset(stats[0:64, 112:120], 1.0)
            p.vd()
            ev.step(p.v.engine_nop())


def _mamba_super(cx, sup, full):
    p, sems, sb, dr, ps = cx.p, cx.sems, cx.sb, cx.dr, cx.ps
    ld, wl2, mm, ev, act = (sems["ld"], sems["wl2"], sems["mm"],
                            sems["ev"], sems["act"])
    bb, bx, stats, keep, hst = (sb["BB"], sb["BX"], sb["stats"],
                                sb["keep"], sb["hstate"])

    if full and getattr(cx, "_xn_pre", -1) == sup:
        xn_ready = cx._xn_ready
        sv = None
    elif full:
        p.vd()
        p.wait("vector", mm, mm.n)
        sv = ev.step(p.v.tensor_scalar(bb[:, XST:XST + 1024],
                                       bx[:, sup * 1024:(sup + 1) * 1024],
                                       stats[:, sup:sup + 1],
                                       stats[:, 16 + sup:16 + sup + 1],
                                       ALU.subtract, ALU.mult))
        p.vd()
    else:
        p.wait("sync", mm, sems["mm"].n)
        p.wait("sync", ev, ev.n)
        lv = ld.step(p.sy.dma_start(out=bb[:, XST:XST + 1024],
                                    in_=dr["x_prev"][sup * 128:(sup + 1) * 128, :]),
                     16)
        p.wait("vector", ld, lv)
        p.wait("scalar", ld, lv)
        p.vd()
        p.v.reduce_sum(stats[:, 94:95], bb[:, XST:XST + 1024], axis=AXX)
        p.vd()
        p.sd()
        aa = act.step(p.s.activation(bb[:, SZ:SZ + 1024], bb[:, XST:XST + 1024],
                                     AF.Square, accum_out=stats[:, 95:96]))
        p.wait("vector", act, aa)
        inv = 1.0 / D
        p.v.tensor_scalar_mul(stats[:, 94:95], stats[:, 94:95], inv)
        p.v.tensor_scalar_mul(stats[:, 95:96], stats[:, 95:96], inv)
        p.vd()
        p.v.tensor_tensor(stats[:, 92:93], stats[:, 94:95], stats[:, 94:95], ALU.mult)
        p.vd()
        p.v.tensor_sub(stats[:, 95:96], stats[:, 95:96], stats[:, 92:93])
        p.vd()
        vv = ev.step(p.v.tensor_scalar_add(stats[:, 95:96], stats[:, 95:96], 1e-5))
        p.vd()
        _rsqrt(cx, stats[:, 92:93], stats[:, 95:96], stats[:, 91:92], (ev, vv))
        sv = ev.step(p.v.tensor_scalar(bb[:, XST:XST + 1024], bb[:, XST:XST + 1024],
                                       stats[:, 94:95], stats[:, 92:93],
                                       ALU.subtract, ALU.mult))
        p.vd()

    if sv is not None:
        dsts = [bb[:, XNF + k * 128:XNF + (k + 1) * 128] for k in range(8)]
        _transpose128(cx, bb[:, XST:XST + 1024], dsts, (ev, sv),
                      pbase=0, n_rot=4)
        xn_ready = ev.n

    p.wait("tensor", ev, xn_ready)
    p.wait("tensor", ev, ev.n)
    ring_guard = getattr(cx, "_mring", [0, 0, 0, 0])
    cx._mring = ring_guard
    ring_i = getattr(cx, "_mring_i", 0)

    ssems = (wl2, sems["qa"], sems["qb"], sems["qc"])

    def stream_tile(col0, ncols, kchunk):
        nonlocal ring_i
        slot = ring_i % 4
        ring_i += 1
        sem = ssems[slot]
        p.wait("sync", mm, ring_guard[slot])
        wv = sem.step(p.sy.dma_start(
            out=bb[:, RING + slot * 512:RING + slot * 512 + ncols],
            in_=dr["wip_t"][kchunk * 128:(kchunk + 1) * 128, col0:col0 + ncols]), 16)
        return slot, wv, sem

    bank_act = [0, 0]
    if full:
        for g in range(4):
            pb = (g % 2) * 512
            if bank_act[g % 2]:
                p.wait("tensor", act, bank_act[g % 2])
            for k in range(8):
                slot, wv, sem = stream_tile(g * 512, 512, k)
                p.wait("tensor", sem, wv)
                tok = p.t.matmul(ps[:, pb:pb + 512],
                                 bb[:, XNF + k * 128:XNF + (k + 1) * 128],
                                 bb[:, RING + slot * 512:RING + slot * 512 + 512],
                                 start=(k == 0), stop=(k == 7))
                ring_guard[slot] = mm.step(tok)
            mv = mm.n
            p.wait("scalar", mm, mv)
            p.sd()
            bank_act[g % 2] = act.step(p.s.activation(
                bb[:, SZ + g * 512:SZ + (g + 1) * 512],
                ps[:, pb:pb + 512], AF.Silu))
        p.wait("tensor", act, act.n)
    for g in range(4):
        for k in range(8):
            slot, wv, sem = stream_tile(2048 + g * 512, 512, k)
            p.wait("tensor", sem, wv)
            for fs in range(4):
                tok = p.t.matmul(
                    ps[:, fs * 512:fs * 512 + 128],
                    bb[:, RING + slot * 512 + fs * 128:
                        RING + slot * 512 + fs * 128 + 128],
                    bb[:, XNF + k * 128:XNF + (k + 1) * 128],
                    start=(k == 0), stop=(k == 7))
            ring_guard[slot] = mm.step(tok)
        mv = mm.n
        p.wait("vector", mm, mv)
        for fs in range(4):
            f = g * 4 + fs
            ev.step(p.v.tensor_copy(
                bb[:, XCE + f * 144 + 3:XCE + f * 144 + 3 + 128],
                ps[:, fs * 512:fs * 512 + 128]))
        p.vd()
        p.wait("tensor", ev, ev.n)
    for k in range(8):
        slot, wv, sem = stream_tile(4096, 136, k)
        p.wait("tensor", sem, wv)
        st0 = RING + slot * 512
        p.t.matmul(ps[0:8, 1024:1152], bb[:, st0:st0 + 8],
                   bb[:, XNF + k * 128:XNF + (k + 1) * 128],
                   start=(k == 0), stop=(k == 7))
        tok = p.t.matmul(ps[0:64, 1536:1664], bb[:, st0 + 8:st0 + 72],
                         bb[:, XNF + k * 128:XNF + (k + 1) * 128],
                         start=(k == 0), stop=(k == 7))
        if full:
            tok = p.t.matmul(ps[0:64, 2048:2176], bb[:, st0 + 72:st0 + 136],
                             bb[:, XNF + k * 128:XNF + (k + 1) * 128],
                             start=(k == 0), stop=(k == 7))
        ring_guard[slot] = mm.step(tok)
    mv = mm.n
    bcdt_mv = mv
    p.wait("vector", mm, mv)
    ev.step(p.v.tensor_copy(bb[0:8, BCD_B:BCD_B + 128], ps[0:8, 1024:1152]))
    ev.step(p.v.tensor_copy(bb[0:64, BCD_C:BCD_C + 128], ps[0:64, 1536:1664]))
    if full:
        ev.step(p.v.tensor_copy(bb[0:64, BCD_D:BCD_D + 128], ps[0:64, 2048:2176]))
    bcd_ev = ev.n
    p.vd()
    cx._mring_i = ring_i

    p.wait("vector", mm, mv)
    p.v.tensor_copy(
        bb[:, XCE:XCE + 2304].rearrange("p (f c) -> p f c", c=144)[:, :, 0:3],
        keep[:, 0:48].rearrange("p (f c) -> p f c", c=3))
    p.vd()
    p.v.tensor_copy(
        keep[:, 0:48].rearrange("p (f c) -> p f c", c=3),
        bb[:, XCE:XCE + 2304].rearrange("p (f c) -> p f c", c=144)[:, :, 128:131])
    p.vd()
    dst3 = bb[:, XH:XH + 2048].rearrange("p (f j) -> p f j", j=128)
    tmp3 = bb[:, RING:RING + 2048].rearrange("p (f j) -> p f j", j=128)
    xce3 = bb[:, XCE:XCE + 2304].rearrange("p (f c) -> p f c", c=144)
    for tap in range(4):
        src3 = xce3[:, :, tap:tap + 128]
        w3 = (C(cx, "conv_w")[:, tap:tap + 61:4]
              .unsqueeze(2).broadcast_to([128, 16, 128]))
        if tap == 0:
            p.v.tensor_tensor(dst3, src3, w3, ALU.mult)
        else:
            p.v.tensor_tensor(tmp3, src3, w3, ALU.mult)
            p.vd()
            p.v.tensor_tensor(dst3, dst3, tmp3, ALU.add)
        p.vd()
    cv = ev.step(p.v.engine_nop())
    p.wait("scalar", ev, cv)
    p.sd()
    for f in range(16):
        aa = act.step(p.s.activation(bb[:, XH + f * 128:XH + (f + 1) * 128],
                                     bb[:, XH + f * 128:XH + (f + 1) * 128],
                                     AF.Silu, bias=C(cx, "conv_b")[:, f:f + 1]))
    p.sd()
    xh_ready = act.n

    p.wait("scalar", ev, bcd_ev)
    p.sd()
    aa = act.step(p.s.activation(bb[0:8, S_DT:S_DT + 128],
                                 bb[0:8, BCD_B:BCD_B + 128], AF.Exp,
                                 bias=C(cx, "dt_bias", 8)))
    p.wait("vector", act, aa)
    vv_sp = ev.step(p.v.tensor_scalar_add(bb[0:8, S_DT:S_DT + 128],
                                          bb[0:8, S_DT:S_DT + 128], 1.0))
    p.vd()
    p.wait("scalar", ev, vv_sp)
    p.sd()
    aa = act.step(p.s.activation(bb[0:8, S_DT:S_DT + 128],
                                 bb[0:8, S_DT:S_DT + 128], AF.Ln))
    p.wait("vector", act, aa)
    p.v.tensor_scalar(bb[0:8, S_LA:S_LA + 128], bb[0:8, S_DT:S_DT + 128],
                      stats[0:8, 93:94], None, ALU.mult)
    p.vd()
    p.v.reduce_sum(bb[0:8, S_SCOL:S_SCOL + 1], bb[0:8, S_LA:S_LA + 128], axis=AXX)
    p.vd()
    lv = ev.step(p.v.engine_nop())
    p.wait("tensor", ev, lv)
    m1 = mm.step(p.t.transpose(ps[:, 1536:1544], bb[0:8, S_LA:S_LA + 128],
                               C(cx, "ident", 8, 8)))
    m2 = mm.step(p.t.transpose(ps[0:1, 1544:1552], bb[0:8, S_SCOL:S_SCOL + 1],
                               C(cx, "ident", 8, 8)))
    p.wait("vector", mm, m2)
    ev.step(p.v.tensor_copy(bb[:, S_LATM:S_LATM + 8], ps[:, 1536:1544]))
    ev.step(p.v.tensor_copy(bb[0:1, S_SROW:S_SROW + 8], ps[0:1, 1544:1552]))
    lv2 = ev.n
    p.vd()
    p.wait("tensor", ev, lv2)
    m3 = mm.step(p.t.matmul(ps[:, 1792:1800], C(cx, "tri_incl"),
                            bb[:, S_LATM:S_LATM + 8], start=True, stop=True))
    m4 = mm.step(p.t.matmul(ps[:, 3584:3592], ONES1(cx, 128),
                            bb[0:1, S_SROW:S_SROW + 8], start=True, stop=True))
    m5 = mm.step(p.t.matmul(ps[0:64, 1024:1032], ONES1(cx, 64),
                            bb[0:1, S_SROW:S_SROW + 8], start=True, stop=True))
    p.wait("vector", mm, m5)
    ev.step(p.v.tensor_copy(bb[:, S_STM:S_STM + 8], ps[:, 1792:1800]))
    p.vd()
    p.v.tensor_sub(bb[:, S_SBC:S_SBC + 8], ps[:, 3584:3592], bb[:, S_STM:S_STM + 8])
    p.vd()
    bc_ev = ev.step(p.v.tensor_copy(bb[0:64, S_DBC:S_DBC + 8], ps[0:64, 1024:1032]))
    p.vd()
    p.wait("scalar", ev, bc_ev)
    p.sd()
    act.step(p.s.activation(bb[:, S_ETM:S_ETM + 8], bb[:, S_SBC:S_SBC + 8], AF.Exp))
    p.sd()
    act.step(p.s.activation(bb[:, S_E2:S_E2 + 8], bb[:, S_STM:S_STM + 8], AF.Exp))
    p.sd()
    prep_a = act.step(p.s.activation(bb[0:64, S_DBC:S_DBC + 8],
                                     bb[0:64, S_DBC:S_DBC + 8], AF.Exp))
    p.sd()
    p.wait("vector", act, prep_a)
    if not full:
        p.v.tensor_tensor(stats[0:64, 112:120], stats[0:64, 112:120],
                          bb[0:64, S_DBC:S_DBC + 8], ALU.mult)
        p.vd()
    ev.step(p.v.engine_nop())

    p.wait("tensor", ev, ev.n)
    m6 = mm.step(p.t.transpose(ps[:, 1920:1984], bb[0:64, BCD_C:BCD_C + 128],
                               C(cx, "ident", 64, 64)))
    p.wait("vector", mm, m6)
    btm_ev = ev.step(p.v.tensor_copy(bb[:, BTM:BTM + 64], ps[:, 1920:1984]))
    p.vd()

    if full:
        p.wait("tensor", ev, bc_ev)
        m7 = mm.step(p.t.matmul(ps[:, 1024:1152], bb[0:64, BCD_C:BCD_C + 128],
                                bb[0:64, BCD_D:BCD_D + 128], start=True, stop=True))
        p.wait("vector", mm, m7)
        gms_ev = ev.step(p.v.tensor_tensor(bb[:, GMS:GMS + 128], ps[:, 1024:1152],
                                           C(cx, "tri_incl"), ALU.mult))
        p.vd()
    else:
        gms_ev = 0

    PBCa, GTa, BEa, XTM2 = RING, RING + 1024, RING, GMS
    ident = C(cx, "ident")
    p.wait("tensor", act, xh_ready)
    if full:
        p.wait("vector", mm, bcdt_mv)
        for h in range(NH):
            p.v.tensor_scalar(bb[:, PBCa + h * 128:PBCa + (h + 1) * 128],
                              ident, 0.0, bb[:, S_STM + h:S_STM + h + 1],
                              ALU.mult, ALU.add)
        pav = ev.step(p.v.engine_nop())
        p.vd()
        p.wait("tensor", ev, pav)
        bank_ev = [0, 0, 0, 0]
        lastme = 0
        for h in range(NH):
            b = h % 4
            if bank_ev[b]:
                p.wait("tensor", ev, bank_ev[b])
            pc = 1024 + b * 512
            lastme = mm.step(p.t.transpose(
                ps[:, pc:pc + 128], bb[:, PBCa + h * 128:PBCa + (h + 1) * 128],
                ident))
            p.wait("vector", mm, lastme)
            bank_ev[b] = ev.step(p.v.tensor_scalar(
                bb[:, GTa + h * 128:GTa + (h + 1) * 128], ps[:, pc:pc + 128],
                bb[:, S_STM + h:S_STM + h + 1], 0.0, ALU.subtract, ALU.min))
        p.vd()
        dv = ev.n
        p.wait("scalar", ev, dv)
        p.sd()
        for h in range(NH):
            aa = act.step(p.s.activation(bb[:, GTa + h * 128:GTa + (h + 1) * 128],
                                         bb[:, GTa + h * 128:GTa + (h + 1) * 128],
                                         AF.Exp))
        p.wait("vector", act, aa)
        for h in range(NH):
            p.v.tensor_tensor(bb[:, GTa + h * 128:GTa + (h + 1) * 128],
                              bb[:, GTa + h * 128:GTa + (h + 1) * 128],
                              bb[:, GMS:GMS + 128], ALU.mult)
        gt_all = ev.step(p.v.engine_nop())
        p.vd()
    else:
        gt_all = 0
    p.wait("vector", mm, lastme if full else bcdt_mv)
    p.wait("vector", ev, btm_ev)
    for h in range(NH):
        p.v.tensor_scalar(bb[:, BEa + h * 64:BEa + (h + 1) * 64],
                          bb[:, BTM:BTM + 64],
                          bb[:, S_ETM + h:S_ETM + h + 1], None, ALU.mult)
    be_all = ev.step(p.v.engine_nop())
    p.vd()

    xtm_guard = [0, 0]
    xtm_ev = [0, 0]
    xtm_dve = [0, 0]
    yi_guard = 0
    y2_guard = 0
    u_guard = 0
    p.wait("tensor", ev, be_all)
    if full:
        p.wait("tensor", ev, gt_all)
    for h in range(NH):
        sl = h % 2
        xb = XTM if sl == 0 else XTM2
        pb4 = 2048 if sl == 0 else 1536
        p.wait("tensor", act, xtm_ev[sl])
        for half in range(2):
            m8 = mm.step(p.t.transpose(
                ps[:, pb4 + half * 128:pb4 + (half + 1) * 128],
                bb[:, XH + h * 256 + half * 128:XH + h * 256 + (half + 1) * 128],
                ident))
        p.wait("scalar", mm, m8)
        p.wait("scalar", mm, xtm_guard[sl])
        p.wait("scalar", ev, xtm_dve[sl])
        p.sd()
        xtm_ev[sl] = act.step(p.s.activation(bb[:, xb:xb + 256],
                                             ps[:, pb4:pb4 + 256], AF.Copy))
        p.wait("tensor", act, xtm_ev[sl])
        if full:
            p.wait("tensor", ev, yi_guard)
            myi = mm.step(p.t.matmul(ps[:, 2560:2816],
                                     bb[:, GTa + h * 128:GTa + (h + 1) * 128],
                                     bb[:, xb:xb + 256], start=True, stop=True))
            p.wait("tensor", act, y2_guard)
            my2 = mm.step(p.t.matmul(ps[:, 3072:3328], bb[0:64, BCD_D:BCD_D + 128],
                                     hst[:, h * 256:(h + 1) * 256],
                                     start=True, stop=True))
        p.wait("tensor", ev, u_guard)
        mu_ = mm.step(p.t.matmul(ps[0:64, 3584:3840], bb[:, BEa + h * 64:BEa + (h + 1) * 64],
                                 bb[:, xb:xb + 256], start=True, stop=True))
        xtm_guard[sl] = mu_
        p.wait("vector", mm, mu_)
        u_guard = ev.step(p.v.scalar_tensor_tensor(
            hst[:, h * 256:(h + 1) * 256], hst[:, h * 256:(h + 1) * 256],
            bb[0:64, S_DBC + h:S_DBC + h + 1], ps[0:64, 3584:3840],
            ALU.mult, ALU.add))
        if full:
            yb = XCE + h * 256
            p.wait("scalar", mm, my2)
            p.sd()
            y2_guard = act.step(p.s.activation(
                bb[:, yb:yb + 256], ps[:, 3072:3328], AF.Copy,
                scale=bb[:, S_E2 + h:S_E2 + h + 1]))
            p.wait("vector", mm, myi)
            p.wait("vector", act, y2_guard)
            yi_guard = ev.step(p.v.tensor_tensor(
                bb[:, yb:yb + 256], bb[:, yb:yb + 256],
                ps[:, 2560:2816], ALU.add))
            p.vd()
            xtm_dve[sl] = ev.step(p.v.scalar_tensor_tensor(
                bb[:, yb:yb + 256], bb[:, xb:xb + 256],
                keep[:, 48 + h:48 + h + 1],
                bb[:, yb:yb + 256], ALU.mult, ALU.add))
            p.vd()
    mend = mm.n
    for i in range(4):
        ring_guard[i] = max(ring_guard[i], mend)
    cx._last_m8 = m8

    if full:
        _mamba_tail(cx, sup)


def _mamba_tail(cx, sup):
    p, sems, sb, dr, ps = cx.p, cx.sems, cx.sb, cx.dr, cx.ps
    mm, ev, act = sems["mm"], sems["ev"], sems["act"]
    bb, bx, stats, keep = sb["BB"], sb["BX"], sb["stats"], sb["keep"]
    wl2 = sems["wl2"]

    if sup + 1 < 16:
        p.wait("vector", mm, cx._last_m8)
        sv2 = ev.step(p.v.tensor_scalar(bb[:, XST:XST + 1024],
                                        bx[:, (sup + 1) * 1024:(sup + 2) * 1024],
                                        stats[:, sup + 1:sup + 2],
                                        stats[:, 17 + sup:18 + sup],
                                        ALU.subtract, ALU.mult))
        p.vd()
        p.wait("tensor", act, act.n)
        dsts = [bb[:, XNF + k * 128:XNF + (k + 1) * 128] for k in range(8)]
        _transpose128(cx, bb[:, XST:XST + 1024], dsts, (ev, sv2),
                      pbase=2048, n_rot=4)
        cx._xn_pre = sup + 1
        cx._xn_ready = ev.n

    p.v.tensor_tensor(bb[:, XCE:XCE + 2048], bb[:, XCE:XCE + 2048],
                      bb[:, SZ:SZ + 2048], ALU.mult)
    yv = ev.step(p.v.engine_nop())
    p.vd()
    p.wait("scalar", ev, yv)
    p.sd()
    aa = act.step(p.s.activation(bb[:, SZ:SZ + 2048], bb[:, XCE:XCE + 2048],
                                 AF.Square, accum_out=stats[:, 92:93]))
    p.wait("vector", act, aa)
    vv = ev.step(p.v.tensor_scalar(stats[:, 92:93], stats[:, 92:93],
                                   1.0 / DIN, 1e-6, ALU.mult, ALU.add))
    p.vd()
    _rsqrt(cx, bb[:, S_RSTD:S_RSTD + 1], stats[:, 92:93],
           bb[:, S_RSCR:S_RSCR + 1], (ev, vv))
    p.v.scalar_tensor_tensor(bb[:, SZ:SZ + 2048], bb[:, XCE:XCE + 2048],
                             bb[:, S_RSTD:S_RSTD + 1], bb[:, NORMB:NORMB + 2048],
                             ALU.mult, ALU.mult)
    rv = ev.step(p.v.engine_nop())
    p.vd()
    dsts = [bb[:, XCE + k * 128:XCE + (k + 1) * 128] for k in range(16)]
    _transpose128(cx, bb[:, SZ:SZ + 2048], dsts, (ev, rv), pbase=0, n_rot=4)
    yfm_ready = ev.n

    opr_guard = getattr(cx, "_opr_guard", [0, 0])
    osems = (sems["oa"], sems["ob"])
    p.wait("tensor", ev, yfm_ready)
    for k in range(16):
        slot = k % 2
        osem = osems[slot]
        p.wait("sync", mm, opr_guard[slot])
        wv = osem.step(p.sy.dma_start(
            out=bb[:, OPR + slot * 1024:OPR + (slot + 1) * 1024],
            in_=dr["wop_t"][k * 128:(k + 1) * 128, :]), 16)
        p.wait("tensor", osem, wv)
        for half in range(2):
            tok = p.t.matmul(ps[:, half * 512:(half + 1) * 512],
                             bb[:, XCE + k * 128:XCE + (k + 1) * 128],
                             bb[:, OPR + slot * 1024 + half * 512:
                                 OPR + slot * 1024 + (half + 1) * 512],
                             start=(k == 0), stop=(k == 15))
        opr_guard[slot] = mm.step(tok)
    cx._opr_guard = opr_guard
    mv = mm.n
    g0 = _g0_ap(cx, sup)
    p.wait("vector", mm, mv)
    p.v.tensor_tensor(bb[:, SZ:SZ + 1024].rearrange("p (k t) -> p k t", t=128),
                      ps[:, 0:1024].rearrange("p (k t) -> p k t", t=128),
                      g0, ALU.mult)
    fv = ev.step(p.v.engine_nop())
    p.vd()
    p.v.tensor_tensor(bx[:, sup * 1024:(sup + 1) * 1024],
                      bx[:, sup * 1024:(sup + 1) * 1024],
                      bb[:, SZ:SZ + 1024], ALU.add)
    fz = ev.step(p.v.engine_nop())
    p.vd()
    p.wait("tensor", ev, fz)


def _stage_prev_state(cx):
    _mamba_pass(cx, full=False)
    p, sems, sb, dr, ps = cx.p, cx.sems, cx.sb, cx.dr, cx.ps
    mm, ev, act, ld = sems["mm"], sems["ev"], sems["act"], sems["ld"]
    bb, stats, keep, hst = sb["BB"], sb["stats"], sb["keep"], sb["hstate"]
    RCV, SND = OPR, OPR + 1024

    ld.step(p.sy.dma_start(out=stats[0:1, 120:121], in_=dr["pgate"][:, :]), 16)
    p.wait("sync", ev, ev.n)
    ld.step(p.sy.dma_start(out=bb[0:64, SND:SND + 1024],
                           in_=hst[0:64, 0:1024]), 16)
    ld.step(p.sy.dma_start(out=bb[64:128, SND:SND + 1024],
                           in_=hst[0:64, 1024:2048]), 16)

    prep, l2, r3 = sems["prep"], sems["lsem2"], sems["rsem3"]
    p.wait("gpsimd", ld, ld.n)
    rdests = [None] * 8
    rdests[1] = (0, 1)
    pv = prep.step(p.g.remote_dma_broadcast(
        bb[:, RCV:RCV + 1024], bb[:, SND:SND + 1024], r3.h, l2.h, rdests=rdests))
    p.wait("gpsimd", prep, pv)
    p.g.trigger_dma()
    l2.n += 16
    r3.n += 2
    p.wait("gpsimd", l2, l2.n)
    p.wait("gpsimd", r3, r3.n)
    gv = sems["gp"].step(p.g.engine_nop())

    p.wait("vector", sems["gp"], gv)
    p.v.tensor_copy(bb[:, RCV:RCV + 1024].bitcast(mybir.dt.int32),
                    bb[:, RCV:RCV + 1024].bitcast(mybir.dt.int32))
    pgv = ev.step(p.v.engine_nop())
    p.vd()
    p.wait("sync", ev, pgv)
    uf = ld.step(p.sy.dma_start(out=bb[0:64, SND:SND + 1024],
                                in_=bb[64:128, RCV:RCV + 1024]), 16)
    p.wait("vector", ld, uf)
    p.wait("tensor", ev, pgv)
    mv = mm.step(p.t.matmul(ps[:, 1024:1025], ONES1(cx, 128),
                            stats[0:1, 120:121], start=True, stop=True))
    p.wait("vector", mm, mv)
    p.v.tensor_copy(stats[:, 121:122], ps[:, 1024:1025])
    p.vd()
    for h in range(NH):
        rh = (bb[0:64, RCV + h * 256:RCV + (h + 1) * 256] if h < 4
              else bb[0:64, SND + (h - 4) * 256:SND + (h - 3) * 256])
        p.v.scalar_tensor_tensor(hst[:, h * 256:(h + 1) * 256], rh,
                                 stats[0:64, 112 + h:113 + h],
                                 hst[:, h * 256:(h + 1) * 256],
                                 ALU.mult, ALU.add)
    p.vd()
    p.v.tensor_scalar(hst[:, 0:2048], hst[:, 0:2048],
                      stats[0:64, 121:122], None, ALU.mult)
    p.vd()
    p.v.tensor_scalar(keep[:, 0:48], keep[:, 0:48],
                      stats[:, 121:122], None, ALU.mult)
    ev.step(p.v.engine_nop())
    p.vd()
    p.wait("sync", ev, ev.n)


def ONES1(cx, m):
    return C(cx, "ones_row", 1)[0:1, 0:m]


def _xn0g(cx, k, g):
    return cx.sb["BA"][:, g * 4096 + k * 512: g * 4096 + k * 512 + 512]


def _stage_experts(cx):
    p, sems, sb, dr, ps = cx.p, cx.sems, cx.sb, cx.dr, cx.ps
    ld, wl, mm, ev, act = sems["ld"], sems["wl"], sems["mm"], sems["ev"], sems["act"]
    bb, stats = sb["BB"], sb["stats"]
    wguard = [0, 0]
    wguard_ev = [0, 0]
    wguard_act = [0, 0]
    e_losscol = 13500

    p.v.memset(bb[:, e_losscol:e_losscol + 1], 0.0)
    p.vd()

    wv_prev = 0
    for e in range(NE):
        buf = e % 2
        w0 = buf * 4096
        p.wait("sync", mm, wguard[buf])
        p.wait("sync", ev, wguard_ev[buf])
        p.wait("sync", act, wguard_act[buf])
        p.wait("sync", wl, wv_prev)
        for k in range(8):
            wl.step(p.sy.dma_start(
                out=bb[:, w0 + k * 256: w0 + k * 256 + 256],
                in_=dr["wd1"][e * D + k * 128: e * D + (k + 1) * 128, :]), 16)
        for fc in range(2):
            wl.step(p.sy.dma_start(
                out=bb[:, w0 + 2048 + fc * 1024: w0 + 2048 + (fc + 1) * 1024],
                in_=dr["wd2"][e * HE + fc * 128: e * HE + (fc + 1) * 128, :]), 16)
        for fc in range(2):
            wl.step(p.sy.dma_start(
                out=bb[:, 12864 + buf * 2 + fc: 12864 + buf * 2 + fc + 1],
                in_=dr["bd1"][e * HE + fc * 128: e * HE + (fc + 1) * 128, :]), 16)
        for k in range(8):
            wl.step(p.sy.dma_start(
                out=bb[:, 12880 + buf * 8 + k: 12880 + buf * 8 + k + 1],
                in_=dr["bd2"][e * D + k * 128: e * D + (k + 1) * 128, :]), 16)
        wv = wl.n
        wv_prev = wv

        p.wait("tensor", wl, wv)
        hev = [0, 0]
        for g in range(4):
            for fc in range(2):
                hb = 2048 + fc * 512
                p.wait("tensor", act, hev[fc])
                for k in range(8):
                    tok = p.t.matmul(ps[:, hb:hb + 512],
                                     bb[:, w0 + k * 256 + fc * 128: w0 + k * 256 + fc * 128 + 128],
                                     _xn0g(cx, k, g),
                                     start=(k == 0), stop=(k == 7))
                mv = mm.step(tok)
                p.wait("scalar", mm, mv)
                p.sd()
                a = act.step(p.s.activation(
                    bb[:, 8192 + fc * 2048 + g * 512: 8192 + fc * 2048 + g * 512 + 512],
                    ps[:, hb:hb + 512], AF.Relu,
                    bias=bb[:, 12864 + buf * 2 + fc: 12864 + buf * 2 + fc + 1]))
                hev[fc] = a
        hready = act.n

        p.wait("tensor", act, hready)
        rguard = [0, 0, 0, 0]
        accv = 0
        for dchunk in range(8):
            for g in range(4):
                bank = (dchunk * 4 + g) % 4
                rb = bank * 512
                p.wait("tensor", ev, rguard[bank])
                for fc in range(2):
                    tok = p.t.matmul(ps[:, rb:rb + 512],
                                     bb[:, w0 + 2048 + fc * 1024 + dchunk * 128:
                                         w0 + 2048 + fc * 1024 + dchunk * 128 + 128],
                                     bb[:, 8192 + fc * 2048 + g * 512:
                                         8192 + fc * 2048 + g * 512 + 512],
                                     start=(fc == 0), stop=(fc == 1))
                mv = mm.step(tok)
                p.wait("vector", mm, mv)
                ve = ev.step(p.v.scalar_tensor_tensor(
                    bb[:, 12288:12800], ps[:, rb:rb + 512],
                    bb[:, 12880 + buf * 8 + dchunk: 12880 + buf * 8 + dchunk + 1],
                    _xn0g(cx, dchunk, g), ALU.add, ALU.subtract))
                rguard[bank] = ve
                p.wait("scalar", ev, ve)
                p.wait("scalar", ev, accv)
                p.sd()
                a = act.step(p.s.activation(
                    bb[:, 12900:13412], bb[:, 12288:12800], AF.Square,
                    accum_out=stats[:, 92:93]))
                p.wait("vector", act, a)
                p.vd()
                accv = ev.step(p.v.tensor_add(bb[:, e_losscol:e_losscol + 1],
                                              bb[:, e_losscol:e_losscol + 1],
                                              stats[:, 92:93]))
                p.vd()
        wguard[buf] = mm.n
        wguard_ev[buf] = ev.n
        wguard_act[buf] = act.n

        p.wait("tensor", ev, accv)
        mv = mm.step(p.t.matmul(ps[0:1, 3072:3073],
                                bb[:, e_losscol:e_losscol + 1],
                                C(cx, "ones_col"), start=True, stop=True))
        p.wait("vector", mm, mv)
        ev.step(p.v.tensor_copy(stats[0:1, 80 + e:80 + e + 1], ps[0:1, 3072:3073]))
        p.vd()
        p.v.memset(bb[:, e_losscol:e_losscol + 1], 0.0)
        p.vd()

    gp, prep, lsem, rsem = sems["gp"], sems["prep"], sems["lsem"], sems["rsem"]
    vready = ev.n
    p.wait("gpsimd", ev, vready)
    rdests = [None] * 8
    rdests[1] = (0, 1)
    pv = prep.step(p.g.remote_dma_broadcast(
        stats[:, 72:80], stats[:, 80:88], rsem.h, lsem.h, rdests=rdests))
    p.wait("gpsimd", prep, pv)
    p.g.trigger_dma()
    lsem.n += 16
    rsem.n += 2
    dr = cx.dr
    _quant_weights(cx, dr["w1_t"], D, HB, dr["w1i_hbm"],
                   qw_col=cx.sb["stats"][:, 96:97])
    _quant_weights(cx, dr["w2_t"], HB, D, dr["w2i_hbm"],
                   qw_col=cx.sb["stats"][:, 97:98])
    wl = sems["wl"]
    for k in range(8):
        wl.step(p.sy.dma_start(out=bb[:, k * 1024:(k + 1) * 1024],
                               in_=dr["w_base"][k * 128:(k + 1) * 128, :]), 16)
    wl.step(p.sy.dma_start(out=bb[0:8, 13072:13136], in_=dr["ba_down"][:, :]), 16)
    wl.step(p.sy.dma_start(out=bb[0:8, 13136:14160], in_=dr["ba_up"][:, :]), 16)
    wl.step(p.sy.dma_start(out=bb[0:1, 14160:15184], in_=dr["b_base"][:, :]), 16)
    cx._gate_wv0 = wl.n
    p.wait("gpsimd", lsem, lsem.n)
    p.wait("gpsimd", rsem, rsem.n)
    gv = sems["gp"].step(p.g.engine_nop())
    p.wait("vector", sems["gp"], gv)
    p.v.tensor_copy(stats[:, 72:80].bitcast(mybir.dt.int32),
                    stats[:, 72:80].bitcast(mybir.dt.int32))
    p.vd()
    p.v.tensor_add(stats[0:1, 88:96], stats[0:1, 80:88], stats[0:1, 72:80])
    p.vd()
    p.v.tensor_reduce(stats[0:1, 71:72], stats[0:1, 88:96], AXX, ALU.min)
    p.vd()
    p.v.tensor_single_scalar(stats[0:1, 64:72], stats[0:1, 88:96],
                             stats[0:1, 71:72], ALU.is_le)
    p.vd()
    ev.step(p.v.engine_nop())


def _g0_ap(cx, c):
    g, cc = c // 4, c % 4
    return (cx.sb["BA"][:, g * 4096:(g + 1) * 4096]
            .rearrange("p (k t) -> p k t", t=512)[:, :, cc * 128:(cc + 1) * 128])


def _stage_gate_ffn(cx):
    p, sems, sb, dr, ps = cx.p, cx.sems, cx.sb, cx.dr, cx.ps
    ld, wl, mm, ev, act = sems["ld"], sems["wl"], sems["mm"], sems["ev"], sems["act"]
    bb, ba, stats = sb["BB"], sb["BA"], sb["stats"]

    wv0 = cx._gate_wv0

    ohrow = stats[0:1, 64:72]
    m1 = mm.step(p.t.matmul(ps[0:8, 3072:3080], ONES1(cx, 8),
                            ohrow, start=True, stop=True))
    p.wait("vector", mm, m1)
    ev.step(p.v.tensor_copy(bb[0:8, 15184:15192], ps[0:8, 3072:3080]))
    p.vd()
    p.v.tensor_tensor(bb[0:8, 15184:15192], bb[0:8, 15184:15192],
                      C(cx, "ident", 8, 8), ALU.mult)
    p.vd()
    e1 = ev.step(p.v.reduce_sum(bb[0:8, 15376:15377], bb[0:8, 15184:15192], axis=AXX))
    p.vd()
    m2 = mm.step(p.t.matmul(ps[:, 3584:3592], ONES1(cx, 128), ohrow,
                            start=True, stop=True))
    p.wait("vector", mm, m2)
    ev.step(p.v.tensor_copy(bb[:, 10752:10760], ps[:, 3584:3592]))
    p.vd()

    p.wait("tensor", ev, e1)
    p.wait("tensor", wl, wv0)
    m3 = mm.step(p.t.matmul(ps[0:1, 3072:3136], bb[0:8, 15376:15377],
                            bb[0:8, 13072:13136], start=True, stop=True))
    p.wait("vector", mm, m3)
    e2 = ev.step(p.v.tensor_copy(bb[0:1, 15400:15464], ps[0:1, 3072:3136]))
    p.vd()
    m4 = mm.step(p.t.matmul(ps[0:1, 2048:2560], bb[0:8, 15376:15377],
                            bb[0:8, 13136:13136 + 512], start=True, stop=True))
    m4 = mm.step(p.t.matmul(ps[0:1, 2560:3072], bb[0:8, 15376:15377],
                            bb[0:8, 13136 + 512:13136 + 1024], start=True, stop=True))
    p.wait("vector", mm, m4)
    e3 = ev.step(p.v.tensor_copy(bb[0:1, 11024:12048],
                                 ps[0:1, 2048:3072]))
    p.vd()
    baup_row = bb[0:1, 11024:12048]

    p.wait("tensor", ev, e2)
    m5 = mm.step(p.t.matmul(ps[0:64, 3072:3136], ONES1(cx, 64),
                            bb[0:1, 15400:15464], start=True, stop=True))
    p.wait("vector", mm, m5)
    ev.step(p.v.tensor_copy(bb[0:64, 15312:15376], ps[0:64, 3072:3136]))
    p.vd()
    p.v.tensor_tensor(bb[0:64, 15312:15376], bb[0:64, 15312:15376],
                      C(cx, "ident", 64, 64), ALU.mult)
    p.vd()
    e4 = ev.step(p.v.reduce_sum(bb[0:64, 10764:10765], bb[0:64, 15312:15376], axis=AXX))
    p.vd()

    p.wait("tensor", ev, e3)
    for half in range(2):
        p.t.matmul(ps[:, half * 512:(half + 1) * 512], ONES1(cx, 128),
                   bb[0:1, 14160 + half * 512:14160 + (half + 1) * 512],
                   start=True, stop=False)
        tok = p.t.matmul(ps[:, half * 512:(half + 1) * 512], ONES1(cx, 128),
                         baup_row[0:1, half * 512:(half + 1) * 512],
                         start=False, stop=True)
    m6 = mm.step(tok)
    p.wait("vector", mm, m6)
    e5 = ev.step(p.v.tensor_copy(bb[:, 9728:10752], ps[:, 0:1024]))
    p.vd()

    osems = (sems["oa"], sems["ob"])
    ring = [10896, 10960]
    rl = [0, 0]
    for e in range(NE):
        for k in range(8):
            slot = k % 2
            p.wait("sync", ev, rl[slot])
            wv = osems[slot].step(p.sy.dma_start(
                out=bb[:, ring[slot]:ring[slot] + 64],
                in_=dr["wa_down"][e * D + k * 128:e * D + (k + 1) * 128, :]), 16)
            p.wait("vector", osems[slot], wv)
            if e == 0:
                tok = p.v.tensor_scalar(bb[:, 8192 + k * 64:8192 + (k + 1) * 64],
                                        bb[:, ring[slot]:ring[slot] + 64],
                                        bb[:, 10752 + e:10752 + e + 1], None,
                                        ALU.mult)
            else:
                tok = p.v.scalar_tensor_tensor(
                    bb[:, 8192 + k * 64:8192 + (k + 1) * 64],
                    bb[:, ring[slot]:ring[slot] + 64],
                    bb[:, 10752 + e:10752 + e + 1],
                    bb[:, 8192 + k * 64:8192 + (k + 1) * 64],
                    ALU.mult, ALU.add)
            rl[slot] = ev.step(tok)
            p.vd()
    rl2 = list(rl)
    for e in range(NE):
        slot = e % 2
        base = 12048 + slot * 1024
        p.wait("sync", ev, rl2[slot])
        wv = osems[slot].step(p.sy.dma_start(out=bb[0:64, base:base + 1024],
                                             in_=dr["wa_up"][e * RA:(e + 1) * RA, :]), 16)
        p.wait("vector", osems[slot], wv)
        if e == 0:
            p.v.tensor_scalar(bb[0:64, 8704:9728], bb[0:64, base:base + 1024],
                              bb[0:64, 10752 + e:10752 + e + 1], None, ALU.mult)
        else:
            p.v.scalar_tensor_tensor(bb[0:64, 8704:9728], bb[0:64, base:base + 1024],
                                     bb[0:64, 10752 + e:10752 + e + 1],
                                     bb[0:64, 8704:9728], ALU.mult, ALU.add)
        p.vd()
        rl2[slot] = ev.step(p.v.engine_nop())

    p.wait("tensor", ev, ev.n)
    p.wait("tensor", act, act.n)
    bank_guard = [0, 0]
    last_aa = 0
    for c in range(16):
        g, cc = c // 4, c % 4
        p.wait("tensor", act, last_aa)
        for k in range(8):
            tok = p.t.matmul(ps[0:64, 3072:3200],
                             bb[:, 8192 + k * 64:8192 + (k + 1) * 64],
                             _xn0c(cx, k, c), start=(k == 0), stop=(k == 7))
        ma = mm.step(tok)
        p.wait("scalar", mm, ma)
        p.sd()
        aa = act.step(p.s.activation(bb[0:64, 10768:10896], ps[0:64, 3072:3200],
                                     AF.Relu, bias=bb[0:64, 10764:10765]))
        last_aa = aa
        pb = (c % 2) * 1024
        p.wait("tensor", ev, bank_guard[c % 2])
        for k in range(8):
            for half in range(2):
                p.t.matmul(ps[:, pb + half * 512:pb + (half + 1) * 512],
                           _xn0c(cx, k, c),
                           bb[:, k * 1024 + half * 512:k * 1024 + (half + 1) * 512],
                           start=(k == 0), stop=False)
        p.wait("tensor", act, aa)
        for half in range(2):
            tok = p.t.matmul(ps[:, pb + half * 512:pb + (half + 1) * 512],
                             bb[0:64, 10768:10896],
                             bb[0:64, 8704 + half * 512:8704 + (half + 1) * 512],
                             start=False, stop=True)
        mv = mm.step(tok)
        g0 = _g0_ap(cx, c)
        p.wait("vector", mm, mv)
        vv = ev.step(p.v.tensor_tensor(
            g0, ps[:, pb:pb + 1024].rearrange("p (k t) -> p k t", t=128),
            bb[:, 9728:10752].rearrange("p (k t) -> p k t", t=128), ALU.add))
        bank_guard[c % 2] = vv
        p.wait("scalar", ev, vv)
        p.sd()
        act.step(p.s.activation(g0, g0, AF.Silu))
    p.wait("vector", act, act.n)
    ev.step(p.v.engine_nop())


def _xn0c(cx, k, c):
    g, cc = c // 4, c % 4
    base = g * 4096 + k * 512 + cc * 128
    return cx.sb["BA"][:, base:base + 128]


def _stage_mamba(cx):
    _mamba_pass(cx, full=True)


def _stage_l1_ffn(cx):
    p, sems, sb, dr, ps = cx.p, cx.sems, cx.sb, cx.dr, cx.ps
    ld, wl, wl2, mm, ev, act = (sems["ld"], sems["wl"], sems["wl2"], sems["mm"],
                                sems["ev"], sems["act"])
    bb, bx, ba, stats = sb["BB"], sb["BX"], sb["BA"], sb["stats"]
    H, X1I, X2I, N1BC, N2BC = 0, 4096, 5120, 11264, 12288

    chunks = [bx[:, c * 1024:(c + 1) * 1024] for c in range(16)]
    _layernorm_16(cx, chunks, stat_base=32, wait_sem=(sems["bar"], 0))
    p.barrier()

    lv = ld.step(p.sy.dma_start(out=bb[0:1, N1BC:N1BC + 1024],
                                in_=dr["norm1_w"][:, :]), 16)
    lv = ld.step(p.sy.dma_start(out=bb[0:1, N2BC:N2BC + 4096],
                                in_=dr["norm2_w"][:, :]), 16)
    p.wait("tensor", ld, lv)
    p.wait("tensor", ev, ev.n)
    for base, n in ((N1BC, 1024), (N2BC, 4096)):
        for q in range(n // 512):
            mv = mm.step(p.t.matmul(ps[:, 1024:1536], ONES1(cx, 128),
                                    bb[0:1, base + q * 512:base + (q + 1) * 512],
                                    start=True, stop=True))
            p.wait("vector", mm, mv)
            vv = ev.step(p.v.tensor_copy(bb[:, base + q * 512:base + (q + 1) * 512],
                                         ps[:, 1024:1536]))
            p.vd()
            p.wait("tensor", ev, vv)
    mv = mm.step(p.t.matmul(ps[:, 1024:1025], ONES1(cx, 128), C(cx, "act_s", 1),
                            start=True, stop=True))
    mv = mm.step(p.t.matmul(ps[:, 1025:1026], ONES1(cx, 128), C(cx, "act_b", 1),
                            start=True, stop=True))
    p.wait("vector", mm, mv)
    ev.step(p.v.tensor_copy(stats[:, 89:90], ps[:, 1024:1025]))
    ev.step(p.v.tensor_copy(stats[:, 90:91], ps[:, 1025:1026]))
    p.vd()
    p.barrier()

    wring_guard = [0, 0, 0, 0]
    h1_aa = [0, 0]
    x1tb, x2tb = sb["X1TB"], sb["X2TB"]
    wbv = sb["WB4"]
    WSL = [s * 1024 for s in range(4)]
    gds = (sems["gdma"], sems["gdb"], sems["gdc"], sems["gdd"])
    ident = C(cx, "ident")
    if not hasattr(cx, "_tb_ev"):
        cx._tb_ev = [0, 0, 0, 0]

    def t_one(src, dst):
        b = cx._tb_i = (getattr(cx, "_tb_i", -1) + 1) % 4
        if cx._tb_ev[b]:
            p.wait("tensor", ev, cx._tb_ev[b])
        pc = 1024 + b * 512
        me = mm.step(p.t.transpose(ps[:, pc:pc + 128], src, ident))
        p.wait("vector", mm, me)
        cx._tb_ev[b] = ev.step(p.v.tensor_copy(dst, ps[:, pc:pc + 128]))
        return cx._tb_ev[b]

    pend = None

    def flush_g1():
        nonlocal pend
        if pend is None:
            return
        pc_, pmv, psc = pend
        p.wait("vector", mm, pmv)
        p.v.tensor_scalar(ba[:, pc_ * 1024:(pc_ + 1) * 1024], ps[:, 0:1024],
                          stats[:, psc:psc + 1], None, ALU.mult)
        gv = ev.step(p.v.engine_nop())
        p.vd()
        p.wait("scalar", ev, gv)
        p.sd()
        act.step(p.s.activation(ba[:, pc_ * 1024:(pc_ + 1) * 1024],
                                ba[:, pc_ * 1024:(pc_ + 1) * 1024], AF.Silu))
        pend = None

    for c in range(16):
        SC, R1, R2, R3 = (84, 91, 92, 85) if c % 2 == 0 else (101, 102, 103, 104)
        p.v.tensor_scalar(bb[:, H:H + 1024], bx[:, c * 1024:(c + 1) * 1024],
                          stats[:, 32 + c:32 + c + 1], stats[:, 48 + c:48 + c + 1],
                          ALU.subtract, ALU.mult)
        p.vd()
        xv = ev.step(p.v.engine_nop())
        p.wait("scalar", ev, xv)
        p.sd()
        aa = act.step(p.s.activation(bb[:, H + 1024:H + 2048], bb[:, H:H + 1024],
                                     AF.Square, accum_out=stats[:, R2:R2 + 1]))
        p.wait("vector", act, aa)
        vv = ev.step(p.v.tensor_scalar(stats[:, R2:R2 + 1], stats[:, R2:R2 + 1],
                                       1.0 / D, 1e-8, ALU.mult, ALU.add))
        p.vd()
        _rsqrt(cx, stats[:, R1:R1 + 1], stats[:, R2:R2 + 1],
               stats[:, R3:R3 + 1], (ev, vv))
        p.v.scalar_tensor_tensor(bb[:, H:H + 1024], bb[:, H:H + 1024],
                                 stats[:, R1:R1 + 1], bb[:, N1BC:N1BC + 1024],
                                 ALU.mult, ALU.mult)
        p.vd()
        p.v.reduce_max(stats[:, R2:R2 + 1], bb[:, H:H + 1024], axis=AXX,
                       apply_absolute_value=True)
        p.vd()
        p.v.tensor_scalar_max(stats[:, R2:R2 + 1], stats[:, R2:R2 + 1], 1e-5)
        p.vd()
        _recip(cx, stats[:, R1:R1 + 1], stats[:, R2:R2 + 1])
        p.v.tensor_scalar_mul(stats[:, R1:R1 + 1], stats[:, R1:R1 + 1], 127.0)
        p.vd()
        p.v.tensor_scalar(stats[:, SC:SC + 1], stats[:, R2:R2 + 1], 1.0 / 127.0,
                          None, ALU.mult)
        p.vd()
        p.v.tensor_tensor(stats[:, SC:SC + 1], stats[:, SC:SC + 1],
                          stats[:, 96:97], ALU.mult)
        p.vd()
        p.v._custom_dve(RINT, out=bb[:, X1I:X1I + 1024], in0=bb[:, H:H + 1024],
                        s0=MAGIC, s1=stats[:, R1:R1 + 1])
        xv = ev.step(p.v.engine_nop())
        p.vd()
        p.wait("tensor", ev, xv)
        for k in range(8):
            t_one(bb[:, X1I + k * 128:X1I + (k + 1) * 128],
                  x1tb[:, k * 128:(k + 1) * 128])

        flush_g1()

        p.wait("tensor", ev, ev.n)
        for grp in range(4):
            pb = 0 if grp % 2 == 0 else 3072
            if h1_aa[grp % 2]:
                p.wait("tensor", act, h1_aa[grp % 2])
            for k in range(8):
                s = cx._wli = (getattr(cx, "_wli", -1) + 1) % 4
                gd = gds[s]
                p.wait("gpsimd", mm, wring_guard[s])
                wv = gd.step(p.g.dma_start(
                    out=wbv[:, WSL[s]:WSL[s] + 1024],
                    in_=dr["w1i_hbm"][k * 128:(k + 1) * 128,
                                      grp * 1024:(grp + 1) * 1024]), 16)
                p.wait("tensor", gd, wv)
                for half in range(2):
                    tok = p.t.matmul(
                        ps[:, pb + half * 512:pb + (half + 1) * 512],
                        x1tb[:, k * 128:(k + 1) * 128],
                        wbv[:, WSL[s] + half * 512:WSL[s] + (half + 1) * 512],
                        start=(k == 0), stop=(k == 7))
                wring_guard[s] = mm.step(tok)
            mv = mm.n
            p.wait("scalar", mm, mv)
            p.sd()
            hg = bb[:, H + grp * 1024:H + (grp + 1) * 1024]
            h1_aa[grp % 2] = act.step(p.s.activation(
                hg, ps[:, pb:pb + 1024], AF.Relu, scale=stats[:, SC:SC + 1]))
            p.sd()
            sqa = act.step(p.s.activation(hg, hg, AF.Square))
            p.wait("vector", act, sqa)
            afv = ev.step(p.v.tensor_scalar(hg, hg, stats[:, 89:90],
                                            stats[:, 90:91], ALU.mult, ALU.add))
            p.vd()
            p.wait("scalar", ev, afv)
            p.sd()
            act.step(p.s.activation(bb[:, X2I + grp * 1024:X2I + (grp + 1) * 1024],
                                    hg, AF.Square,
                                    accum_out=stats[:, 105 + grp:106 + grp]))
        p.wait("vector", act, act.n)
        p.v.tensor_reduce(stats[:, R2:R2 + 1], stats[:, 105:109], AXX, ALU.add)
        p.vd()
        vv = ev.step(p.v.tensor_scalar(stats[:, R2:R2 + 1], stats[:, R2:R2 + 1],
                                       1.0 / HB, 1e-8, ALU.mult, ALU.add))
        p.vd()
        _rsqrt(cx, stats[:, R1:R1 + 1], stats[:, R2:R2 + 1],
               stats[:, R3:R3 + 1], (ev, vv))
        p.v.scalar_tensor_tensor(bb[:, H:H + 4096], bb[:, H:H + 4096],
                                 stats[:, R1:R1 + 1], bb[:, N2BC:N2BC + 4096],
                                 ALU.mult, ALU.mult)
        p.vd()
        p.v.reduce_max(stats[:, R2:R2 + 1], bb[:, H:H + 4096], axis=AXX,
                       apply_absolute_value=True)
        p.vd()
        p.v.tensor_scalar_max(stats[:, R2:R2 + 1], stats[:, R2:R2 + 1], 1e-5)
        p.vd()
        _recip(cx, stats[:, R1:R1 + 1], stats[:, R2:R2 + 1])
        p.v.tensor_scalar_mul(stats[:, R1:R1 + 1], stats[:, R1:R1 + 1], 127.0)
        p.vd()
        p.v.tensor_scalar(stats[:, SC:SC + 1], stats[:, R2:R2 + 1], 1.0 / 127.0,
                          None, ALU.mult)
        p.vd()
        p.v.tensor_tensor(stats[:, SC:SC + 1], stats[:, SC:SC + 1],
                          stats[:, 97:98], ALU.mult)
        p.vd()
        p.v._custom_dve(RINT, out=bb[:, X2I:X2I + 4096], in0=bb[:, H:H + 4096],
                        s0=MAGIC, s1=stats[:, R1:R1 + 1])
        xv = ev.step(p.v.engine_nop())
        p.vd()
        p.wait("tensor", ev, xv)
        evk = [0] * 32
        for k in range(4):
            evk[k] = t_one(bb[:, X2I + k * 128:X2I + (k + 1) * 128],
                           x2tb[:, (k % 4) * 128:(k % 4) * 128 + 128])

        if h1_aa[0]:
            p.wait("tensor", act, h1_aa[0])
        for k in range(32):
            s = cx._wli = (cx._wli + 1) % 4
            gd = gds[s]
            p.wait("gpsimd", mm, wring_guard[s])
            wv = gd.step(p.g.dma_start(
                out=wbv[:, WSL[s]:WSL[s] + 1024],
                in_=dr["w2i_hbm"][k * 128:(k + 1) * 128, :]), 16)
            p.wait("tensor", gd, wv)
            p.wait("tensor", ev, evk[k])
            for half in range(2):
                tok = p.t.matmul(
                    ps[:, half * 512:(half + 1) * 512],
                    x2tb[:, (k % 4) * 128:(k % 4) * 128 + 128],
                    wbv[:, WSL[s] + half * 512:WSL[s] + (half + 1) * 512],
                    start=(k == 0), stop=(k == 31))
            wring_guard[s] = mm.step(tok)
            if k + 4 < 32:
                evk[k + 4] = t_one(
                    bb[:, X2I + (k + 4) * 128:X2I + (k + 5) * 128],
                    x2tb[:, ((k + 4) % 4) * 128:((k + 4) % 4) * 128 + 128])
        mv = mm.n
        pend = (c, mv, SC)
    flush_g1()
    p.wait("vector", act, act.n)
    ev.step(p.v.engine_nop())


def _quant_weights(cx, w_dram, rows, cols, wi_hbm, qw_col):
    p, sems, sb, dr, ps = cx.p, cx.sems, cx.sb, cx.dr, cx.ps
    mm, ev, act = sems["mm"], sems["ev"], sems["act"]
    qs = (sems["qa"], sems["qb"])
    sts = (sems["sta"], sems["stb"])
    bb, stats = sb["BB"], sb["stats"]
    nk = rows // 128
    p.wait("sync", mm, mm.n)
    p.wait("sync", ev, ev.n)
    p.wait("sync", act, act.n)
    p.v.memset(stats[:, 98:99], 0.0)
    p.vd()
    lv = [0, 0]
    red_ev = [0, 0]
    for k in range(nk + 1):
        s = k % 2
        if k < nk:
            if red_ev[s]:
                p.wait("sync", ev, red_ev[s])
            lv[s] = qs[s].step(p.sy.dma_start(
                out=bb[:, s * cols:(s + 1) * cols],
                in_=w_dram[k * 128:(k + 1) * 128, :]), 16)
        if k > 0:
            sp = (k - 1) % 2
            p.wait("vector", qs[sp], lv[sp])
            p.v.tensor_reduce(stats[:, 100:101], bb[:, sp * cols:(sp + 1) * cols],
                              AXX, ALU.add, apply_absolute_value=True)
            p.vd()
            p.v.tensor_add(stats[:, 98:99], stats[:, 98:99], stats[:, 100:101])
            red_ev[sp] = ev.step(p.v.engine_nop())
            p.vd()
    p.wait("tensor", ev, ev.n)
    mv = mm.step(p.t.matmul(ps[0:1, 1024:1025], stats[:, 98:99],
                            C(cx, "ones_col"), start=True, stop=True))
    p.wait("vector", mm, mv)
    p.v.tensor_scalar(stats[0:1, 100:101], ps[0:1, 1024:1025],
                      1.0 / (rows * cols), 1e-5, ALU.mult, ALU.max)
    p.vd()
    sv = ev.step(p.v.engine_nop())
    p.wait("tensor", ev, sv)
    mv = mm.step(p.t.matmul(ps[:, 1024:1025], ONES1(cx, 128), stats[0:1, 100:101],
                            start=True, stop=True))
    p.wait("vector", mm, mv)
    p.v.tensor_copy(qw_col, ps[:, 1024:1025])
    p.vd()
    _recip(cx, stats[:, 99:100], qw_col)
    ev.step(p.v.engine_nop())
    p.vd()
    lv = [0, 0]
    used = list(red_ev)
    stv = [0, 0]
    for k in range(nk + 1):
        s = k % 2
        if k < nk:
            if used[s]:
                p.wait("sync", ev, used[s])
            lv[s] = qs[s].step(p.sy.dma_start(
                out=bb[:, s * cols:(s + 1) * cols],
                in_=w_dram[k * 128:(k + 1) * 128, :]), 16)
        if k > 0:
            kk = k - 1
            sp = kk % 2
            p.wait("vector", qs[sp], lv[sp])
            if stv[sp]:
                p.wait("vector", sts[sp], stv[sp])
            p.v._custom_dve(RINT, out=bb[:, (2 + sp) * cols:(3 + sp) * cols],
                            in0=bb[:, sp * cols:(sp + 1) * cols],
                            s0=MAGIC, s1=stats[:, 99:100])
            used[sp] = ev.step(p.v.engine_nop())
            p.vd()
            p.v.tensor_scalar(bb[:, (2 + sp) * cols:(3 + sp) * cols],
                              bb[:, (2 + sp) * cols:(3 + sp) * cols],
                              -1.0, 1.0, ALU.max, ALU.min)
            qv = ev.step(p.v.engine_nop())
            p.vd()
            p.wait("gpsimd", ev, qv)
            stv[sp] = sts[sp].step(p.g.dma_start(
                out=wi_hbm[kk * 128:(kk + 1) * 128, :],
                in_=bb[:, (2 + sp) * cols:(3 + sp) * cols]), 16)
    for e in ("sync", "vector", "gpsimd", "tensor"):
        p.wait(e, sems["sta"], sems["sta"].n)
        p.wait(e, sems["stb"], sems["stb"].n)


AM, APR, ATMP, ARG, AQ, AV, AKT, AKF, ASB = (
    0, 8192, 8336, 9360, 11408, 12432, 13456, 14480, 15504)


def _stage_l1_attn(cx):
    p, sems, sb, dr, ps = cx.p, cx.sems, cx.sb, cx.dr, cx.ps
    ld, wl2, mm, ev, act = (sems["ld"], sems["wl2"], sems["mm"],
                            sems["ev"], sems["act"])
    bb, bx, ba, stats, keep = sb["BB"], sb["BX"], sb["BA"], sb["stats"], sb["keep"]

    P0 = APR
    WL = APR + 16
    p.v.memset(bb[:, AM:AM + 8192], 0.0)
    p.vd()
    mzv = ev.step(p.v.engine_nop())

    aa = act.step(p.s.activation(stats[0:1, 82:83], C(cx, "decay", 1), AF.Sigmoid))
    p.sd()
    p.wait("scalar", act, aa)
    aa = act.step(p.s.activation(stats[0:1, 82:83], stats[0:1, 82:83], AF.Ln))
    p.sd()
    p.wait("tensor", act, aa)
    mv = mm.step(p.t.matmul(ps[:, 1024:1025], ONES1(cx, 128), stats[0:1, 82:83],
                            start=True, stop=True))
    mv = mm.step(p.t.matmul(ps[:, 1025:1026], ONES1(cx, 128), C(cx, "oscale", 1),
                            start=True, stop=True))
    mv = mm.step(p.t.matmul(ps[:, 1026:1027], ONES1(cx, 128), C(cx, "gate_attn", 1),
                            start=True, stop=True))
    p.wait("vector", mm, mv)
    ev.step(p.v.tensor_copy(bb[:, P0 + 5:P0 + 6], ps[:, 1024:1025]))
    ev.step(p.v.tensor_copy(bb[:, P0 + 3:P0 + 4], ps[:, 1025:1026]))
    ev.step(p.v.tensor_copy(bb[:, P0 + 4:P0 + 5], ps[:, 1026:1027]))
    p.vd()
    p.v.tensor_tensor(bb[:, P0 + 0:P0 + 1], C(cx, "iota"), bb[:, P0 + 5:P0 + 6],
                      ALU.mult)
    p.vd()
    p.v.tensor_scalar(bb[:, P0 + 1:P0 + 2], C(cx, "iota"), 127.0, -1.0,
                      ALU.subtract, ALU.mult)
    p.vd()
    p.v.tensor_tensor(bb[:, P0 + 1:P0 + 2], bb[:, P0 + 1:P0 + 2],
                      bb[:, P0 + 5:P0 + 6], ALU.mult)
    p.vd()
    p.v.tensor_scalar_mul(bb[:, P0 + 2:P0 + 3], bb[:, P0 + 5:P0 + 6], 128.0)
    vv = ev.step(p.v.engine_nop())
    p.vd()
    p.wait("scalar", ev, vv)
    p.sd()
    act.step(p.s.activation(bb[:, P0 + 0:P0 + 1], bb[:, P0 + 0:P0 + 1], AF.Exp))
    p.sd()
    act.step(p.s.activation(bb[:, P0 + 1:P0 + 2], bb[:, P0 + 1:P0 + 2], AF.Exp))
    p.sd()
    aa = act.step(p.s.activation(bb[:, P0 + 2:P0 + 3], bb[:, P0 + 2:P0 + 3], AF.Exp))
    p.sd()
    p.wait("vector", act, aa)
    p.v.tensor_scalar(bb[:, WL:WL + 128], C(cx, "ident"), 0.0, C(cx, "iota"),
                      ALU.mult, ALU.add)
    tv = ev.step(p.v.engine_nop())
    p.vd()
    p.wait("tensor", ev, tv)
    mv = mm.step(p.t.transpose(ps[:, 1280:1408], bb[:, WL:WL + 128],
                               C(cx, "ident")))
    p.wait("vector", mm, mv)
    p.v.tensor_scalar(bb[:, WL:WL + 128], ps[:, 1280:1408], C(cx, "iota"), None,
                      ALU.subtract)
    p.vd()
    p.v.tensor_scalar(bb[:, WL:WL + 128], bb[:, WL:WL + 128], 1.0, None, ALU.add)
    p.vd()
    p.v.tensor_scalar(bb[:, WL:WL + 128], bb[:, WL:WL + 128], -1.0,
                      bb[:, P0 + 5:P0 + 6], ALU.mult, ALU.mult)
    p.vd()
    p.v.tensor_scalar_min(bb[:, WL:WL + 128], bb[:, WL:WL + 128], 0.0)
    wv = ev.step(p.v.engine_nop())
    p.vd()
    p.wait("scalar", ev, wv)
    p.sd()
    aa = act.step(p.s.activation(bb[:, WL:WL + 128], bb[:, WL:WL + 128], AF.Exp))
    p.wait("vector", act, aa)
    wlv = ev.step(p.v.tensor_tensor(bb[:, WL:WL + 128], bb[:, WL:WL + 128],
                                    C(cx, "tri_strict"), ALU.mult))
    p.vd()

    p.barrier()

    ring_guard = [0, 0]
    for ci in range(16):
        c = 15 - ci
        defer = (c >= 12)
        _l1_chunk(cx, c, defer, ring_guard)

    gp, prep, lsem, rsem = sems["gp"], sems["prep"], sems["lsem"], sems["rsem"]
    rsem2 = sems["rsem2"]
    keep = sb["keep"]
    p.wait("gpsimd", ev, ev.n)
    p.wait("gpsimd", mm, mm.n)
    rdests = [None] * 8
    rdests[1] = (0, 1)
    pv = prep.step(p.g.remote_dma_broadcast(
        keep[:, 60:64], keep[:, 56:60], rsem2.h, lsem.h, rdests=rdests))
    p.wait("gpsimd", prep, pv)
    p.g.trigger_dma(1)
    lsem.n += 16
    rsem2.n += 2
    p.wait("gpsimd", lsem, lsem.n)
    p.wait("gpsimd", rsem2, rsem2.n)
    for i in range(8):
        pv = prep.step(p.g.remote_dma_broadcast(
            ba[:, i * 1024:(i + 1) * 1024], bb[:, AM + i * 1024:AM + (i + 1) * 1024],
            rsem.h, lsem.h, rdests=rdests))
    p.wait("gpsimd", prep, prep.n)
    p.g.trigger_dma(8)
    lsem.n += 8 * 16
    rsem.n += 8 * 2
    p.wait("gpsimd", lsem, lsem.n)
    p.wait("gpsimd", rsem, rsem.n)
    gv = gp.step(p.g.engine_nop())
    p.wait("vector", gp, gv)
    p.wait("tensor", gp, gv)
    p.wait("sync", gp, gv)
    p.v.tensor_copy(ba[:, 0:8192].bitcast(mybir.dt.int32),
                    ba[:, 0:8192].bitcast(mybir.dt.int32))
    p.vd()
    ev.step(p.v.engine_nop())
    p.wait("tensor", sems["ev"], sems["ev"].n)

    for c in range(12, 16):
        _l1_correction(cx, c, ring_guard)


def _l1_chunk(cx, c, defer, ring_guard):
    p, sems, sb, dr, ps = cx.p, cx.sems, cx.sb, cx.dr, cx.ps
    ld, wl2, mm, ev, act = (sems["ld"], sems["wl2"], sems["mm"],
                            sems["ev"], sems["act"])
    bb, bx, ba, stats = sb["BB"], sb["BX"], sb["BA"], sb["stats"]
    P0 = APR

    p.wait("vector", mm, mm.n)
    sv = ev.step(p.v.tensor_scalar(bb[:, ATMP:ATMP + 1024],
                                   bx[:, c * 1024:(c + 1) * 1024],
                                   stats[:, 32 + c:32 + c + 1],
                                   stats[:, 48 + c:48 + c + 1],
                                   ALU.subtract, ALU.mult))
    p.vd()
    _inplace_t128(cx, ATMP, 8, (ev, sv))
    xnf_ready = ev.n

    p.wait("tensor", ev, xnf_ready)
    osems = (sems["oa"], sems["ob"])
    pbank = [0, 0]
    for which, wsrc, dstc, scale, pp in (("k", "wk_t", AKT, None, 0),
                                         ("v", "wv_t", AV, None, 1024),
                                         ("q", "wq_t", AQ, 1.0 / 32.0, 0)):
        if pbank[pp // 1024]:
            p.wait("tensor", ev, pbank[pp // 1024])
        for k in range(8):
            slot = k % 2
            p.wait("sync", mm, ring_guard[slot])
            wv = osems[slot].step(p.sy.dma_start(
                out=bb[:, ARG + slot * 1024:ARG + (slot + 1) * 1024],
                in_=dr[wsrc][k * 128:(k + 1) * 128, :]), 16)
            p.wait("tensor", osems[slot], wv)
            for half in range(2):
                tok = p.t.matmul(ps[:, pp + half * 512:pp + (half + 1) * 512],
                                 bb[:, ATMP + k * 128:ATMP + (k + 1) * 128],
                                 bb[:, ARG + slot * 1024 + half * 512:
                                     ARG + slot * 1024 + (half + 1) * 512],
                                 start=(k == 0), stop=(k == 7))
            ring_guard[slot] = mm.step(tok)
        mv = mm.n
        p.wait("vector", mm, mv)
        if scale is None:
            vv = ev.step(p.v.tensor_copy(bb[:, dstc:dstc + 1024],
                                         ps[:, pp:pp + 1024]))
        else:
            p.wait("vector", sems["store"], sems["store"].n)
            vv = ev.step(p.v.tensor_scalar_mul(bb[:, dstc:dstc + 1024],
                                               ps[:, pp:pp + 1024], scale))
        p.vd()
        pbank[pp // 1024] = vv
    _oop_t128(cx, AKT, AKF, 8)
    _inplace_t128(cx, AQ, 8, (ev, ev.n))
    kq_ready = ev.n
    if defer:
        p.wait("gpsimd", ev, kq_ready)
        sems["store"].step(p.g.dma_start(
            out=dr["q_defer"][(c - 12) * 128:(c - 11) * 128, :],
            in_=bb[:, AQ:AQ + 1024]), 16)

    p.wait("tensor", ev, kq_ready)
    for kk in range(8):
        tok = p.t.matmul(ps[:, 1024:1152],
                         bb[:, AKF + kk * 128:AKF + (kk + 1) * 128],
                         bb[:, AQ + kk * 128:AQ + (kk + 1) * 128],
                         start=(kk == 0), stop=(kk == 7))
    msv = mm.step(tok)
    p.wait("vector", mm, msv)
    ssv = ev.step(p.v.tensor_tensor(bb[:, ASB:ASB + 128], ps[:, 1024:1152],
                                    bb[:, APR + 16:APR + 16 + 128], ALU.mult))
    p.vd()

    p.wait("tensor", ev, ssv)
    for half in range(2):
        tok = p.t.matmul(ps[:, 2560 + half * 512:2560 + (half + 1) * 512],
                         bb[:, ASB:ASB + 128],
                         bb[:, AV + half * 512:AV + (half + 1) * 512],
                         start=True, stop=True)
    myi = mm.step(tok)
    for kk in range(8):
        for half in range(2):
            tok = p.t.matmul(ps[:, 1536 + half * 512:1536 + (half + 1) * 512],
                             bb[:, AQ + kk * 128:AQ + (kk + 1) * 128],
                             bb[:, AM + kk * 1024 + half * 512:
                                 AM + kk * 1024 + (half + 1) * 512],
                             start=(kk == 0), stop=(kk == 7))
    my2 = mm.step(tok)
    p.wait("vector", mm, my2)
    p.v.tensor_scalar(bb[:, ATMP:ATMP + 1024], ps[:, 1536:2560],
                      bb[:, P0 + 1:P0 + 2], None, ALU.mult)
    p.vd()
    rv = ev.step(p.v.tensor_tensor(bb[:, ATMP:ATMP + 1024], bb[:, ATMP:ATMP + 1024],
                                   ps[:, 2560:3584], ALU.add))
    p.vd()

    p.v.tensor_scalar(bb[:, AKT:AKT + 1024], bb[:, AKT:AKT + 1024],
                      bb[:, P0 + 0:P0 + 1], None, ALU.mult)
    klv = ev.step(p.v.engine_nop())
    p.vd()
    p.wait("tensor", ev, klv)
    p.wait("tensor", ev, rv)
    ug = [rv, rv]
    for kk in range(8):
        pb = 2560 if kk % 2 == 0 else 1536
        p.wait("tensor", ev, ug[kk % 2])
        for half in range(2):
            tok = p.t.matmul(ps[:, pb + half * 512:pb + (half + 1) * 512],
                             bb[:, AKT + kk * 128:AKT + (kk + 1) * 128],
                             bb[:, AV + half * 512:AV + (half + 1) * 512],
                             start=True, stop=True)
        muv = mm.step(tok)
        p.wait("vector", mm, muv)
        ug[kk % 2] = ev.step(p.v.scalar_tensor_tensor(
            bb[:, AM + kk * 1024:AM + (kk + 1) * 1024],
            bb[:, AM + kk * 1024:AM + (kk + 1) * 1024],
            bb[:, P0 + 2:P0 + 3], ps[:, pb:pb + 1024], ALU.mult, ALU.add))
    p.vd()

    _inplace_t128(cx, ATMP, 8, (ev, ev.n))
    rf_ready = ev.n
    p.wait("tensor", ev, rf_ready)
    for k in range(8):
        slot = k % 2
        p.wait("sync", mm, ring_guard[slot])
        wv = osems[slot].step(p.sy.dma_start(
            out=bb[:, ARG + slot * 1024:ARG + (slot + 1) * 1024],
            in_=dr["wo_t"][k * 128:(k + 1) * 128, :]), 16)
        p.wait("tensor", osems[slot], wv)
        for half in range(2):
            tok = p.t.matmul(ps[:, half * 512:(half + 1) * 512],
                             bb[:, ATMP + k * 128:ATMP + (k + 1) * 128],
                             bb[:, ARG + slot * 1024 + half * 512:
                                 ARG + slot * 1024 + (half + 1) * 512],
                             start=(k == 0), stop=(k == 7))
        ring_guard[slot] = mm.step(tok)
    mv = mm.n
    p.wait("vector", mm, mv)
    if defer:
        vv = ev.step(p.v.tensor_scalar(bb[:, ATMP:ATMP + 1024], ps[:, 0:1024],
                                       bb[:, P0 + 3:P0 + 4], None, ALU.mult))
        p.vd()
        p.wait("gpsimd", ev, vv)
        p.wait("gpsimd", sems["store"], sems["store"].n)
        sems["store"].step(p.g.dma_start(
            out=dr["s1_defer"][(c - 12) * 128:(c - 11) * 128, :],
            in_=bb[:, ATMP:ATMP + 1024]), 16)
        p.wait("vector", sems["store"], sems["store"].n)
    else:
        p.v.scalar_tensor_tensor(bb[:, ATMP:ATMP + 1024], ps[:, 0:1024],
                                 bb[:, P0 + 3:P0 + 4],
                                 ba[:, c * 1024:(c + 1) * 1024],
                                 ALU.mult, ALU.mult)
        p.vd()
        fv = ev.step(p.v.tensor_tensor(bx[:, c * 1024:(c + 1) * 1024],
                                       bx[:, c * 1024:(c + 1) * 1024],
                                       bb[:, ATMP:ATMP + 1024], ALU.add))
        p.vd()
        p.wait("tensor", ev, fv)


def _l1_correction(cx, c, ring_guard):
    p, sems, sb, dr, ps = cx.p, cx.sems, cx.sb, cx.dr, cx.ps
    wl2, mm, ev, act = sems["wl2"], sems["mm"], sems["ev"], sems["act"]
    bb, bx, ba, stats = sb["BB"], sb["BX"], sb["BA"], sb["stats"]
    P0 = APR

    p.v.tensor_scalar(bb[:, P0 + 6:P0 + 7], C(cx, "iota"),
                      float(2047 - c * 128), -1.0, ALU.subtract, ALU.mult)
    p.vd()
    p.v.tensor_tensor(bb[:, P0 + 6:P0 + 7], bb[:, P0 + 6:P0 + 7],
                      bb[:, P0 + 5:P0 + 6], ALU.mult)
    cv = ev.step(p.v.engine_nop())
    p.vd()
    p.wait("scalar", ev, cv)
    p.sd()
    aa = act.step(p.s.activation(bb[:, P0 + 6:P0 + 7], bb[:, P0 + 6:P0 + 7],
                                 AF.Exp))
    p.wait("vector", act, aa)
    p.v.tensor_tensor(bb[:, P0 + 6:P0 + 7], bb[:, P0 + 6:P0 + 7],
                      bb[:, P0 + 3:P0 + 4], ALU.mult)
    p.vd()
    p.v.tensor_tensor(bb[:, P0 + 6:P0 + 7], bb[:, P0 + 6:P0 + 7],
                      bb[:, P0 + 4:P0 + 5], ALU.mult)
    p.vd()
    osems = (sems["oa"], sems["ob"])
    p.wait("sync", ev, ev.n)
    lq = sems["ld"].step(p.sy.dma_start(
        out=bb[:, AQ:AQ + 1024],
        in_=dr["q_defer"][(c - 12) * 128:(c - 11) * 128, :]), 16)
    p.wait("tensor", sems["ld"], lq)
    p.wait("tensor", ev, ev.n)
    for kk in range(8):
        for half in range(2):
            tok = p.t.matmul(ps[:, 1536 + half * 512:1536 + (half + 1) * 512],
                             bb[:, AQ + kk * 128:AQ + (kk + 1) * 128],
                             ba[:, kk * 1024 + half * 512:
                                 kk * 1024 + (half + 1) * 512],
                             start=(kk == 0), stop=(kk == 7))
    my = mm.step(tok)
    p.wait("vector", mm, my)
    rv = ev.step(p.v.tensor_scalar(bb[:, ATMP:ATMP + 1024], ps[:, 1536:2560],
                                   bb[:, P0 + 6:P0 + 7], None, ALU.mult))
    p.vd()
    _inplace_t128(cx, ATMP, 8, (ev, rv))
    p.wait("tensor", ev, ev.n)
    for k in range(8):
        slot = k % 2
        p.wait("sync", mm, ring_guard[slot])
        wv = osems[slot].step(p.sy.dma_start(
            out=bb[:, ARG + slot * 1024:ARG + (slot + 1) * 1024],
            in_=dr["wo_t"][k * 128:(k + 1) * 128, :]), 16)
        p.wait("tensor", osems[slot], wv)
        for half in range(2):
            tok = p.t.matmul(ps[:, half * 512:(half + 1) * 512],
                             bb[:, ATMP + k * 128:ATMP + (k + 1) * 128],
                             bb[:, ARG + slot * 1024 + half * 512:
                                 ARG + slot * 1024 + (half + 1) * 512],
                             start=(k == 0), stop=(k == 7))
        ring_guard[slot] = mm.step(tok)
    mv = mm.n
    lv = sems["ld"].step(p.sy.dma_start(
        out=bb[:, AQ:AQ + 1024],
        in_=dr["s1_defer"][(c - 12) * 128:(c - 11) * 128, :]), 16)
    p.wait("vector", sems["ld"], lv)
    p.wait("vector", mm, mv)
    p.v.tensor_tensor(bb[:, ATMP:ATMP + 1024], ps[:, 0:1024],
                      bb[:, AQ:AQ + 1024], ALU.add)
    p.vd()
    p.v.tensor_tensor(bb[:, ATMP:ATMP + 1024], bb[:, ATMP:ATMP + 1024],
                      ba[:, c * 1024:(c + 1) * 1024], ALU.mult)
    p.vd()
    fv = ev.step(p.v.tensor_tensor(bx[:, c * 1024:(c + 1) * 1024],
                                   bx[:, c * 1024:(c + 1) * 1024],
                                   bb[:, ATMP:ATMP + 1024], ALU.add))
    p.vd()
    p.wait("tensor", ev, fv)


def _t128_rot(cx, srcf, dstf, nblk, wait):
    p, sems, ps = cx.p, cx.sems, cx.ps
    mm, ev = sems["mm"], sems["ev"]
    ident = C(cx, "ident")
    if not hasattr(cx, "_it_ev"):
        cx._it_ev = [0, 0, 0, 0]
    p.wait("tensor", *wait)
    for k in range(nblk):
        b = k % 4
        if cx._it_ev[b]:
            p.wait("tensor", ev, cx._it_ev[b])
        pc = 2048 + b * 512
        me = mm.step(p.t.transpose(ps[:, pc:pc + 128], srcf(k), ident))
        p.wait("vector", mm, me)
        cx._it_ev[b] = ev.step(p.v.tensor_copy(dstf(k), ps[:, pc:pc + 128]))
    p.vd()


def _inplace_t128(cx, base, nblk, wait):
    bb = cx.sb["BB"]
    _t128_rot(cx, lambda k: bb[:, base + k * 128:base + (k + 1) * 128],
              lambda k: bb[:, base + k * 128:base + (k + 1) * 128], nblk, wait)


def _oop_t128(cx, src, dst, nblk):
    bb = cx.sb["BB"]
    _t128_rot(cx, lambda k: bb[:, src + k * 128:src + (k + 1) * 128],
              lambda k: bb[:, dst + k * 128:dst + (k + 1) * 128], nblk,
              (cx.sems["ev"], cx.sems["ev"].n))


def kernel(**inputs):
    global _TAP_RESULTS
    per_core = _host_prep(inputs)
    nc = build_program(tap=_TAP)
    res = run_bass_kernel_spmd(nc, per_core, core_ids=list(range(8)))
    _TAP_RESULTS = res.results
    out = np.empty((4, 4096, 1024), dtype=np.float32)
    for core in range(8):
        b, h = core // 2, core % 2
        out[b, h * NT:(h + 1) * NT] = res.results[core]["out"]
    return out
